# version 02
# baseline (speedup 1.0000x reference)
import numpy as np
from contextlib import ExitStack

import concourse.bass as bass
import concourse.mybir as mybir
from concourse.bass_utils import run_bass_kernel_spmd

F32 = mybir.dt.float32
BF16 = mybir.dt.bfloat16
I16 = mybir.dt.int16
AF = mybir.ActivationFunctionType
ALU = mybir.AluOpType
AX = mybir.AxisListType

D = 2048
S = 8192
TOWN = 2048
H = 1408
E = 8
EPS = 1e-5
SCALE = 1.0 / float(np.sqrt(2048.0))

DC = 16
HC = 11
NSB = 16
NT = 16


class Prog:
    def __init__(self, debug=False):
        self.debug = debug
        nc = bass.Bass("TRN2", target_bir_lowering=False)
        self.nc = nc
        self.ctx = ExitStack()
        dt = nc.dram_tensor
        self.xb = dt("xb", [S, D], F32, kind="ExternalInput").ap()
        self.xq = dt("xq", [TOWN, D], F32, kind="ExternalInput").ap()
        self.xqT = dt("xqT", [D, TOWN], F32, kind="ExternalInput").ap()
        self.qpos_i = dt("qpos", [128, NT], F32, kind="ExternalInput").ap()
        self.qkv = dt("qkv", [D, 3 * D], F32, kind="ExternalInput").ap()
        self.o_proj = dt("o_proj", [D, D], F32, kind="ExternalInput").ap()
        self.gammaT_i = dt("gammaT", [128, DC], F32, kind="ExternalInput").ap()
        self.betaT_i = dt("betaT", [128, DC], F32, kind="ExternalInput").ap()
        self.gate_wT = dt("gate_wT", [D, E], F32, kind="ExternalInput").ap()
        self.gate_b = dt("gate_b", [128, E], F32, kind="ExternalInput").ap()
        self.gate_bias = dt("gate_bias", [128, E], F32, kind="ExternalInput").ap()
        self.w1T = dt("w1T", [E * D, H], F32, kind="ExternalInput").ap()
        self.w2T = dt("w2T", [E * H, D], F32, kind="ExternalInput").ap()
        self.b1 = dt("b1", [E, H], F32, kind="ExternalInput").ap()
        self.b2 = dt("b2", [E, D], F32, kind="ExternalInput").ap()
        self.out = dt("out", [TOWN, D], F32, kind="ExternalOutput").ap()
        kind = "ExternalOutput" if debug else "Internal"
        self.xn2TS = dt("xn2TS", [128, DC, S], BF16, kind=kind).ap()
        self.xn2TownS = dt("xn2TownS", [128, DC, TOWN], BF16, kind=kind).ap()
        self.ktS = dt("ktS", [128, DC, S], BF16, kind=kind).ap()
        self.vS = dt("vS", [128, S // 128, D], BF16, kind=kind).ap()
        self.xnS = dt("xnS", [128, NT, D], BF16, kind=kind).ap()
        self.xnTS = dt("xnTS", [128, DC, TOWN], BF16, kind=kind).ap()
        self.qTS = dt("qTS", [128, DC, TOWN], BF16, kind=kind).ap()
        self.oS = dt("oS", [128, NT, D], BF16, kind=kind).ap()
        self.sS = dt("sS", [128, NT, D], BF16, kind=kind).ap()
        if debug:
            self.dbg_comb = dt("dbg_comb", [128, NT * E], F32, kind=kind).ap()
            self.dbg_acc = dt("dbg_acc", [128, 8 * D], F32, kind=kind).ap()
            self.dbg_raw = dt("dbg_raw", [128, NT * E], F32, kind=kind).ap()
            self.dbg_sc = dt("dbg_sc", [128, NT * E], F32, kind=kind).ap()
            self.dbg_gw = dt("dbg_gw", [128, DC * E], F32, kind=kind).ap()
            self.dbg_cons = dt("dbg_cons", [128, 2 * E], F32, kind=kind).ap()

        self.sems = {}
        self.ct = {}
        for name in ["ld", "gd", "pe", "dv", "ac"]:
            self.sems[name] = self.ctx.enter_context(nc.semaphore(name))
            self.ct[name] = 0
        self.lastw = {}

        sb = lambda n, shp, d=F32: self.ctx.enter_context(nc.sbuf_tensor(n, shp, d))
        self.ident = sb("ident", [128, 128], BF16)
        self.ones_row = sb("ones_row", [32, 512], BF16)
        self.ones128f = sb("ones128f", [128, 128], F32)
        self.ones_colb = sb("ones_colb", [128, 1], BF16)
        self.ones_colf = sb("ones_colf", [128, 1], F32)
        self.gammaT = sb("gammaT_s", [128, DC])
        self.betaT = sb("betaT_s", [128, DC])
        self.qpos = sb("qpos_s", [128, NT])
        self.gate_wT_sb = sb("gate_wT_sb", [128, DC * E])
        self.gb_bc = sb("gb_bc", [128, E])
        self.gbias_bc = sb("gbias_bc", [128, E])
        self.rowsum_bc = sb("rowsum_bc", [128, E])
        self.gbb_bc = sb("gbb_bc", [128, E])
        self.comb = sb("comb", [128, NT * E])
        for n in ["sx", "sxx", "mu", "m2", "musq", "var", "a1", "l1", "nh", "r1",
                  "ia1", "var2", "a2", "l2", "nh2", "r2", "rr", "mrr", "mr1n"]:
            setattr(self, n, sb(n, [128, 4]))
        for n in ["g_t1", "g_sc", "g_eq1", "g_msk", "g_eq2", "g_c1"]:
            setattr(self, n, sb(n, [128, E]))
        self.dbg_raw_sb = sb("dbg_raw_sb", [128, NT * E])
        self.dbg_sc_sb = sb("dbg_sc_sb", [128, NT * E])
        for n in ["g_m1", "g_m2", "g_d12", "g_e", "g_den", "g_g1", "g_g2"]:
            setattr(self, n, sb(n, [128, 1]))

        self.g = nc.gpsimd
        self.v = nc.vector
        self.a = nc.scalar
        self.t = nc.tensor
        self.sy = nc.sync

    def inc(self, ins, sem, n=1):
        ins.then_inc(self.sems[sem], n)
        self.ct[sem] += n
        return (sem, self.ct[sem])

    def tok(self, sem):
        return (sem, self.ct[sem])

    def w(self, eng, engname, tok):
        if tok is None:
            return
        sem, val = tok
        if val <= 0:
            return
        key = (engname, sem)
        if self.lastw.get(key, 0) >= val:
            return
        eng.wait_ge(self.sems[sem], val)
        self.lastw[key] = val

    def dma(self, qeng, qname, sem, dst, src, toks=()):
        for t in toks:
            self.w(qeng, qname, t)
        ins = qeng.dma_start(dst, src)
        return self.inc(ins, sem, 16)

    def phase_start(self):
        toks = [self.tok(s) for s in ("pe", "dv", "ac", "ld", "gd")]
        for eng, name in ((self.t, "t"), (self.g, "g"), (self.sy, "s")):
            for tk in toks:
                if (name, tk[0]) not in (("t", "pe"), ("g", "gd"), ("s", "ld")):
                    self.w(eng, name, tk)

    def emit_stats(self, x4, x4_tok, want_mr1n):
        P, v, a = self, self.v, self.a
        t_sx = t_sxx = None
        for j in range(4):
            P.w(v, "v", x4_tok)
            t_sx = P.inc(v.reduce_sum(P.sx[:, j:j + 1], x4[:, j * D:(j + 1) * D], AX.X), "dv")
        for j in range(4):
            P.w(a, "a", x4_tok)
            t_sxx = P.inc(a.activation(P.sqdump[:], x4[:, j * D:(j + 1) * D], AF.Square,
                                       bias=0.0, scale=1.0, accum_out=P.sxx[:, j:j + 1]), "ac")
        P.w(v, "v", t_sx)
        tk = P.inc(v.tensor_scalar_mul(P.mu[:], P.sx[:], 1.0 / D), "dv")
        P.w(v, "v", t_sxx)
        P.w(v, "v", tk)
        tk = P.inc(v.tensor_scalar_mul(P.m2[:], P.sxx[:], 1.0 / D), "dv")
        P.w(v, "v", tk)
        tk = P.inc(v.tensor_tensor(P.musq[:], P.mu[:], P.mu[:], ALU.mult), "dv")
        P.w(v, "v", tk)
        tk = P.inc(v.tensor_tensor(P.var[:], P.m2[:], P.musq[:], ALU.subtract), "dv")
        P.w(v, "v", tk)
        t_a1 = P.inc(v.tensor_scalar_add(P.a1[:], P.var[:], EPS), "dv")
        P.w(a, "a", t_a1)
        tk = P.inc(a.activation(P.l1[:], P.a1[:], AF.Ln), "ac")
        P.w(v, "v", tk)
        tk = P.inc(v.tensor_scalar_mul(P.nh[:], P.l1[:], -0.5), "dv")
        P.w(a, "a", tk)
        t_r1 = P.inc(a.activation(P.r1[:], P.nh[:], AF.Exp), "ac")
        P.w(v, "v", t_a1)
        tk = P.inc(v.reciprocal(P.ia1[:], P.a1[:]), "dv")
        P.w(v, "v", tk)
        tk = P.inc(v.tensor_tensor(P.var2[:], P.var[:], P.ia1[:], ALU.mult), "dv")
        P.w(v, "v", tk)
        tk = P.inc(v.tensor_scalar_add(P.a2[:], P.var2[:], EPS), "dv")
        P.w(a, "a", tk)
        tk = P.inc(a.activation(P.l2[:], P.a2[:], AF.Ln), "ac")
        P.w(v, "v", tk)
        tk = P.inc(v.tensor_scalar_mul(P.nh2[:], P.l2[:], -0.5), "dv")
        P.w(a, "a", tk)
        tk = P.inc(a.activation(P.r2[:], P.nh2[:], AF.Exp), "ac")
        P.w(v, "v", tk)
        P.w(v, "v", t_r1)
        tk = P.inc(v.tensor_tensor(P.rr[:], P.r1[:], P.r2[:], ALU.mult), "dv")
        P.w(v, "v", tk)
        t_mrr = P.inc(v.scalar_tensor_tensor(P.mrr[:], P.mu[:], -1.0, P.rr[:], ALU.mult, ALU.mult), "dv")
        t_r1mu = None
        if want_mr1n:
            P.w(v, "v", t_mrr)
            t_r1mu = P.inc(v.scalar_tensor_tensor(P.mr1n[:], P.mu[:], -1.0, P.r1[:], ALU.mult, ALU.mult), "dv")
        return t_mrr, t_r1mu

    def emit_transposes(self, src_sb, src_tok, pT, evac_tok, dst, dst_dc_stride,
                        gamma, beta, n_tiles=4, free_tok=None):
        P, t, v = self, self.t, self.v
        last = None
        for dc in range(DC):
            bank = dc % 2
            P.w(t, "t", evac_tok[bank])
            P.w(t, "t", src_tok)
            if free_tok is not None:
                P.w(t, "t", free_tok)
            for j in range(n_tiles):
                ins = t.transpose(pT[bank][:, j * 128:(j + 1) * 128],
                                  src_sb[:, j * D + dc * 128: j * D + (dc + 1) * 128],
                                  P.ident[:])
            tk_pe = P.inc(ins, "pe")
            P.w(v, "v", tk_pe)
            dsl = dst[:, dc * dst_dc_stride: dc * dst_dc_stride + n_tiles * 128]
            if gamma is not None:
                ins = v.tensor_scalar(dsl, pT[bank][:, 0:n_tiles * 128],
                                      gamma[:, dc:dc + 1], beta[:, dc:dc + 1],
                                      ALU.mult, ALU.add)
            else:
                ins = v.tensor_copy(dsl, pT[bank][:, 0:n_tiles * 128])
            last = P.inc(ins, "dv")
            evac_tok[bank] = last
        return last

    def emit_proj_kc(self, w_sb, src, pX, evac_toks, dstbuf, dst_free_tok, ready_tok):
        P, t, a = self, self.t, self.a
        evac = evac_toks
        for kc in range(DC):
            bank = kc % 2
            P.w(t, "t", evac[bank])
            P.w(t, "t", ready_tok)
            for dc in range(DC):
                ins = t.matmul(pX[:, bank * 512:(bank + 1) * 512],
                               w_sb[:, dc * D + kc * 128: dc * D + (kc + 1) * 128],
                               src[:, dc * 512:(dc + 1) * 512],
                               start=(dc == 0), stop=(dc == DC - 1))
            tk_pe = P.inc(ins, "pe")
            P.w(a, "a", tk_pe)
            P.w(a, "a", dst_free_tok)
            ins = a.activation(dstbuf[:, kc * 512:(kc + 1) * 512],
                               pX[:, bank * 512:(bank + 1) * 512], AF.Copy,
                               bias=0.0, scale=1.0)
            evac[bank] = P.inc(ins, "ac")
        return evac[(DC - 1) % 2]

    def build(self):
        self.emit_prep()
        self.emit_a()
        self.emit_b()
        self.emit_c1()
        self.emit_c2()
        self.emit_d()
        self.emit_d2()
        self.emit_e_f()
        self.ctx.close()
        return self.nc

    def emit_prep(self):
        P = self
        g, v, a, t, sy = self.g, self.v, self.a, self.t, self.sy
        g.memset(P.ident[:], 0.0)
        g.affine_select(P.ident[:], P.ident[:], [[1, 128]], ALU.not_equal, 1.0,
                        base=0, channel_multiplier=-1)
        g.memset(P.ones_row[:], 0.0)
        g.memset(P.ones_row[0:1, :], 1.0)
        g.memset(P.ones128f[:], 1.0)
        g.memset(P.ones_colb[:], 1.0)
        tok_consts = P.inc(g.memset(P.ones_colf[:], 1.0), "gd")

        P.dma(sy, "s", "ld", P.gammaT[:], P.gammaT_i)
        P.dma(sy, "s", "ld", P.betaT[:], P.betaT_i)
        P.tok_qpos = P.dma(sy, "s", "ld", P.qpos[:], P.qpos_i)
        tok_gw = P.dma(sy, "s", "ld",
                       P.gate_wT_sb[:].rearrange("p (dc e) -> p dc e", dc=DC),
                       P.gate_wT.rearrange("(dc p) e -> p dc e", p=128))
        tok_gb = P.dma(sy, "s", "ld", P.gb_bc[:], P.gate_b)
        tok_gbias = P.dma(sy, "s", "ld", P.gbias_bc[:], P.gate_bias)
        P.tok_gamma_ready = P.tok_qpos

        with self.nc.psum_tensor("pPrep", [128, 512], F32) as pP:
            P.w(t, "t", tok_gw)
            P.w(t, "t", tok_consts)
            for dc in range(DC):
                ins = t.matmul(pP[:, 0:E], P.ones128f[:], P.gate_wT_sb[:, dc * E:(dc + 1) * E],
                               start=(dc == 0), stop=(dc == DC - 1))
            tk_pe = P.inc(ins, "pe")
            P.w(v, "v", tk_pe)
            P.inc(v.tensor_copy(P.rowsum_bc[:], pP[:, 0:E]), "dv")
            P.w(v, "v", tok_gb)
            P.w(v, "v", tok_gbias)
            P.w(v, "v", P.tok("dv"))
            P.tok_gate_ready = P.inc(v.tensor_tensor(P.gbb_bc[:], P.gb_bc[:], P.gbias_bc[:], ALU.add), "dv")

    def emit_a(self):
        P = self
        nc = self.nc
        g, v, a, t, sy = self.g, self.v, self.a, self.t, self.sy
        with ExitStack() as phase_ctx:
            x4 = phase_ctx.enter_context(nc.sbuf_tensor("ax4", [128, 4 * D], F32))
            xnl2 = phase_ctx.enter_context(nc.sbuf_tensor("axnl2", [128, 4 * D], BF16))
            xn2T = phase_ctx.enter_context(nc.sbuf_tensor("axn2T", [128, DC * 512], BF16))
            ktbuf = phase_ctx.enter_context(nc.sbuf_tensor("aktbuf", [128, DC * 512], BF16))
            wk_sb = phase_ctx.enter_context(nc.sbuf_tensor("awk", [128, DC * D], BF16))
            self.sqdump = phase_ctx.enter_context(nc.sbuf_tensor("asq", [128, D], BF16))
            pK = phase_ctx.enter_context(nc.psum_tensor("apK", [128, 2 * 512], F32))
            pT0 = phase_ctx.enter_context(nc.psum_tensor("apT0", [128, 512], BF16))
            pT1 = phase_ctx.enter_context(nc.psum_tensor("apT1", [128, 512], BF16))
            pT = [pT0, pT1]
            P.phase_start()
            tok_wk = P.dma(g, "g", "gd", wk_sb[:].rearrange("p (dc f) -> p dc f", dc=DC),
                           P.qkv[:, D:2 * D].rearrange("(dc p) f -> p dc f", p=128))
            tok_x4_free = None
            tok_sp_xn2T = tok_sp_kt = None
            evac_tok = [None, None]
            evacK = [None, None]
            for sb_i in range(NSB):
                t0 = sb_i * 512
                tok_x4 = P.dma(sy, "s", "ld",
                               x4[:].rearrange("p (s d) -> p s d", s=4),
                               P.xb.rearrange("(s p) d -> p s d", p=128)[:, sb_i * 4:(sb_i + 1) * 4, :],
                               toks=(tok_x4_free, P.tok_gamma_ready))
                t_mrr, _ = P.emit_stats(x4, tok_x4, want_mr1n=False)
                last_xnl2 = None
                for j in range(4):
                    P.w(v, "v", t_mrr)
                    ins = v.tensor_scalar(xnl2[:, j * D:(j + 1) * D], x4[:, j * D:(j + 1) * D],
                                          P.rr[:, j:j + 1], P.mrr[:, j:j + 1], ALU.mult, ALU.add)
                    last_xnl2 = P.inc(ins, "dv")
                tok_x4_free = last_xnl2
                tok_xn2T_ready = P.emit_transposes(xnl2, last_xnl2, pT, evac_tok, xn2T, 512,
                                                  P.gammaT, P.betaT, free_tok=tok_sp_xn2T)
                P.w(t, "t", tok_wk)
                tok_kt_ready = P.emit_proj_kc(wk_sb, xn2T, pK, evacK, ktbuf,
                                              tok_sp_kt, tok_xn2T_ready)
                tok_sp_xn2T = P.dma(sy, "s", "ld", P.xn2TS[:, :, t0:t0 + 512],
                                    xn2T[:].rearrange("p (dc t) -> p dc t", dc=DC),
                                    toks=(tok_xn2T_ready,))
                tok_sp_kt = P.dma(sy, "s", "ld", P.ktS[:, :, t0:t0 + 512],
                                  ktbuf[:].rearrange("p (kc t) -> p kc t", kc=DC),
                                  toks=(tok_kt_ready,))
            self.tok_a_done = tok_sp_kt

    def emit_b(self):
        P = self
        nc = self.nc
        g, v, a, t, sy = self.g, self.v, self.a, self.t, self.sy
        with ExitStack() as phase_ctx:
            xt0 = phase_ctx.enter_context(nc.sbuf_tensor("bxt0", [128, DC * 512], BF16))
            xt1 = phase_ctx.enter_context(nc.sbuf_tensor("bxt1", [128, DC * 512], BF16))
            vbuf = phase_ctx.enter_context(nc.sbuf_tensor("bvbuf", [128, 4 * D], BF16))
            wv_sb = phase_ctx.enter_context(nc.sbuf_tensor("bwv", [128, DC * D], BF16))
            pV = phase_ctx.enter_context(nc.psum_tensor("bpV", [128, 2 * 512], F32))
            xts = [xt0, xt1]
            P.phase_start()
            tok_wv = P.dma(g, "g", "gd", wv_sb[:].rearrange("p (dc f) -> p dc f", dc=DC),
                           P.qkv[:, 2 * D:3 * D].rearrange("(dc p) f -> p dc f", p=128))
            tok_xt_free = [None, None]
            tok_sp_v = None
            evacV = [None, None]
            for sb_i in range(NSB):
                t0 = sb_i * 512
                bk = sb_i % 2
                tok_xt = P.dma(sy, "s", "ld", xts[bk][:].rearrange("p (dc t) -> p dc t", dc=DC),
                               P.xn2TS[:, :, t0:t0 + 512],
                               toks=(tok_xt_free[bk], self.tok_a_done))
                P.w(t, "t", tok_wv)
                nvb = 0
                tok_v_ready = None
                for j in range(4):
                    for dvc in range(4):
                        bank = nvb % 2
                        nvb += 1
                        P.w(t, "t", evacV[bank])
                        P.w(t, "t", tok_xt)
                        for dc in range(DC):
                            ins = t.matmul(pV[:, bank * 512:(bank + 1) * 512],
                                           xts[bk][:, dc * 512 + j * 128: dc * 512 + (j + 1) * 128],
                                           wv_sb[:, dc * D + dvc * 512: dc * D + (dvc + 1) * 512],
                                           start=(dc == 0), stop=(dc == DC - 1))
                        tk_pe = P.inc(ins, "pe")
                        P.w(a, "a", tk_pe)
                        P.w(a, "a", tok_sp_v)
                        ins = a.activation(vbuf[:, j * D + dvc * 512: j * D + (dvc + 1) * 512],
                                           pV[:, bank * 512:(bank + 1) * 512], AF.Copy,
                                           bias=0.0, scale=1.0)
                        evacV[bank] = P.inc(ins, "ac")
                        tok_v_ready = evacV[bank]
                tok_xt_free[bk] = P.tok("pe")
                tok_sp_v = P.dma(sy, "s", "ld", P.vS[:, sb_i * 4:(sb_i + 1) * 4, :],
                                 vbuf[:].rearrange("p (s d) -> p s d", s=4),
                                 toks=(tok_v_ready,))
            self.tok_b_done = tok_sp_v

    def emit_c1(self):
        P = self
        nc = self.nc
        g, v, a, t, sy = self.g, self.v, self.a, self.t, self.sy
        with ExitStack() as phase_ctx:
            x4 = phase_ctx.enter_context(nc.sbuf_tensor("cx4", [128, 4 * D], F32))
            xqT4 = phase_ctx.enter_context(nc.sbuf_tensor("cxqT4", [128, DC * 512], F32))
            xnl2 = phase_ctx.enter_context(nc.sbuf_tensor("cxnl2", [128, 4 * D], BF16))
            xn2T = phase_ctx.enter_context(nc.sbuf_tensor("cxn2T", [128, DC * 512], BF16))
            xn_b = phase_ctx.enter_context(nc.sbuf_tensor("cxn", [128, 4 * D], BF16))
            xnT = phase_ctx.enter_context(nc.sbuf_tensor("cxnT", [128, DC * 512], BF16))
            self.sqdump = phase_ctx.enter_context(nc.sbuf_tensor("csq", [128, D], BF16))
            pRaw = phase_ctx.enter_context(nc.psum_tensor("cpRaw", [128, 512], F32))
            pT0 = phase_ctx.enter_context(nc.psum_tensor("cpT0", [128, 512], BF16))
            pT1 = phase_ctx.enter_context(nc.psum_tensor("cpT1", [128, 512], BF16))
            pT = [pT0, pT1]
            P.phase_start()
            tok_x4_free = None
            tok_sp_xn = tok_sp_xnT = tok_sp_x2 = None
            tok_raw_read = None
            evac_tok = [None, None]
            for c in range(4):
                t0 = c * 512
                tok_x4 = P.dma(sy, "s", "ld",
                               x4[:].rearrange("p (s d) -> p s d", s=4),
                               P.xq.rearrange("(s p) d -> p s d", p=128)[:, c * 4:(c + 1) * 4, :],
                               toks=(tok_x4_free,))
                tok_xqT = P.dma(sy, "s", "ld",
                                xqT4[:].rearrange("p (dc t) -> p dc t", dc=DC),
                                P.xqT.rearrange("(dc p) t -> p dc t", p=128)[:, :, c * 512:(c + 1) * 512],
                                toks=(tok_raw_read,))
                t_mrr, t_r1mu = P.emit_stats(x4, tok_x4, want_mr1n=True)
                for j in range(4):
                    ti = c * 4 + j
                    P.w(t, "t", tok_xqT)
                    P.w(t, "t", P.tok_gate_ready)
                    P.w(t, "t", tok_raw_read)
                    for dc in range(DC):
                        ins = t.matmul(pRaw[:, 0:E],
                                       xqT4[:, dc * 512 + j * 128: dc * 512 + (j + 1) * 128],
                                       P.gate_wT_sb[:, dc * E:(dc + 1) * E],
                                       start=(dc == 0), stop=(dc == DC - 1))
                    tk_raw = P.inc(ins, "pe")
                    P.w(v, "v", tk_raw)
                    P.w(v, "v", t_r1mu)
                    if P.debug:
                        P.inc(v.tensor_copy(P.dbg_raw_sb[:, ti * E:(ti + 1) * E], pRaw[:, 0:E]), "dv")
                    tk = P.inc(v.scalar_tensor_tensor(P.g_t1[:], pRaw[:, 0:E], P.r1[:, j:j + 1],
                                                      P.gbb_bc[:], ALU.mult, ALU.add), "dv")
                    tok_raw_read = tk
                    P.w(v, "v", tk)
                    tk = P.inc(v.scalar_tensor_tensor(P.g_sc[:], P.rowsum_bc[:], P.mr1n[:, j:j + 1],
                                                      P.g_t1[:], ALU.mult, ALU.add), "dv")
                    P.w(v, "v", tk)
                    if P.debug:
                        P.inc(v.tensor_copy(P.dbg_sc_sb[:, ti * E:(ti + 1) * E], P.g_sc[:]), "dv")
                        P.w(v, "v", P.tok("dv"))
                    tk = P.inc(v.reduce_max(P.g_m1[:], P.g_sc[:], AX.X), "dv")
                    P.w(v, "v", tk)
                    tk = P.inc(v.tensor_scalar(P.g_eq1[:], P.g_sc[:], P.g_m1[:, 0:1], None, ALU.is_ge), "dv")
                    P.w(v, "v", tk)
                    tk = P.inc(v.scalar_tensor_tensor(P.g_msk[:], P.g_eq1[:], -1e30, P.g_sc[:],
                                                      ALU.mult, ALU.add), "dv")
                    P.w(v, "v", tk)
                    tk = P.inc(v.reduce_max(P.g_m2[:], P.g_msk[:], AX.X), "dv")
                    P.w(v, "v", tk)
                    tk = P.inc(v.tensor_scalar(P.g_eq2[:], P.g_msk[:], P.g_m2[:, 0:1], None, ALU.is_ge), "dv")
                    P.w(v, "v", tk)
                    tk = P.inc(v.tensor_tensor(P.g_d12[:], P.g_m1[:], P.g_m2[:], ALU.subtract), "dv")
                    P.w(a, "a", tk)
                    tk = P.inc(a.activation(P.g_e[:], P.g_d12[:], AF.Exp, bias=0.0, scale=-1.0), "ac")
                    P.w(v, "v", tk)
                    tk = P.inc(v.tensor_scalar_add(P.g_den[:], P.g_e[:], 1.0), "dv")
                    P.w(v, "v", tk)
                    tk = P.inc(v.reciprocal(P.g_g1[:], P.g_den[:]), "dv")
                    P.w(v, "v", tk)
                    tk = P.inc(v.tensor_scalar(P.g_g2[:], P.g_g1[:], -1.0, 1.0, ALU.mult, ALU.add), "dv")
                    P.w(v, "v", tk)
                    tk = P.inc(v.tensor_scalar_mul(P.g_c1[:], P.g_eq1[:], P.g_g1[:, 0:1]), "dv")
                    P.w(v, "v", tk)
                    P.inc(v.scalar_tensor_tensor(P.comb[:, ti * E:(ti + 1) * E], P.g_eq2[:],
                                                 P.g_g2[:, 0:1], P.g_c1[:], ALU.mult, ALU.add), "dv")
                last_xn = last_xnl2 = None
                for j in range(4):
                    P.w(v, "v", t_mrr)
                    P.w(v, "v", t_r1mu)
                    ins = v.tensor_scalar(xn_b[:, j * D:(j + 1) * D], x4[:, j * D:(j + 1) * D],
                                          P.r1[:, j:j + 1], P.mr1n[:, j:j + 1], ALU.mult, ALU.add)
                    last_xn = P.inc(ins, "dv")
                    ins = v.tensor_scalar(xnl2[:, j * D:(j + 1) * D], x4[:, j * D:(j + 1) * D],
                                          P.rr[:, j:j + 1], P.mrr[:, j:j + 1], ALU.mult, ALU.add)
                    last_xnl2 = P.inc(ins, "dv")
                tok_x4_free = last_xnl2
                tok_xnT_ready = P.emit_transposes(xn_b, last_xn, pT, evac_tok, xnT, 512,
                                                 None, None, free_tok=tok_sp_xnT)
                tok_x2_ready = P.emit_transposes(xnl2, last_xnl2, pT, evac_tok, xn2T, 512,
                                                 P.gammaT, P.betaT, free_tok=tok_sp_x2)
                tok_sp_xn = P.dma(sy, "s", "ld", P.xnS[:, c * 4:(c + 1) * 4, :],
                                  xn_b[:].rearrange("p (s d) -> p s d", s=4),
                                  toks=(last_xn,))
                tok_sp_xnT = P.dma(sy, "s", "ld", P.xnTS[:, :, t0:t0 + 512],
                                   xnT[:].rearrange("p (dc t) -> p dc t", dc=DC),
                                   toks=(tok_xnT_ready,))
                tok_sp_x2 = P.dma(sy, "s", "ld", P.xn2TownS[:, :, t0:t0 + 512],
                                  xn2T[:].rearrange("p (dc t) -> p dc t", dc=DC),
                                  toks=(tok_x2_ready,))
            self.tok_c1_done = tok_sp_x2
            if P.debug:
                P.w(g, "g", P.tok("dv"))
                P.inc(g.dma_start(P.dbg_comb, P.comb[:]), "gd", 16)
                P.inc(g.dma_start(P.dbg_raw, P.dbg_raw_sb[:]), "gd", 16)
                P.inc(g.dma_start(P.dbg_sc, P.dbg_sc_sb[:]), "gd", 16)
                P.inc(g.dma_start(P.dbg_gw, P.gate_wT_sb[:]), "gd", 16)
                P.inc(g.dma_start(P.dbg_cons[:, 0:E], P.rowsum_bc[:]), "gd", 16)
                P.inc(g.dma_start(P.dbg_cons[:, E:2 * E], P.gbb_bc[:]), "gd", 16)

    def emit_c2(self):
        P = self
        nc = self.nc
        g, v, a, t, sy = self.g, self.v, self.a, self.t, self.sy
        with ExitStack() as phase_ctx:
            xt0 = phase_ctx.enter_context(nc.sbuf_tensor("qxt0", [128, DC * 512], BF16))
            xt1 = phase_ctx.enter_context(nc.sbuf_tensor("qxt1", [128, DC * 512], BF16))
            qtbuf = phase_ctx.enter_context(nc.sbuf_tensor("qqt", [128, DC * 512], BF16))
            wq_sb = phase_ctx.enter_context(nc.sbuf_tensor("qwq", [128, DC * D], BF16))
            pQ = phase_ctx.enter_context(nc.psum_tensor("qpQ", [128, 2 * 512], F32))
            xts = [xt0, xt1]
            P.phase_start()
            tok_wq = P.dma(g, "g", "gd", wq_sb[:].rearrange("p (dc f) -> p dc f", dc=DC),
                           P.qkv[:, 0:D].rearrange("(dc p) f -> p dc f", p=128))
            tok_xt_free = [None, None]
            tok_sp_qt = None
            evacQ = [None, None]
            for c in range(4):
                t0 = c * 512
                bk = c % 2
                tok_xt = P.dma(sy, "s", "ld", xts[bk][:].rearrange("p (dc t) -> p dc t", dc=DC),
                               P.xn2TownS[:, :, t0:t0 + 512],
                               toks=(tok_xt_free[bk], self.tok_c1_done))
                P.w(t, "t", tok_wq)
                tok_qt_ready = P.emit_proj_kc(wq_sb, xts[bk], pQ, evacQ, qtbuf,
                                              tok_sp_qt, tok_xt)
                tok_xt_free[bk] = P.tok("pe")
                tok_sp_qt = P.dma(sy, "s", "ld", P.qTS[:, :, t0:t0 + 512],
                                  qtbuf[:].rearrange("p (kc t) -> p kc t", kc=DC),
                                  toks=(tok_qt_ready,))
            self.tok_c2_done = tok_sp_qt

    def emit_d(self):
        P = self
        nc = self.nc
        g, v, a, t, sy = self.g, self.v, self.a, self.t, self.sy
        with ExitStack() as phase_ctx:
            scores = phase_ctx.enter_context(nc.sbuf_tensor("dscore", [128, S], BF16))
            Pb = phase_ctx.enter_context(nc.sbuf_tensor("dP", [128, S], BF16))
            mask = phase_ctx.enter_context(nc.sbuf_tensor("dmask", [128, S], BF16))
            ptb = phase_ctx.enter_context(nc.sbuf_tensor("dptb", [128, S], BF16))
            kvidx = phase_ctx.enter_context(nc.sbuf_tensor("dkvidx", [128, S], I16))
            kt0 = phase_ctx.enter_context(nc.sbuf_tensor("dkt0", [128, DC * 512], BF16))
            kt1 = phase_ctx.enter_context(nc.sbuf_tensor("dkt1", [128, DC * 512], BF16))
            vb0 = phase_ctx.enter_context(nc.sbuf_tensor("dvb0", [128, 4 * D], BF16))
            vb1 = phase_ctx.enter_context(nc.sbuf_tensor("dvb1", [128, 4 * D], BF16))
            qt = phase_ctx.enter_context(nc.sbuf_tensor("dqt", [128, DC * 128], BF16))
            ob = phase_ctx.enter_context(nc.sbuf_tensor("dob", [128, D], BF16))
            bm = phase_ctx.enter_context(nc.sbuf_tensor("dbm", [128, 16], F32))
            negm = phase_ctx.enter_context(nc.sbuf_tensor("dnegm", [128, 1], F32))
            negms = phase_ctx.enter_context(nc.sbuf_tensor("dnegms", [128, 1], F32))
            lT = phase_ctx.enter_context(nc.sbuf_tensor("dlT", [32, 128], F32))
            lcol = phase_ctx.enter_context(nc.sbuf_tensor("dlcol", [128, 2], F32))
            linv = phase_ctx.enter_context(nc.sbuf_tensor("dlinv", [128, 1], F32))
            pS = phase_ctx.enter_context(nc.psum_tensor("dpS", [128, 2 * 512], F32))
            pO = phase_ctx.enter_context(nc.psum_tensor("dpO", [128, 4 * 512], F32))
            pTr = phase_ctx.enter_context(nc.psum_tensor("dpT", [128, 512], BF16))
            pL = phase_ctx.enter_context(nc.psum_tensor("dpL", [128, 512], F32))
            kts = [kt0, kt1]
            P.phase_start()
            vbs = [vb0, vb1]
            g.memset(lT[:], 0.0)
            tok_kvidx = P.inc(g.iota(kvidx[:], [[1, S]], base=0, channel_multiplier=0,
                                     allow_small_or_imprecise_dtypes=True), "gd")
            tok_kt_free = [None, None]
            tok_vb_free = [None, None]
            tok_qt_free = None
            tok_ob_free = None
            tok_scores_lastread = None
            tok_P_free = None
            tok_ptb_free = None
            tok_mask_free = None
            tok_pO_free = None
            tok_pL_free = None
            tok_pTr_free = None
            tok_pS_free = [None, None]

            for qi in range(NT):
                L = 4096 if qi < 8 else S
                nblk = L // 512
                nch = L // 128
                tok_qt = P.dma(sy, "s", "ld", qt[:].rearrange("p (kc t) -> p kc t", kc=DC),
                               P.qTS[:, :, qi * 128:(qi + 1) * 128],
                               toks=(tok_qt_free, self.tok_c2_done))
                P.w(v, "v", tok_kvidx)
                P.w(v, "v", tok_mask_free)
                P.w(v, "v", P.tok_qpos)
                tok_mask = P.inc(v.tensor_scalar(mask[:, 0:L], kvidx[:, 0:L],
                                                 P.qpos[:, qi:qi + 1], None, ALU.is_le), "dv")
                tok_sc_evac = [None] * nblk
                for b in range(nblk):
                    bk = b % 2
                    tok_kt = P.dma(sy, "s", "ld",
                                   kts[bk][:].rearrange("p (kc t) -> p kc t", kc=DC),
                                   P.ktS[:, :, b * 512:(b + 1) * 512],
                                   toks=(tok_kt_free[bk], self.tok_b_done))
                    P.w(t, "t", tok_kt)
                    P.w(t, "t", tok_qt)
                    P.w(t, "t", tok_sc_evac[b - 2] if b >= 2 else tok_pS_free[bk])
                    for kc in range(DC):
                        ins = t.matmul(pS[:, bk * 512:(bk + 1) * 512],
                                       qt[:, kc * 128:(kc + 1) * 128],
                                       kts[bk][:, kc * 512:(kc + 1) * 512],
                                       start=(kc == 0), stop=(kc == DC - 1))
                    tk_pe = P.inc(ins, "pe")
                    tok_kt_free[bk] = tk_pe
                    P.w(a, "a", tk_pe)
                    P.w(a, "a", tok_scores_lastread)
                    P.w(a, "a", tok_P_free)
                    ins = a.activation(scores[:, b * 512:(b + 1) * 512],
                                       pS[:, bk * 512:(bk + 1) * 512], AF.Copy,
                                       bias=0.0, scale=1.0)
                    tok_sc_evac[b] = P.inc(ins, "ac")
                    P.w(v, "v", tok_sc_evac[b])
                    P.inc(v.reduce_max(bm[:, b:b + 1], scores[:, b * 512:(b + 1) * 512], AX.X), "dv")
                P.w(v, "v", P.tok("dv"))
                tk = P.inc(v.reduce_max(negm[:], bm[:, 0:nblk], AX.X, negate=True), "dv")
                tok_scores_lastread = tk
                P.w(v, "v", tk)
                tk_nm = P.inc(v.tensor_scalar_mul(negms[:], negm[:], SCALE), "dv")
                P.w(a, "a", tk_nm)
                P.w(a, "a", tok_P_free)
                tok_exp = P.inc(a.activation(Pb[:, 0:L], scores[:, 0:L], AF.Exp,
                                             bias=negms[:, 0:1], scale=SCALE), "ac")
                P.w(v, "v", tok_exp)
                P.w(v, "v", tok_mask)
                tok_Pm = P.inc(v.tensor_tensor(Pb[:, 0:L], Pb[:, 0:L], mask[:, 0:L], ALU.mult), "dv")
                tok_mask_free = tok_Pm
                tok_tr_evac = tok_pTr_free
                tok_ptb_chunk = [None] * nch
                P.w(t, "t", tok_ptb_free)
                for g4 in range(nch // 4):
                    P.w(t, "t", tok_tr_evac)
                    P.w(t, "t", tok_Pm)
                    for j in range(4):
                        k = g4 * 4 + j
                        ins = t.transpose(pTr[:, j * 128:(j + 1) * 128],
                                          Pb[:, k * 128:(k + 1) * 128], P.ident[:])
                    tk_pe = P.inc(ins, "pe")
                    P.w(v, "v", tk_pe)
                    ins = v.tensor_copy(ptb[:, g4 * 512:(g4 + 1) * 512], pTr[:, 0:512])
                    tok_tr_evac = P.inc(ins, "dv")
                    for j in range(4):
                        tok_ptb_chunk[g4 * 4 + j] = tok_tr_evac
                tok_pTr_free = tok_tr_evac
                tok_P_free = P.tok("pe")
                last_l = None
                last_pv = None
                for b in range(nblk):
                    bk = b % 2
                    tok_vb = P.dma(sy, "s", "ld",
                                   vbs[bk][:].rearrange("p (s d) -> p s d", s=4),
                                   P.vS[:, b * 4:(b + 1) * 4, :],
                                   toks=(tok_vb_free[bk],))
                    for s4 in range(4):
                        k = b * 4 + s4
                        P.w(t, "t", tok_ptb_chunk[k])
                        if k == 0:
                            P.w(t, "t", tok_pL_free)
                            P.w(t, "t", tok_pO_free)
                        ins = t.matmul(pL[0:1, 0:128], P.ones_colb[:],
                                       ptb[:, k * 128:(k + 1) * 128],
                                       start=(k == 0), stop=(k == nch - 1))
                        if k == nch - 1:
                            last_l = P.inc(ins, "pe")
                        P.w(t, "t", tok_vb)
                        for dvc in range(4):
                            ins = t.matmul(pO[:, dvc * 512:(dvc + 1) * 512],
                                           ptb[:, k * 128:(k + 1) * 128],
                                           vbs[bk][:, s4 * D + dvc * 512: s4 * D + (dvc + 1) * 512],
                                           start=(k == 0), stop=(k == nch - 1))
                    tok_vb_free[bk] = P.inc(ins, "pe")
                    if b == nblk - 1:
                        last_pv = tok_vb_free[bk]
                tok_ptb_free = last_pv
                P.w(v, "v", last_l)
                tk = P.inc(v.tensor_copy(lT[0:1, :], pL[0:1, 0:128]), "dv")
                P.w(t, "t", tk)
                ins = t.matmul(pL[:, 256:258], lT[0:32, :], P.ones128f[0:32, 0:2],
                               start=True, stop=True)
                tk_pe = P.inc(ins, "pe")
                tok_pL_free = tk_pe
                P.w(v, "v", tk_pe)
                tk = P.inc(v.tensor_copy(lcol[:], pL[:, 256:258]), "dv")
                P.w(v, "v", tk)
                tok_linv = P.inc(v.reciprocal(linv[:], lcol[:, 0:1]), "dv")
                P.w(a, "a", tok_linv)
                P.w(a, "a", last_pv)
                P.w(a, "a", tok_ob_free)
                tok_ob = P.inc(a.activation(ob[:], pO[:, 0:4 * 512], AF.Copy,
                                            bias=0.0, scale=linv[:, 0:1]), "ac")
                tok_pO_free = tok_ob
                tok_sp_ob = P.dma(sy, "s", "ld", P.oS[:, qi:qi + 1, :],
                                  ob[:].rearrange("p (o d) -> p o d", o=1),
                                  toks=(tok_ob,))
                tok_ob_free = tok_sp_ob
                tok_qt_free = P.tok("pe")
                tok_pS_free = [tok_sc_evac[nblk - 2], tok_sc_evac[nblk - 1]]
            self.tok_d_done = tok_sp_ob

    def emit_d2(self):
        P = self
        nc = self.nc
        g, v, a, t, sy = self.g, self.v, self.a, self.t, self.sy
        with ExitStack() as phase_ctx:
            wo_sb = phase_ctx.enter_context(nc.sbuf_tensor("ewo", [128, DC * D], BF16))
            ob = phase_ctx.enter_context(nc.sbuf_tensor("eob", [128, D], BF16))
            otb = phase_ctx.enter_context(nc.sbuf_tensor("eotb", [128, DC * 128], BF16))
            xnt = phase_ctx.enter_context(nc.sbuf_tensor("exn", [128, D], BF16))
            s_bf = phase_ctx.enter_context(nc.sbuf_tensor("esbf", [128, D], BF16))
            pS = phase_ctx.enter_context(nc.psum_tensor("epS2", [128, 2 * 512], F32))
            pTr = phase_ctx.enter_context(nc.psum_tensor("epT2", [128, 512], BF16))
            P.phase_start()
            tok_wo = P.dma(g, "g", "gd", wo_sb[:].rearrange("p (dc f) -> p dc f", dc=DC),
                           P.o_proj.rearrange("(dc p) f -> p dc f", p=128))
            tok_ob_free = None
            tok_xn_free = None
            tok_sbf_free = None
            tok_otb_free = None
            tok_pTr_free = None
            tok_sp_s = None
            tok_pS_free = [None, None]
            for qi in range(NT):
                tok_ob = P.dma(sy, "s", "ld", ob[:].rearrange("p (o d) -> p o d", o=1),
                               P.oS[:, qi:qi + 1, :], toks=(tok_ob_free, self.tok_d_done))
                tok_xn = P.dma(sy, "s", "ld", xnt[:].rearrange("p (o d) -> p o d", o=1),
                               P.xnS[:, qi:qi + 1, :], toks=(tok_xn_free,))
                tok_tr2 = tok_pTr_free
                P.w(t, "t", tok_otb_free)
                for g4 in range(4):
                    P.w(t, "t", tok_tr2)
                    P.w(t, "t", tok_ob)
                    for j in range(4):
                        dvc = g4 * 4 + j
                        ins = t.transpose(pTr[:, j * 128:(j + 1) * 128],
                                          ob[:, dvc * 128:(dvc + 1) * 128], P.ident[:])
                    tk_pe = P.inc(ins, "pe")
                    P.w(v, "v", tk_pe)
                    ins = v.tensor_copy(otb[:, g4 * 512:(g4 + 1) * 512], pTr[:, 0:512])
                    tok_tr2 = P.inc(ins, "dv")
                tok_pTr_free = tok_tr2
                P.w(t, "t", tok_wo)
                tok_s_parts = []
                for fc in range(4):
                    bank = fc % 2
                    P.w(t, "t", tok_s_parts[-2] if fc >= 2 else tok_pS_free[bank])
                    P.w(t, "t", tok_tr2)
                    for dvc in range(DC):
                        ins = t.matmul(pS[:, bank * 512:(bank + 1) * 512],
                                       otb[:, dvc * 128:(dvc + 1) * 128],
                                       wo_sb[:, dvc * D + fc * 512: dvc * D + (fc + 1) * 512],
                                       start=(dvc == 0), stop=(dvc == DC - 1))
                    tk_pe = P.inc(ins, "pe")
                    P.w(v, "v", tk_pe)
                    P.w(v, "v", tok_xn)
                    P.w(v, "v", tok_sbf_free)
                    ins = v.scalar_tensor_tensor(s_bf[:, fc * 512:(fc + 1) * 512],
                                                 pS[:, bank * 512:(bank + 1) * 512], 1.0,
                                                 xnt[:, fc * 512:(fc + 1) * 512],
                                                 ALU.mult, ALU.add)
                    tok_s_parts.append(P.inc(ins, "dv"))
                tok_pS_free = [tok_s_parts[2], tok_s_parts[3]]
                tok_sp_s = P.dma(sy, "s", "ld", P.sS[:, qi:qi + 1, :],
                                 s_bf[:].rearrange("p (o d) -> p o d", o=1),
                                 toks=(tok_s_parts[-1],))
                tok_sbf_free = tok_sp_s
                tok_ob_free = P.tok("pe")
                tok_xn_free = tok_s_parts[-1]
                tok_otb_free = P.tok("pe")
            self.tok_d2_done = tok_sp_s

    def emit_e_f(self):
        P = self
        nc = self.nc
        g, v, a, t, sy = self.g, self.v, self.a, self.t, self.sy
        with ExitStack() as phase_ctx:
            acc = phase_ctx.enter_context(nc.sbuf_tensor("eacc", [128, 8 * D], BF16))
            w1_sb = phase_ctx.enter_context(nc.sbuf_tensor("ew1", [128, DC * H], BF16))
            w2_sb = phase_ctx.enter_context(nc.sbuf_tensor("ew2", [128, HC * D], BF16))
            hT = phase_ctx.enter_context(nc.sbuf_tensor("ehT", [128, HC * 1024], BF16))
            xnTh = phase_ctx.enter_context(nc.sbuf_tensor("exnTh", [128, DC * 1024], BF16))
            b1e = phase_ctx.enter_context(nc.sbuf_tensor("eb1", [32, H], BF16))
            b2e = phase_ctx.enter_context(nc.sbuf_tensor("eb2", [32, D], BF16))
            fs = phase_ctx.enter_context(nc.sbuf_tensor("fs", [128, D], BF16))
            fo = phase_ctx.enter_context(nc.sbuf_tensor("fo", [128, D], F32))
            pH = phase_ctx.enter_context(nc.psum_tensor("epH", [128, 2 * 512], F32))
            pY = phase_ctx.enter_context(nc.psum_tensor("epY", [128, 4 * 512], F32))
            tok_w_free = None
            P.phase_start()
            tok_hT_free = None
            tok_xnT_free = None
            evacH = [None, None]
            tok_pY = {}
            g.memset(b1e[:], 0.0)
            P.inc(g.memset(b2e[:], 0.0), "gd")
            tok_fs_free = None
            tok_fo_free = None
            last_store = None
            tok_acc_free = None
            tok_xcopy = P.dma(sy, "s", "ld", P.out, P.xq)
            for h in range(2):
                tok_acc = {}
                tok_xnT = P.dma(sy, "s", "ld",
                                xnTh[:].rearrange("p (dc t) -> p dc t", dc=DC),
                                P.xnTS[:, :, h * 1024:(h + 1) * 1024],
                                toks=(tok_xnT_free, self.tok_c1_done))
                for e in range(E):
                    tok_w1 = P.dma(g, "g", "gd", w1_sb[:].rearrange("p (dc h) -> p dc h", dc=DC),
                                   P.w1T[e * D:(e + 1) * D, :].rearrange("(dc p) h -> p dc h", p=128),
                                   toks=(tok_w_free,))
                    tok_w2 = P.dma(g, "g", "gd", w2_sb[:].rearrange("p (hc d) -> p hc d", hc=HC),
                                   P.w2T[e * H:(e + 1) * H, :].rearrange("(hc p) d -> p hc d", p=128))
                    tok_b1 = P.dma(g, "g", "gd", b1e[0:1, :], P.b1[e:e + 1, :])
                    tok_b2 = P.dma(g, "g", "gd", b2e[0:1, :], P.b2[e:e + 1, :])
                    for s in range(2):
                        for hc in range(HC):
                            bank = hc % 2
                            P.w(t, "t", tok_w1)
                            P.w(t, "t", tok_b1)
                            P.w(t, "t", tok_xnT)
                            P.w(t, "t", tok_hT_free)
                            P.w(t, "t", evacH[bank])
                            for dc in range(DC):
                                t.matmul(pH[:, bank * 512:(bank + 1) * 512],
                                         w1_sb[:, dc * H + hc * 128: dc * H + (hc + 1) * 128],
                                         xnTh[:, dc * 1024 + s * 512: dc * 1024 + (s + 1) * 512],
                                         start=(dc == 0), stop=False)
                            ins = t.matmul(pH[:, bank * 512:(bank + 1) * 512],
                                           b1e[0:32, hc * 128:(hc + 1) * 128],
                                           P.ones_row[0:32, 0:512], start=False, stop=True)
                            tk_pe = P.inc(ins, "pe")
                            P.w(a, "a", tk_pe)
                            ins = a.activation(hT[:, hc * 1024 + s * 512: hc * 1024 + (s + 1) * 512],
                                               pH[:, bank * 512:(bank + 1) * 512], AF.Silu,
                                               bias=0.0, scale=1.0)
                            evacH[bank] = P.inc(ins, "ac")
                    tok_hT_ready = P.tok("ac")
                    for tloc in range(8):
                        ti = h * 8 + tloc
                        for dc2 in range(4):
                            P.w(t, "t", tok_w2)
                            P.w(t, "t", tok_b2)
                            P.w(t, "t", tok_hT_ready)
                            P.w(t, "t", tok_pY.get(dc2))
                            for hc in range(HC):
                                t.matmul(pY[:, dc2 * 512:(dc2 + 1) * 512],
                                         hT[:, hc * 1024 + tloc * 128: hc * 1024 + (tloc + 1) * 128],
                                         w2_sb[:, hc * D + dc2 * 512: hc * D + (dc2 + 1) * 512],
                                         start=(hc == 0), stop=False)
                            ins = t.matmul(pY[:, dc2 * 512:(dc2 + 1) * 512],
                                           P.ones_row[0:32, 0:128],
                                           b2e[0:32, dc2 * 512:(dc2 + 1) * 512],
                                           start=False, stop=True)
                            tk_pe = P.inc(ins, "pe")
                            P.w(v, "v", tk_pe)
                            cslice = P.comb[:, ti * E + e: ti * E + e + 1]
                            dst = acc[:, tloc * D + dc2 * 512: tloc * D + (dc2 + 1) * 512]
                            if e == 0:
                                P.w(v, "v", tok_acc_free)
                                ins = v.tensor_scalar_mul(dst, pY[:, dc2 * 512:(dc2 + 1) * 512], cslice)
                            else:
                                P.w(v, "v", tok_acc.get((tloc, dc2)))
                                ins = v.scalar_tensor_tensor(dst, pY[:, dc2 * 512:(dc2 + 1) * 512],
                                                             cslice, dst, ALU.mult, ALU.add)
                            tok_acc[(tloc, dc2)] = P.inc(ins, "dv")
                            tok_pY[dc2] = tok_acc[(tloc, dc2)]
                    tok_w_free = P.tok("pe")
                    tok_hT_free = P.tok("pe")
                tok_xnT_free = P.tok("pe")
                for tloc in range(NT // 2):
                    ti = h * 8 + tloc
                    tok_fs = P.dma(sy, "s", "ld", fs[:].rearrange("p (o d) -> p o d", o=1),
                                   P.sS[:, ti:ti + 1, :],
                                   toks=(tok_fs_free, self.tok_d2_done))
                    P.w(a, "a", tok_acc[(tloc, 3)])
                    asl = acc[:, tloc * D:(tloc + 1) * D]
                    tok_gs = P.inc(a.activation(asl, asl, AF.Silu, bias=0.0, scale=1.0), "ac")
                    P.w(v, "v", tok_gs)
                    P.w(v, "v", tok_fs)
                    P.w(v, "v", tok_fo_free)
                    tok_o = P.inc(v.tensor_tensor(fo[:], fs[:], asl, ALU.mult), "dv")
                    tok_fs_free = tok_o
                    P.w(g, "g", tok_o)
                    P.w(g, "g", tok_xcopy)
                    ins = g.dma_start(P.out.rearrange("(s p) d -> p s d", p=128)[:, ti:ti + 1, :],
                                      fo[:].rearrange("p (o d) -> p o d", o=1),
                                      accum_op=ALU.add)
                    last_store = P.inc(ins, "gd", 16)
                    tok_fo_free = last_store
                tok_acc_free = P.tok("dv")
            if P.debug:
                P.w(g, "g", P.tok("dv"))
                P.inc(g.dma_start(P.dbg_acc, acc[:]), "gd", 16)
            P.w(sy, "s", last_store)
            P.w(sy, "s", P.tok("ld"))


def build_program(debug=False):
    return Prog(debug=debug).build()


def kernel(L0_attn_x, L0_attn_qkv, L0_attn_o_proj, L0_attn_gamma, L0_attn_beta,
           L0_ffn_gate_w, L0_ffn_gate_b, L0_ffn_bias, L0_ffn_w1, L0_ffn_b1,
           L0_ffn_w2, L0_ffn_b2):
    x = np.ascontiguousarray(np.asarray(L0_attn_x, dtype=np.float32))
    qkv = np.ascontiguousarray(np.asarray(L0_attn_qkv, dtype=np.float32))
    o_proj = np.ascontiguousarray(np.asarray(L0_attn_o_proj, dtype=np.float32))
    gamma = np.asarray(L0_attn_gamma, dtype=np.float32)
    beta = np.asarray(L0_attn_beta, dtype=np.float32)
    gate_w = np.asarray(L0_ffn_gate_w, dtype=np.float32)
    gate_b = np.asarray(L0_ffn_gate_b, dtype=np.float32)
    gbias = np.asarray(L0_ffn_bias, dtype=np.float32)
    w1 = np.asarray(L0_ffn_w1, dtype=np.float32)
    b1 = np.ascontiguousarray(np.asarray(L0_ffn_b1, dtype=np.float32))
    w2 = np.asarray(L0_ffn_w2, dtype=np.float32)
    b2 = np.ascontiguousarray(np.asarray(L0_ffn_b2, dtype=np.float32))

    shared = dict(
        qkv=qkv,
        o_proj=o_proj,
        gammaT=np.ascontiguousarray(gamma.reshape(DC, 128).T),
        betaT=np.ascontiguousarray(beta.reshape(DC, 128).T),
        gate_wT=np.ascontiguousarray(gate_w.T),
        gate_b=np.ascontiguousarray(np.broadcast_to(gate_b.reshape(1, E), (128, E))),
        gate_bias=np.ascontiguousarray(np.broadcast_to(gbias.reshape(1, E), (128, E))),
        w1T=np.ascontiguousarray(w1.transpose(0, 2, 1).reshape(E * D, H)),
        w2T=np.ascontiguousarray(w2.transpose(0, 2, 1).reshape(E * H, D)),
        b1=b1, b2=b2,
    )
    in_maps = []
    metas = []
    for core in range(8):
        bi, c = core // 4, core % 4
        cA, cB = c, 7 - c
        xb = np.ascontiguousarray(x[bi])
        xq = np.ascontiguousarray(np.concatenate(
            [xb[cA * 1024:(cA + 1) * 1024], xb[cB * 1024:(cB + 1) * 1024]], axis=0))
        xqT = np.ascontiguousarray(xq.T)
        qpos = np.concatenate([np.arange(cA * 1024, (cA + 1) * 1024),
                               np.arange(cB * 1024, (cB + 1) * 1024)]).astype(np.float32)
        qpos_t = np.ascontiguousarray(qpos.reshape(NT, 128).T)
        m = dict(xb=xb, xq=xq, xqT=xqT, qpos=qpos_t)
        m.update(shared)
        in_maps.append(m)
        metas.append((bi, cA, cB))

    nc = build_program()
    res = run_bass_kernel_spmd(nc, in_maps, core_ids=list(range(8)))

    out = np.empty((2, S, D), dtype=np.float32)
    for core in range(8):
        bi, cA, cB = metas[core]
        oc = res.results[core]["out"]
        out[bi, cA * 1024:(cA + 1) * 1024] = oc[0:1024]
        out[bi, cB * 1024:(cB + 1) * 1024] = oc[1024:2048]
    return out


# version 03
# speedup vs baseline: 1.0255x; 1.0255x over previous
import numpy as np
from contextlib import ExitStack

import concourse.bass as bass
import concourse.mybir as mybir
from concourse.bass_utils import run_bass_kernel_spmd

F32 = mybir.dt.float32
BF16 = mybir.dt.bfloat16
I16 = mybir.dt.int16
AF = mybir.ActivationFunctionType
ALU = mybir.AluOpType
AX = mybir.AxisListType

D = 2048
S = 8192
TOWN = 2048
H = 1408
E = 8
EPS = 1e-5
SCALE = 1.0 / float(np.sqrt(2048.0))

DC = 16
HC = 11
NSB = 16
NT = 16


class Prog:
    def __init__(self, debug=False):
        self.debug = debug
        nc = bass.Bass("TRN2", target_bir_lowering=False)
        self.nc = nc
        self.ctx = ExitStack()
        dt = nc.dram_tensor
        self.xb = dt("xb", [S, D], F32, kind="ExternalInput").ap()
        self.xq = dt("xq", [TOWN, D], F32, kind="ExternalInput").ap()
        self.xqT = dt("xqT", [D, TOWN], F32, kind="ExternalInput").ap()
        self.qpos_i = dt("qpos", [128, NT], F32, kind="ExternalInput").ap()
        self.qkv = dt("qkv", [D, 3 * D], F32, kind="ExternalInput").ap()
        self.o_proj = dt("o_proj", [D, D], F32, kind="ExternalInput").ap()
        self.gammaT_i = dt("gammaT", [128, DC], F32, kind="ExternalInput").ap()
        self.betaT_i = dt("betaT", [128, DC], F32, kind="ExternalInput").ap()
        self.gate_wT = dt("gate_wT", [D, E], F32, kind="ExternalInput").ap()
        self.gate_b = dt("gate_b", [128, E], F32, kind="ExternalInput").ap()
        self.gate_bias = dt("gate_bias", [128, E], F32, kind="ExternalInput").ap()
        self.w1T = dt("w1T", [E * D, H], F32, kind="ExternalInput").ap()
        self.w2T = dt("w2T", [E * H, D], F32, kind="ExternalInput").ap()
        self.b1 = dt("b1", [E, H], F32, kind="ExternalInput").ap()
        self.b2 = dt("b2", [E, D], F32, kind="ExternalInput").ap()
        self.out = dt("out", [TOWN, D], F32, kind="ExternalOutput").ap()
        kind = "ExternalOutput" if debug else "Internal"
        self.xn2TS = dt("xn2TS", [128, DC, S], BF16, kind=kind).ap()
        self.xn2TownS = dt("xn2TownS", [128, DC, TOWN], BF16, kind=kind).ap()
        self.ktS = dt("ktS", [128, DC, S], BF16, kind=kind).ap()
        self.vS = dt("vS", [128, S // 128, D], BF16, kind=kind).ap()
        self.xnS = dt("xnS", [128, NT, D], BF16, kind=kind).ap()
        self.xnTS = dt("xnTS", [128, DC, TOWN], BF16, kind=kind).ap()
        self.qTS = dt("qTS", [128, DC, TOWN], BF16, kind=kind).ap()
        self.oS = dt("oS", [128, NT, D], BF16, kind=kind).ap()
        self.sS = dt("sS", [128, NT, D], BF16, kind=kind).ap()
        if debug:
            self.dbg_comb = dt("dbg_comb", [128, NT * E], F32, kind=kind).ap()
            self.dbg_acc = dt("dbg_acc", [128, 8 * D], F32, kind=kind).ap()
            self.dbg_raw = dt("dbg_raw", [128, NT * E], F32, kind=kind).ap()
            self.dbg_sc = dt("dbg_sc", [128, NT * E], F32, kind=kind).ap()
            self.dbg_gw = dt("dbg_gw", [128, DC * E], F32, kind=kind).ap()
            self.dbg_cons = dt("dbg_cons", [128, 2 * E], F32, kind=kind).ap()

        self.sems = {}
        self.ct = {}
        for name in ["ld", "gd", "pe", "dv", "ac"]:
            self.sems[name] = self.ctx.enter_context(nc.semaphore(name))
            self.ct[name] = 0
        self.lastw = {}

        sb = lambda n, shp, d=F32: self.ctx.enter_context(nc.sbuf_tensor(n, shp, d))
        self.ident = sb("ident", [128, 128], BF16)
        self.ones_row = sb("ones_row", [32, 512], BF16)
        self.ones128f = sb("ones128f", [128, 128], F32)
        self.ones_colb = sb("ones_colb", [128, 1], BF16)
        self.ones_colf = sb("ones_colf", [128, 1], F32)
        self.gammaT = sb("gammaT_s", [128, DC])
        self.betaT = sb("betaT_s", [128, DC])
        self.qpos = sb("qpos_s", [128, NT])
        self.gate_wT_sb = sb("gate_wT_sb", [128, DC * E])
        self.gb_bc = sb("gb_bc", [128, E])
        self.gbias_bc = sb("gbias_bc", [128, E])
        self.rowsum_bc = sb("rowsum_bc", [128, E])
        self.gbb_bc = sb("gbb_bc", [128, E])
        self.comb = sb("comb", [128, NT * E])
        for n in ["sx", "sxx", "mu", "m2", "musq", "var", "a1", "l1", "nh", "r1",
                  "ia1", "var2", "a2", "l2", "nh2", "r2", "rr", "mrr", "mr1n"]:
            setattr(self, n, sb(n, [128, 4]))
        for n in ["g_t1", "g_sc", "g_eq1", "g_msk", "g_eq2", "g_c1"]:
            setattr(self, n, sb(n, [128, E]))
        self.dbg_raw_sb = sb("dbg_raw_sb", [128, NT * E])
        self.dbg_sc_sb = sb("dbg_sc_sb", [128, NT * E])
        for n in ["g_m1", "g_m2", "g_d12", "g_e", "g_den", "g_g1", "g_g2"]:
            setattr(self, n, sb(n, [128, 1]))

        self.g = nc.gpsimd
        self.v = nc.vector
        self.a = nc.scalar
        self.t = nc.tensor
        self.sy = nc.sync

    def inc(self, ins, sem, n=1):
        ins.then_inc(self.sems[sem], n)
        self.ct[sem] += n
        return (sem, self.ct[sem])

    def tok(self, sem):
        return (sem, self.ct[sem])

    def w(self, eng, engname, tok):
        if tok is None:
            return
        sem, val = tok
        if val <= 0:
            return
        key = (engname, sem)
        if self.lastw.get(key, 0) >= val:
            return
        eng.wait_ge(self.sems[sem], val)
        self.lastw[key] = val

    def dma(self, qeng, qname, sem, dst, src, toks=()):
        for t in toks:
            self.w(qeng, qname, t)
        ins = qeng.dma_start(dst, src)
        return self.inc(ins, sem, 16)

    def phase_start(self):
        toks = [self.tok(s) for s in ("pe", "dv", "ac", "ld", "gd")]
        for eng, name in ((self.t, "t"), (self.g, "g"), (self.sy, "s")):
            for tk in toks:
                if (name, tk[0]) not in (("t", "pe"), ("g", "gd"), ("s", "ld")):
                    self.w(eng, name, tk)

    def emit_stats(self, x4, x4_tok, want_mr1n):
        P, v, a = self, self.v, self.a
        t_sx = t_sxx = None
        for j in range(4):
            P.w(v, "v", x4_tok)
            t_sx = P.inc(v.reduce_sum(P.sx[:, j:j + 1], x4[:, j * D:(j + 1) * D], AX.X), "dv")
        for j in range(4):
            P.w(a, "a", x4_tok)
            t_sxx = P.inc(a.activation(P.sqdump[:], x4[:, j * D:(j + 1) * D], AF.Square,
                                       bias=0.0, scale=1.0, accum_out=P.sxx[:, j:j + 1]), "ac")
        P.w(v, "v", t_sx)
        tk = P.inc(v.tensor_scalar_mul(P.mu[:], P.sx[:], 1.0 / D), "dv")
        P.w(v, "v", t_sxx)
        P.w(v, "v", tk)
        tk = P.inc(v.tensor_scalar_mul(P.m2[:], P.sxx[:], 1.0 / D), "dv")
        P.w(v, "v", tk)
        tk = P.inc(v.tensor_tensor(P.musq[:], P.mu[:], P.mu[:], ALU.mult), "dv")
        P.w(v, "v", tk)
        tk = P.inc(v.tensor_tensor(P.var[:], P.m2[:], P.musq[:], ALU.subtract), "dv")
        P.w(v, "v", tk)
        t_a1 = P.inc(v.tensor_scalar_add(P.a1[:], P.var[:], EPS), "dv")
        P.w(a, "a", t_a1)
        tk = P.inc(a.activation(P.l1[:], P.a1[:], AF.Ln), "ac")
        P.w(v, "v", tk)
        tk = P.inc(v.tensor_scalar_mul(P.nh[:], P.l1[:], -0.5), "dv")
        P.w(a, "a", tk)
        t_r1 = P.inc(a.activation(P.r1[:], P.nh[:], AF.Exp), "ac")
        P.w(v, "v", t_a1)
        tk = P.inc(v.reciprocal(P.ia1[:], P.a1[:]), "dv")
        P.w(v, "v", tk)
        tk = P.inc(v.tensor_tensor(P.var2[:], P.var[:], P.ia1[:], ALU.mult), "dv")
        P.w(v, "v", tk)
        tk = P.inc(v.tensor_scalar_add(P.a2[:], P.var2[:], EPS), "dv")
        P.w(a, "a", tk)
        tk = P.inc(a.activation(P.l2[:], P.a2[:], AF.Ln), "ac")
        P.w(v, "v", tk)
        tk = P.inc(v.tensor_scalar_mul(P.nh2[:], P.l2[:], -0.5), "dv")
        P.w(a, "a", tk)
        tk = P.inc(a.activation(P.r2[:], P.nh2[:], AF.Exp), "ac")
        P.w(v, "v", tk)
        P.w(v, "v", t_r1)
        tk = P.inc(v.tensor_tensor(P.rr[:], P.r1[:], P.r2[:], ALU.mult), "dv")
        P.w(v, "v", tk)
        t_mrr = P.inc(v.scalar_tensor_tensor(P.mrr[:], P.mu[:], -1.0, P.rr[:], ALU.mult, ALU.mult), "dv")
        t_r1mu = None
        if want_mr1n:
            P.w(v, "v", t_mrr)
            t_r1mu = P.inc(v.scalar_tensor_tensor(P.mr1n[:], P.mu[:], -1.0, P.r1[:], ALU.mult, ALU.mult), "dv")
        return t_mrr, t_r1mu

    def emit_transposes(self, src_sb, src_tok, pT, evac_tok, dst, dst_dc_stride,
                        gamma, beta, n_tiles=4, free_tok=None):
        P, t, v = self, self.t, self.v
        last = None
        for dc in range(DC):
            bank = dc % 2
            P.w(t, "t", evac_tok[bank])
            P.w(t, "t", src_tok)
            if free_tok is not None:
                P.w(t, "t", free_tok)
            for j in range(n_tiles):
                ins = t.transpose(pT[bank][:, j * 128:(j + 1) * 128],
                                  src_sb[:, j * D + dc * 128: j * D + (dc + 1) * 128],
                                  P.ident[:])
            tk_pe = P.inc(ins, "pe")
            P.w(v, "v", tk_pe)
            dsl = dst[:, dc * dst_dc_stride: dc * dst_dc_stride + n_tiles * 128]
            if gamma is not None:
                ins = v.tensor_scalar(dsl, pT[bank][:, 0:n_tiles * 128],
                                      gamma[:, dc:dc + 1], beta[:, dc:dc + 1],
                                      ALU.mult, ALU.add)
            else:
                ins = v.tensor_copy(dsl, pT[bank][:, 0:n_tiles * 128])
            last = P.inc(ins, "dv")
            evac_tok[bank] = last
        return last

    def emit_proj_kc(self, w_sb, src, pX, evac_toks, dstbuf, dst_free_tok, ready_tok):
        P, t, a = self, self.t, self.a
        evac = evac_toks
        for kc in range(DC):
            bank = kc % 2
            P.w(t, "t", evac[bank])
            P.w(t, "t", ready_tok)
            for dc in range(DC):
                ins = t.matmul(pX[:, bank * 512:(bank + 1) * 512],
                               w_sb[:, dc * D + kc * 128: dc * D + (kc + 1) * 128],
                               src[:, dc * 512:(dc + 1) * 512],
                               start=(dc == 0), stop=(dc == DC - 1))
            tk_pe = P.inc(ins, "pe")
            P.w(a, "a", tk_pe)
            P.w(a, "a", dst_free_tok)
            ins = a.activation(dstbuf[:, kc * 512:(kc + 1) * 512],
                               pX[:, bank * 512:(bank + 1) * 512], AF.Copy,
                               bias=0.0, scale=1.0)
            evac[bank] = P.inc(ins, "ac")
        return evac[(DC - 1) % 2]

    def build(self):
        self.emit_prep()
        self.emit_a()
        self.emit_b()
        self.emit_c1()
        self.emit_c2()
        self.emit_d()
        self.emit_d2()
        self.emit_e_f()
        self.ctx.close()
        return self.nc

    def emit_prep(self):
        P = self
        g, v, a, t, sy = self.g, self.v, self.a, self.t, self.sy
        g.memset(P.ident[:], 0.0)
        g.affine_select(P.ident[:], P.ident[:], [[1, 128]], ALU.not_equal, 1.0,
                        base=0, channel_multiplier=-1)
        g.memset(P.ones_row[:], 0.0)
        g.memset(P.ones_row[0:1, :], 1.0)
        g.memset(P.ones128f[:], 1.0)
        g.memset(P.ones_colb[:], 1.0)
        tok_consts = P.inc(g.memset(P.ones_colf[:], 1.0), "gd")

        P.dma(sy, "s", "ld", P.gammaT[:], P.gammaT_i)
        P.dma(sy, "s", "ld", P.betaT[:], P.betaT_i)
        P.tok_qpos = P.dma(sy, "s", "ld", P.qpos[:], P.qpos_i)
        tok_gw = P.dma(sy, "s", "ld",
                       P.gate_wT_sb[:].rearrange("p (dc e) -> p dc e", dc=DC),
                       P.gate_wT.rearrange("(dc p) e -> p dc e", p=128))
        tok_gb = P.dma(sy, "s", "ld", P.gb_bc[:], P.gate_b)
        tok_gbias = P.dma(sy, "s", "ld", P.gbias_bc[:], P.gate_bias)
        P.tok_gamma_ready = P.tok_qpos

        with self.nc.psum_tensor("pPrep", [128, 512], F32) as pP:
            P.w(t, "t", tok_gw)
            P.w(t, "t", tok_consts)
            for dc in range(DC):
                ins = t.matmul(pP[:, 0:E], P.ones128f[:], P.gate_wT_sb[:, dc * E:(dc + 1) * E],
                               start=(dc == 0), stop=(dc == DC - 1))
            tk_pe = P.inc(ins, "pe")
            P.w(v, "v", tk_pe)
            P.inc(v.tensor_copy(P.rowsum_bc[:], pP[:, 0:E]), "dv")
            P.w(v, "v", tok_gb)
            P.w(v, "v", tok_gbias)
            P.w(v, "v", P.tok("dv"))
            P.tok_gate_ready = P.inc(v.tensor_tensor(P.gbb_bc[:], P.gb_bc[:], P.gbias_bc[:], ALU.add), "dv")

    def emit_a(self):
        P = self
        nc = self.nc
        g, v, a, t, sy = self.g, self.v, self.a, self.t, self.sy
        with ExitStack() as phase_ctx:
            x4 = phase_ctx.enter_context(nc.sbuf_tensor("ax4", [128, 4 * D], F32))
            xnl2 = phase_ctx.enter_context(nc.sbuf_tensor("axnl2", [128, 4 * D], BF16))
            xn2T = phase_ctx.enter_context(nc.sbuf_tensor("axn2T", [128, DC * 512], BF16))
            ktbuf = phase_ctx.enter_context(nc.sbuf_tensor("aktbuf", [128, DC * 512], BF16))
            wk_sb = phase_ctx.enter_context(nc.sbuf_tensor("awk", [128, DC * D], BF16))
            self.sqdump = phase_ctx.enter_context(nc.sbuf_tensor("asq", [128, D], BF16))
            pK = phase_ctx.enter_context(nc.psum_tensor("apK", [128, 2 * 512], F32))
            pT0 = phase_ctx.enter_context(nc.psum_tensor("apT0", [128, 512], BF16))
            pT1 = phase_ctx.enter_context(nc.psum_tensor("apT1", [128, 512], BF16))
            pT = [pT0, pT1]
            P.phase_start()
            tok_wk = P.dma(g, "g", "gd", wk_sb[:].rearrange("p (dc f) -> p dc f", dc=DC),
                           P.qkv[:, D:2 * D].rearrange("(dc p) f -> p dc f", p=128))
            tok_x4_free = None
            tok_sp_xn2T = tok_sp_kt = None
            evac_tok = [None, None]
            evacK = [None, None]
            for sb_i in range(NSB):
                t0 = sb_i * 512
                tok_x4 = P.dma(sy, "s", "ld",
                               x4[:].rearrange("p (s d) -> p s d", s=4),
                               P.xb.rearrange("(s p) d -> p s d", p=128)[:, sb_i * 4:(sb_i + 1) * 4, :],
                               toks=(tok_x4_free, P.tok_gamma_ready))
                t_mrr, _ = P.emit_stats(x4, tok_x4, want_mr1n=False)
                last_xnl2 = None
                for j in range(4):
                    P.w(v, "v", t_mrr)
                    ins = v.tensor_scalar(xnl2[:, j * D:(j + 1) * D], x4[:, j * D:(j + 1) * D],
                                          P.rr[:, j:j + 1], P.mrr[:, j:j + 1], ALU.mult, ALU.add)
                    last_xnl2 = P.inc(ins, "dv")
                tok_x4_free = last_xnl2
                tok_xn2T_ready = P.emit_transposes(xnl2, last_xnl2, pT, evac_tok, xn2T, 512,
                                                  P.gammaT, P.betaT, free_tok=tok_sp_xn2T)
                P.w(t, "t", tok_wk)
                tok_kt_ready = P.emit_proj_kc(wk_sb, xn2T, pK, evacK, ktbuf,
                                              tok_sp_kt, tok_xn2T_ready)
                tok_sp_xn2T = P.dma(sy, "s", "ld", P.xn2TS[:, :, t0:t0 + 512],
                                    xn2T[:].rearrange("p (dc t) -> p dc t", dc=DC),
                                    toks=(tok_xn2T_ready,))
                tok_sp_kt = P.dma(sy, "s", "ld", P.ktS[:, :, t0:t0 + 512],
                                  ktbuf[:].rearrange("p (kc t) -> p kc t", kc=DC),
                                  toks=(tok_kt_ready,))
            self.tok_a_done = tok_sp_kt

    def emit_b(self):
        P = self
        nc = self.nc
        g, v, a, t, sy = self.g, self.v, self.a, self.t, self.sy
        with ExitStack() as phase_ctx:
            xt0 = phase_ctx.enter_context(nc.sbuf_tensor("bxt0", [128, DC * 512], BF16))
            xt1 = phase_ctx.enter_context(nc.sbuf_tensor("bxt1", [128, DC * 512], BF16))
            vbuf = phase_ctx.enter_context(nc.sbuf_tensor("bvbuf", [128, 4 * D], BF16))
            wv_sb = phase_ctx.enter_context(nc.sbuf_tensor("bwv", [128, DC * D], BF16))
            pV = phase_ctx.enter_context(nc.psum_tensor("bpV", [128, 2 * 512], F32))
            xts = [xt0, xt1]
            P.phase_start()
            tok_wv = P.dma(g, "g", "gd", wv_sb[:].rearrange("p (dc f) -> p dc f", dc=DC),
                           P.qkv[:, 2 * D:3 * D].rearrange("(dc p) f -> p dc f", p=128))
            tok_xt_free = [None, None]
            tok_sp_v = None
            evacV = [None, None]
            for sb_i in range(NSB):
                t0 = sb_i * 512
                bk = sb_i % 2
                tok_xt = P.dma(sy, "s", "ld", xts[bk][:].rearrange("p (dc t) -> p dc t", dc=DC),
                               P.xn2TS[:, :, t0:t0 + 512],
                               toks=(tok_xt_free[bk], self.tok_a_done))
                P.w(t, "t", tok_wv)
                nvb = 0
                tok_v_ready = None
                for j in range(4):
                    for dvc in range(4):
                        bank = nvb % 2
                        nvb += 1
                        P.w(t, "t", evacV[bank])
                        P.w(t, "t", tok_xt)
                        for dc in range(DC):
                            ins = t.matmul(pV[:, bank * 512:(bank + 1) * 512],
                                           xts[bk][:, dc * 512 + j * 128: dc * 512 + (j + 1) * 128],
                                           wv_sb[:, dc * D + dvc * 512: dc * D + (dvc + 1) * 512],
                                           start=(dc == 0), stop=(dc == DC - 1))
                        tk_pe = P.inc(ins, "pe")
                        P.w(a, "a", tk_pe)
                        P.w(a, "a", tok_sp_v)
                        ins = a.activation(vbuf[:, j * D + dvc * 512: j * D + (dvc + 1) * 512],
                                           pV[:, bank * 512:(bank + 1) * 512], AF.Copy,
                                           bias=0.0, scale=1.0)
                        evacV[bank] = P.inc(ins, "ac")
                        tok_v_ready = evacV[bank]
                tok_xt_free[bk] = P.tok("pe")
                tok_sp_v = P.dma(sy, "s", "ld", P.vS[:, sb_i * 4:(sb_i + 1) * 4, :],
                                 vbuf[:].rearrange("p (s d) -> p s d", s=4),
                                 toks=(tok_v_ready,))
            self.tok_b_done = tok_sp_v

    def emit_c1(self):
        P = self
        nc = self.nc
        g, v, a, t, sy = self.g, self.v, self.a, self.t, self.sy
        with ExitStack() as phase_ctx:
            x4 = phase_ctx.enter_context(nc.sbuf_tensor("cx4", [128, 4 * D], F32))
            xqT4 = phase_ctx.enter_context(nc.sbuf_tensor("cxqT4", [128, DC * 512], F32))
            xnl2 = phase_ctx.enter_context(nc.sbuf_tensor("cxnl2", [128, 4 * D], BF16))
            xn2T = phase_ctx.enter_context(nc.sbuf_tensor("cxn2T", [128, DC * 512], BF16))
            xn_b = phase_ctx.enter_context(nc.sbuf_tensor("cxn", [128, 4 * D], BF16))
            xnT = phase_ctx.enter_context(nc.sbuf_tensor("cxnT", [128, DC * 512], BF16))
            self.sqdump = phase_ctx.enter_context(nc.sbuf_tensor("csq", [128, D], BF16))
            pRaw = phase_ctx.enter_context(nc.psum_tensor("cpRaw", [128, 512], F32))
            pT0 = phase_ctx.enter_context(nc.psum_tensor("cpT0", [128, 512], BF16))
            pT1 = phase_ctx.enter_context(nc.psum_tensor("cpT1", [128, 512], BF16))
            pT = [pT0, pT1]
            P.phase_start()
            tok_x4_free = None
            tok_sp_xn = tok_sp_xnT = tok_sp_x2 = None
            tok_raw_read = None
            evac_tok = [None, None]
            for c in range(4):
                t0 = c * 512
                tok_x4 = P.dma(sy, "s", "ld",
                               x4[:].rearrange("p (s d) -> p s d", s=4),
                               P.xq.rearrange("(s p) d -> p s d", p=128)[:, c * 4:(c + 1) * 4, :],
                               toks=(tok_x4_free,))
                tok_xqT = P.dma(sy, "s", "ld",
                                xqT4[:].rearrange("p (dc t) -> p dc t", dc=DC),
                                P.xqT.rearrange("(dc p) t -> p dc t", p=128)[:, :, c * 512:(c + 1) * 512],
                                toks=(tok_raw_read,))
                t_mrr, t_r1mu = P.emit_stats(x4, tok_x4, want_mr1n=True)
                for j in range(4):
                    ti = c * 4 + j
                    P.w(t, "t", tok_xqT)
                    P.w(t, "t", P.tok_gate_ready)
                    P.w(t, "t", tok_raw_read)
                    for dc in range(DC):
                        ins = t.matmul(pRaw[:, 0:E],
                                       xqT4[:, dc * 512 + j * 128: dc * 512 + (j + 1) * 128],
                                       P.gate_wT_sb[:, dc * E:(dc + 1) * E],
                                       start=(dc == 0), stop=(dc == DC - 1))
                    tk_raw = P.inc(ins, "pe")
                    P.w(v, "v", tk_raw)
                    P.w(v, "v", t_r1mu)
                    if P.debug:
                        P.inc(v.tensor_copy(P.dbg_raw_sb[:, ti * E:(ti + 1) * E], pRaw[:, 0:E]), "dv")
                    tk = P.inc(v.scalar_tensor_tensor(P.g_t1[:], pRaw[:, 0:E], P.r1[:, j:j + 1],
                                                      P.gbb_bc[:], ALU.mult, ALU.add), "dv")
                    tok_raw_read = tk
                    P.w(v, "v", tk)
                    tk = P.inc(v.scalar_tensor_tensor(P.g_sc[:], P.rowsum_bc[:], P.mr1n[:, j:j + 1],
                                                      P.g_t1[:], ALU.mult, ALU.add), "dv")
                    P.w(v, "v", tk)
                    if P.debug:
                        P.inc(v.tensor_copy(P.dbg_sc_sb[:, ti * E:(ti + 1) * E], P.g_sc[:]), "dv")
                        P.w(v, "v", P.tok("dv"))
                    tk = P.inc(v.reduce_max(P.g_m1[:], P.g_sc[:], AX.X), "dv")
                    P.w(v, "v", tk)
                    tk = P.inc(v.tensor_scalar(P.g_eq1[:], P.g_sc[:], P.g_m1[:, 0:1], None, ALU.is_ge), "dv")
                    P.w(v, "v", tk)
                    tk = P.inc(v.scalar_tensor_tensor(P.g_msk[:], P.g_eq1[:], -1e30, P.g_sc[:],
                                                      ALU.mult, ALU.add), "dv")
                    P.w(v, "v", tk)
                    tk = P.inc(v.reduce_max(P.g_m2[:], P.g_msk[:], AX.X), "dv")
                    P.w(v, "v", tk)
                    tk = P.inc(v.tensor_scalar(P.g_eq2[:], P.g_msk[:], P.g_m2[:, 0:1], None, ALU.is_ge), "dv")
                    P.w(v, "v", tk)
                    tk = P.inc(v.tensor_tensor(P.g_d12[:], P.g_m1[:], P.g_m2[:], ALU.subtract), "dv")
                    P.w(a, "a", tk)
                    tk = P.inc(a.activation(P.g_e[:], P.g_d12[:], AF.Exp, bias=0.0, scale=-1.0), "ac")
                    P.w(v, "v", tk)
                    tk = P.inc(v.tensor_scalar_add(P.g_den[:], P.g_e[:], 1.0), "dv")
                    P.w(v, "v", tk)
                    tk = P.inc(v.reciprocal(P.g_g1[:], P.g_den[:]), "dv")
                    P.w(v, "v", tk)
                    tk = P.inc(v.tensor_scalar(P.g_g2[:], P.g_g1[:], -1.0, 1.0, ALU.mult, ALU.add), "dv")
                    P.w(v, "v", tk)
                    tk = P.inc(v.tensor_scalar_mul(P.g_c1[:], P.g_eq1[:], P.g_g1[:, 0:1]), "dv")
                    P.w(v, "v", tk)
                    P.inc(v.scalar_tensor_tensor(P.comb[:, ti * E:(ti + 1) * E], P.g_eq2[:],
                                                 P.g_g2[:, 0:1], P.g_c1[:], ALU.mult, ALU.add), "dv")
                last_xn = last_xnl2 = None
                for j in range(4):
                    P.w(v, "v", t_mrr)
                    P.w(v, "v", t_r1mu)
                    ins = v.tensor_scalar(xn_b[:, j * D:(j + 1) * D], x4[:, j * D:(j + 1) * D],
                                          P.r1[:, j:j + 1], P.mr1n[:, j:j + 1], ALU.mult, ALU.add)
                    last_xn = P.inc(ins, "dv")
                    ins = v.tensor_scalar(xnl2[:, j * D:(j + 1) * D], x4[:, j * D:(j + 1) * D],
                                          P.rr[:, j:j + 1], P.mrr[:, j:j + 1], ALU.mult, ALU.add)
                    last_xnl2 = P.inc(ins, "dv")
                tok_x4_free = last_xnl2
                tok_xnT_ready = P.emit_transposes(xn_b, last_xn, pT, evac_tok, xnT, 512,
                                                 None, None, free_tok=tok_sp_xnT)
                tok_x2_ready = P.emit_transposes(xnl2, last_xnl2, pT, evac_tok, xn2T, 512,
                                                 P.gammaT, P.betaT, free_tok=tok_sp_x2)
                tok_sp_xn = P.dma(sy, "s", "ld", P.xnS[:, c * 4:(c + 1) * 4, :],
                                  xn_b[:].rearrange("p (s d) -> p s d", s=4),
                                  toks=(last_xn,))
                tok_sp_xnT = P.dma(sy, "s", "ld", P.xnTS[:, :, t0:t0 + 512],
                                   xnT[:].rearrange("p (dc t) -> p dc t", dc=DC),
                                   toks=(tok_xnT_ready,))
                tok_sp_x2 = P.dma(sy, "s", "ld", P.xn2TownS[:, :, t0:t0 + 512],
                                  xn2T[:].rearrange("p (dc t) -> p dc t", dc=DC),
                                  toks=(tok_x2_ready,))
            self.tok_c1_done = tok_sp_x2
            if P.debug:
                P.w(g, "g", P.tok("dv"))
                P.inc(g.dma_start(P.dbg_comb, P.comb[:]), "gd", 16)
                P.inc(g.dma_start(P.dbg_raw, P.dbg_raw_sb[:]), "gd", 16)
                P.inc(g.dma_start(P.dbg_sc, P.dbg_sc_sb[:]), "gd", 16)
                P.inc(g.dma_start(P.dbg_gw, P.gate_wT_sb[:]), "gd", 16)
                P.inc(g.dma_start(P.dbg_cons[:, 0:E], P.rowsum_bc[:]), "gd", 16)
                P.inc(g.dma_start(P.dbg_cons[:, E:2 * E], P.gbb_bc[:]), "gd", 16)

    def emit_c2(self):
        P = self
        nc = self.nc
        g, v, a, t, sy = self.g, self.v, self.a, self.t, self.sy
        with ExitStack() as phase_ctx:
            xt0 = phase_ctx.enter_context(nc.sbuf_tensor("qxt0", [128, DC * 512], BF16))
            xt1 = phase_ctx.enter_context(nc.sbuf_tensor("qxt1", [128, DC * 512], BF16))
            qtbuf = phase_ctx.enter_context(nc.sbuf_tensor("qqt", [128, DC * 512], BF16))
            wq_sb = phase_ctx.enter_context(nc.sbuf_tensor("qwq", [128, DC * D], BF16))
            pQ = phase_ctx.enter_context(nc.psum_tensor("qpQ", [128, 2 * 512], F32))
            xts = [xt0, xt1]
            P.phase_start()
            tok_wq = P.dma(g, "g", "gd", wq_sb[:].rearrange("p (dc f) -> p dc f", dc=DC),
                           P.qkv[:, 0:D].rearrange("(dc p) f -> p dc f", p=128))
            tok_xt_free = [None, None]
            tok_sp_qt = None
            evacQ = [None, None]
            for c in range(4):
                t0 = c * 512
                bk = c % 2
                tok_xt = P.dma(sy, "s", "ld", xts[bk][:].rearrange("p (dc t) -> p dc t", dc=DC),
                               P.xn2TownS[:, :, t0:t0 + 512],
                               toks=(tok_xt_free[bk], self.tok_c1_done))
                P.w(t, "t", tok_wq)
                tok_qt_ready = P.emit_proj_kc(wq_sb, xts[bk], pQ, evacQ, qtbuf,
                                              tok_sp_qt, tok_xt)
                tok_xt_free[bk] = P.tok("pe")
                tok_sp_qt = P.dma(sy, "s", "ld", P.qTS[:, :, t0:t0 + 512],
                                  qtbuf[:].rearrange("p (kc t) -> p kc t", kc=DC),
                                  toks=(tok_qt_ready,))
            self.tok_c2_done = tok_sp_qt

    def emit_d(self):
        P = self
        nc = self.nc
        g, v, a, t, sy = self.g, self.v, self.a, self.t, self.sy
        with ExitStack() as phase_ctx:
            scores = phase_ctx.enter_context(nc.sbuf_tensor("dscore", [128, S], BF16))
            Pb = phase_ctx.enter_context(nc.sbuf_tensor("dP", [128, S], BF16))
            mask = phase_ctx.enter_context(nc.sbuf_tensor("dmask", [128, S], BF16))
            ptb = phase_ctx.enter_context(nc.sbuf_tensor("dptb", [128, S], BF16))
            kvidx = phase_ctx.enter_context(nc.sbuf_tensor("dkvidx", [128, S], I16))
            kt0 = phase_ctx.enter_context(nc.sbuf_tensor("dkt0", [128, DC * 512], BF16))
            kt1 = phase_ctx.enter_context(nc.sbuf_tensor("dkt1", [128, DC * 512], BF16))
            vb0 = phase_ctx.enter_context(nc.sbuf_tensor("dvb0", [128, 4 * D], BF16))
            vb1 = phase_ctx.enter_context(nc.sbuf_tensor("dvb1", [128, 4 * D], BF16))
            qt = phase_ctx.enter_context(nc.sbuf_tensor("dqt", [128, DC * 128], BF16))
            ob = phase_ctx.enter_context(nc.sbuf_tensor("dob", [128, D], BF16))
            bm = phase_ctx.enter_context(nc.sbuf_tensor("dbm", [128, 16], F32))
            negm = phase_ctx.enter_context(nc.sbuf_tensor("dnegm", [128, 1], F32))
            negms = phase_ctx.enter_context(nc.sbuf_tensor("dnegms", [128, 1], F32))
            lT = phase_ctx.enter_context(nc.sbuf_tensor("dlT", [32, 128], F32))
            lcol = phase_ctx.enter_context(nc.sbuf_tensor("dlcol", [128, 2], F32))
            linv = phase_ctx.enter_context(nc.sbuf_tensor("dlinv", [128, 1], F32))
            pS = phase_ctx.enter_context(nc.psum_tensor("dpS", [128, 2 * 512], F32))
            pO = phase_ctx.enter_context(nc.psum_tensor("dpO", [128, 4 * 512], F32))
            pTr = phase_ctx.enter_context(nc.psum_tensor("dpT", [128, 512], BF16))
            pL = phase_ctx.enter_context(nc.psum_tensor("dpL", [128, 512], F32))
            kts = [kt0, kt1]
            P.phase_start()
            vbs = [vb0, vb1]
            g.memset(lT[:], 0.0)
            tok_kvidx = P.inc(g.iota(kvidx[:], [[1, S]], base=0, channel_multiplier=0,
                                     allow_small_or_imprecise_dtypes=True), "gd")
            tok_kt_free = [None, None]
            tok_vb_free = [None, None]
            tok_qt_free = None
            tok_ob_free = None
            tok_scores_lastread = None
            tok_P_free = None
            tok_ptb_free = None
            tok_mask_free = None
            tok_pO_free = None
            tok_pL_free = None
            tok_pTr_free = None
            tok_pS_free = [None, None]

            for qi in range(NT):
                L = 4096 if qi < 8 else S
                nblk = L // 512
                nch = L // 128
                tok_qt = P.dma(sy, "s", "ld", qt[:].rearrange("p (kc t) -> p kc t", kc=DC),
                               P.qTS[:, :, qi * 128:(qi + 1) * 128],
                               toks=(tok_qt_free, self.tok_c2_done))
                P.w(v, "v", tok_kvidx)
                P.w(v, "v", tok_mask_free)
                P.w(v, "v", P.tok_qpos)
                tok_mask = P.inc(v.tensor_scalar(mask[:, 0:L], kvidx[:, 0:L],
                                                 P.qpos[:, qi:qi + 1], None, ALU.is_le), "dv")
                tok_sc_evac = [None] * nblk
                for b in range(nblk):
                    bk = b % 2
                    tok_kt = P.dma(sy, "s", "ld",
                                   kts[bk][:].rearrange("p (kc t) -> p kc t", kc=DC),
                                   P.ktS[:, :, b * 512:(b + 1) * 512],
                                   toks=(tok_kt_free[bk], self.tok_b_done))
                    P.w(t, "t", tok_kt)
                    P.w(t, "t", tok_qt)
                    P.w(t, "t", tok_sc_evac[b - 2] if b >= 2 else tok_pS_free[bk])
                    for kc in range(DC):
                        ins = t.matmul(pS[:, bk * 512:(bk + 1) * 512],
                                       qt[:, kc * 128:(kc + 1) * 128],
                                       kts[bk][:, kc * 512:(kc + 1) * 512],
                                       start=(kc == 0), stop=(kc == DC - 1))
                    tk_pe = P.inc(ins, "pe")
                    tok_kt_free[bk] = tk_pe
                    P.w(a, "a", tk_pe)
                    P.w(a, "a", tok_scores_lastread)
                    P.w(a, "a", tok_P_free)
                    ins = a.activation(scores[:, b * 512:(b + 1) * 512],
                                       pS[:, bk * 512:(bk + 1) * 512], AF.Copy,
                                       bias=0.0, scale=1.0)
                    tok_sc_evac[b] = P.inc(ins, "ac")
                    P.w(v, "v", tok_sc_evac[b])
                    P.inc(v.reduce_max(bm[:, b:b + 1], scores[:, b * 512:(b + 1) * 512], AX.X), "dv")
                P.w(v, "v", P.tok("dv"))
                tk = P.inc(v.reduce_max(negm[:], bm[:, 0:nblk], AX.X, negate=True), "dv")
                tok_scores_lastread = tk
                P.w(v, "v", tk)
                tk_nm = P.inc(v.tensor_scalar_mul(negms[:], negm[:], SCALE), "dv")
                P.w(a, "a", tk_nm)
                P.w(a, "a", tok_P_free)
                tok_exp = P.inc(a.activation(Pb[:, 0:L], scores[:, 0:L], AF.Exp,
                                             bias=negms[:, 0:1], scale=SCALE), "ac")
                P.w(v, "v", tok_exp)
                P.w(v, "v", tok_mask)
                tok_Pm = P.inc(v.tensor_tensor(Pb[:, 0:L], Pb[:, 0:L], mask[:, 0:L], ALU.mult), "dv")
                tok_mask_free = tok_Pm
                tok_tr_evac = tok_pTr_free
                tok_ptb_chunk = [None] * nch
                P.w(t, "t", tok_ptb_free)
                for g4 in range(nch // 4):
                    P.w(t, "t", tok_tr_evac)
                    P.w(t, "t", tok_Pm)
                    for j in range(4):
                        k = g4 * 4 + j
                        ins = t.transpose(pTr[:, j * 128:(j + 1) * 128],
                                          Pb[:, k * 128:(k + 1) * 128], P.ident[:])
                    tk_pe = P.inc(ins, "pe")
                    P.w(v, "v", tk_pe)
                    ins = v.tensor_copy(ptb[:, g4 * 512:(g4 + 1) * 512], pTr[:, 0:512])
                    tok_tr_evac = P.inc(ins, "dv")
                    for j in range(4):
                        tok_ptb_chunk[g4 * 4 + j] = tok_tr_evac
                tok_pTr_free = tok_tr_evac
                tok_P_free = P.tok("pe")
                last_l = None
                last_pv = None
                for b in range(nblk):
                    bk = b % 2
                    tok_vb = P.dma(sy, "s", "ld",
                                   vbs[bk][:].rearrange("p (s d) -> p s d", s=4),
                                   P.vS[:, b * 4:(b + 1) * 4, :],
                                   toks=(tok_vb_free[bk],))
                    for s4 in range(4):
                        k = b * 4 + s4
                        P.w(t, "t", tok_ptb_chunk[k])
                        if k == 0:
                            P.w(t, "t", tok_pL_free)
                            P.w(t, "t", tok_pO_free)
                        ins = t.matmul(pL[0:1, 0:128], P.ones_colb[:],
                                       ptb[:, k * 128:(k + 1) * 128],
                                       start=(k == 0), stop=(k == nch - 1))
                        if k == nch - 1:
                            last_l = P.inc(ins, "pe")
                        P.w(t, "t", tok_vb)
                        for dvc in range(4):
                            ins = t.matmul(pO[:, dvc * 512:(dvc + 1) * 512],
                                           ptb[:, k * 128:(k + 1) * 128],
                                           vbs[bk][:, s4 * D + dvc * 512: s4 * D + (dvc + 1) * 512],
                                           start=(k == 0), stop=(k == nch - 1))
                    tok_vb_free[bk] = P.inc(ins, "pe")
                    if b == nblk - 1:
                        last_pv = tok_vb_free[bk]
                tok_ptb_free = last_pv
                P.w(v, "v", last_l)
                tk = P.inc(v.tensor_copy(lT[0:1, :], pL[0:1, 0:128]), "dv")
                P.w(t, "t", tk)
                ins = t.matmul(pL[:, 256:258], lT[0:32, :], P.ones128f[0:32, 0:2],
                               start=True, stop=True)
                tk_pe = P.inc(ins, "pe")
                tok_pL_free = tk_pe
                P.w(v, "v", tk_pe)
                tk = P.inc(v.tensor_copy(lcol[:], pL[:, 256:258]), "dv")
                P.w(v, "v", tk)
                tok_linv = P.inc(v.reciprocal(linv[:], lcol[:, 0:1]), "dv")
                P.w(a, "a", tok_linv)
                P.w(a, "a", last_pv)
                P.w(a, "a", tok_ob_free)
                tok_ob = P.inc(a.activation(ob[:], pO[:, 0:4 * 512], AF.Copy,
                                            bias=0.0, scale=linv[:, 0:1]), "ac")
                tok_pO_free = tok_ob
                tok_sp_ob = P.dma(sy, "s", "ld", P.oS[:, qi:qi + 1, :],
                                  ob[:].rearrange("p (o d) -> p o d", o=1),
                                  toks=(tok_ob,))
                tok_ob_free = tok_sp_ob
                tok_qt_free = P.tok("pe")
                tok_pS_free = [tok_sc_evac[nblk - 2], tok_sc_evac[nblk - 1]]
            self.tok_d_done = tok_sp_ob

    def emit_d2(self):
        P = self
        nc = self.nc
        g, v, a, t, sy = self.g, self.v, self.a, self.t, self.sy
        with ExitStack() as phase_ctx:
            wo_sb = phase_ctx.enter_context(nc.sbuf_tensor("ewo", [128, DC * D], BF16))
            ob = phase_ctx.enter_context(nc.sbuf_tensor("eob", [128, D], BF16))
            otb = phase_ctx.enter_context(nc.sbuf_tensor("eotb", [128, DC * 128], BF16))
            xnt = phase_ctx.enter_context(nc.sbuf_tensor("exn", [128, D], BF16))
            s_bf = phase_ctx.enter_context(nc.sbuf_tensor("esbf", [128, D], BF16))
            pS = phase_ctx.enter_context(nc.psum_tensor("epS2", [128, 2 * 512], F32))
            pTr = phase_ctx.enter_context(nc.psum_tensor("epT2", [128, 512], BF16))
            P.phase_start()
            tok_wo = P.dma(g, "g", "gd", wo_sb[:].rearrange("p (dc f) -> p dc f", dc=DC),
                           P.o_proj.rearrange("(dc p) f -> p dc f", p=128))
            tok_ob_free = None
            tok_xn_free = None
            tok_sbf_free = None
            tok_otb_free = None
            tok_pTr_free = None
            tok_sp_s = None
            tok_pS_free = [None, None]
            for qi in range(NT):
                tok_ob = P.dma(sy, "s", "ld", ob[:].rearrange("p (o d) -> p o d", o=1),
                               P.oS[:, qi:qi + 1, :], toks=(tok_ob_free, self.tok_d_done))
                tok_xn = P.dma(sy, "s", "ld", xnt[:].rearrange("p (o d) -> p o d", o=1),
                               P.xnS[:, qi:qi + 1, :], toks=(tok_xn_free,))
                tok_tr2 = tok_pTr_free
                P.w(t, "t", tok_otb_free)
                for g4 in range(4):
                    P.w(t, "t", tok_tr2)
                    P.w(t, "t", tok_ob)
                    for j in range(4):
                        dvc = g4 * 4 + j
                        ins = t.transpose(pTr[:, j * 128:(j + 1) * 128],
                                          ob[:, dvc * 128:(dvc + 1) * 128], P.ident[:])
                    tk_pe = P.inc(ins, "pe")
                    P.w(v, "v", tk_pe)
                    ins = v.tensor_copy(otb[:, g4 * 512:(g4 + 1) * 512], pTr[:, 0:512])
                    tok_tr2 = P.inc(ins, "dv")
                tok_pTr_free = tok_tr2
                P.w(t, "t", tok_wo)
                tok_s_parts = []
                for fc in range(4):
                    bank = fc % 2
                    P.w(t, "t", tok_s_parts[-2] if fc >= 2 else tok_pS_free[bank])
                    P.w(t, "t", tok_tr2)
                    for dvc in range(DC):
                        ins = t.matmul(pS[:, bank * 512:(bank + 1) * 512],
                                       otb[:, dvc * 128:(dvc + 1) * 128],
                                       wo_sb[:, dvc * D + fc * 512: dvc * D + (fc + 1) * 512],
                                       start=(dvc == 0), stop=(dvc == DC - 1))
                    tk_pe = P.inc(ins, "pe")
                    P.w(v, "v", tk_pe)
                    P.w(v, "v", tok_xn)
                    P.w(v, "v", tok_sbf_free)
                    ins = v.scalar_tensor_tensor(s_bf[:, fc * 512:(fc + 1) * 512],
                                                 pS[:, bank * 512:(bank + 1) * 512], 1.0,
                                                 xnt[:, fc * 512:(fc + 1) * 512],
                                                 ALU.mult, ALU.add)
                    tok_s_parts.append(P.inc(ins, "dv"))
                tok_pS_free = [tok_s_parts[2], tok_s_parts[3]]
                tok_sp_s = P.dma(sy, "s", "ld", P.sS[:, qi:qi + 1, :],
                                 s_bf[:].rearrange("p (o d) -> p o d", o=1),
                                 toks=(tok_s_parts[-1],))
                tok_sbf_free = tok_sp_s
                tok_ob_free = P.tok("pe")
                tok_xn_free = tok_s_parts[-1]
                tok_otb_free = P.tok("pe")
            self.tok_d2_done = tok_sp_s

    def emit_e_f(self):
        P = self
        nc = self.nc
        g, v, a, t, sy = self.g, self.v, self.a, self.t, self.sy
        with ExitStack() as phase_ctx:
            acc = phase_ctx.enter_context(nc.sbuf_tensor("eacc", [128, 8 * D], BF16))
            w1_sb = phase_ctx.enter_context(nc.sbuf_tensor("ew1", [128, DC * H], BF16))
            w2_sb = phase_ctx.enter_context(nc.sbuf_tensor("ew2", [128, HC * D], BF16))
            hT = phase_ctx.enter_context(nc.sbuf_tensor("ehT", [128, HC * 1024], BF16))
            xnTh = phase_ctx.enter_context(nc.sbuf_tensor("exnTh", [128, DC * 1024], BF16))
            b1e = phase_ctx.enter_context(nc.sbuf_tensor("eb1", [32, H], BF16))
            b2e = phase_ctx.enter_context(nc.sbuf_tensor("eb2", [32, D], BF16))
            fs = phase_ctx.enter_context(nc.sbuf_tensor("fs", [128, D], BF16))
            fo = phase_ctx.enter_context(nc.sbuf_tensor("fo", [128, D], F32))
            pH = phase_ctx.enter_context(nc.psum_tensor("epH", [128, 2 * 512], F32))
            pY = phase_ctx.enter_context(nc.psum_tensor("epY", [128, 4 * 512], F32))
            tok_w_free = None
            P.phase_start()
            tok_hT_free = None
            tok_xnT_free = None
            evacH = [None, None]
            tok_pY = {}
            g.memset(b1e[:], 0.0)
            P.inc(g.memset(b2e[:], 0.0), "gd")
            tok_fs_free = None
            tok_fo_free = None
            last_store = None
            tok_acc_free = None
            tok_xcopy = P.dma(sy, "s", "ld", P.out, P.xq)
            tok_w1 = tok_w2 = tok_b1 = tok_b2 = None
            tok_w1_free = tok_w2_free = None
            first = True
            for h in range(2):
                tok_acc = {}
                tok_xnT = P.dma(sy, "s", "ld",
                                xnTh[:].rearrange("p (dc t) -> p dc t", dc=DC),
                                P.xnTS[:, :, h * 1024:(h + 1) * 1024],
                                toks=(tok_xnT_free, self.tok_c1_done))
                for e in range(E):
                    if first:
                        tok_w1 = P.dma(g, "g", "gd", w1_sb[:].rearrange("p (dc h) -> p dc h", dc=DC),
                                       P.w1T[e * D:(e + 1) * D, :].rearrange("(dc p) h -> p dc h", p=128))
                        tok_b1 = P.dma(g, "g", "gd", b1e[0:1, :], P.b1[e:e + 1, :])
                        tok_w2 = P.dma(g, "g", "gd", w2_sb[:].rearrange("p (hc d) -> p hc d", hc=HC),
                                       P.w2T[e * H:(e + 1) * H, :].rearrange("(hc p) d -> p hc d", p=128))
                        tok_b2 = P.dma(g, "g", "gd", b2e[0:1, :], P.b2[e:e + 1, :])
                        first = False
                    for s in range(2):
                        for hc in range(HC):
                            bank = hc % 2
                            P.w(t, "t", tok_w1)
                            P.w(t, "t", tok_b1)
                            P.w(t, "t", tok_xnT)
                            P.w(t, "t", tok_hT_free)
                            P.w(t, "t", evacH[bank])
                            for dc in range(DC):
                                t.matmul(pH[:, bank * 512:(bank + 1) * 512],
                                         w1_sb[:, dc * H + hc * 128: dc * H + (hc + 1) * 128],
                                         xnTh[:, dc * 1024 + s * 512: dc * 1024 + (s + 1) * 512],
                                         start=(dc == 0), stop=False)
                            ins = t.matmul(pH[:, bank * 512:(bank + 1) * 512],
                                           b1e[0:32, hc * 128:(hc + 1) * 128],
                                           P.ones_row[0:32, 0:512], start=False, stop=True)
                            tk_pe = P.inc(ins, "pe")
                            P.w(a, "a", tk_pe)
                            ins = a.activation(hT[:, hc * 1024 + s * 512: hc * 1024 + (s + 1) * 512],
                                               pH[:, bank * 512:(bank + 1) * 512], AF.Silu,
                                               bias=0.0, scale=1.0)
                            evacH[bank] = P.inc(ins, "ac")
                    tok_hT_ready = P.tok("ac")
                    ne_h, ne_e = (h, e + 1) if e + 1 < E else (h + 1, 0)
                    if ne_h < 2:
                        tok_w1_free = P.tok("pe")
                        tok_w1 = P.dma(g, "g", "gd", w1_sb[:].rearrange("p (dc h) -> p dc h", dc=DC),
                                       P.w1T[ne_e * D:(ne_e + 1) * D, :].rearrange("(dc p) h -> p dc h", p=128),
                                       toks=(tok_w1_free,))
                        tok_b1 = P.dma(g, "g", "gd", b1e[0:1, :], P.b1[ne_e:ne_e + 1, :])
                    for tloc in range(8):
                        ti = h * 8 + tloc
                        for dc2 in range(4):
                            P.w(t, "t", tok_w2)
                            P.w(t, "t", tok_b2)
                            P.w(t, "t", tok_hT_ready)
                            P.w(t, "t", tok_pY.get(dc2))
                            for hc in range(HC):
                                t.matmul(pY[:, dc2 * 512:(dc2 + 1) * 512],
                                         hT[:, hc * 1024 + tloc * 128: hc * 1024 + (tloc + 1) * 128],
                                         w2_sb[:, hc * D + dc2 * 512: hc * D + (dc2 + 1) * 512],
                                         start=(hc == 0), stop=False)
                            ins = t.matmul(pY[:, dc2 * 512:(dc2 + 1) * 512],
                                           P.ones_row[0:32, 0:128],
                                           b2e[0:32, dc2 * 512:(dc2 + 1) * 512],
                                           start=False, stop=True)
                            tk_pe = P.inc(ins, "pe")
                            P.w(v, "v", tk_pe)
                            cslice = P.comb[:, ti * E + e: ti * E + e + 1]
                            dst = acc[:, tloc * D + dc2 * 512: tloc * D + (dc2 + 1) * 512]
                            if e == 0:
                                P.w(v, "v", tok_acc_free)
                                ins = v.tensor_scalar_mul(dst, pY[:, dc2 * 512:(dc2 + 1) * 512], cslice)
                            else:
                                P.w(v, "v", tok_acc.get((tloc, dc2)))
                                ins = v.scalar_tensor_tensor(dst, pY[:, dc2 * 512:(dc2 + 1) * 512],
                                                             cslice, dst, ALU.mult, ALU.add)
                            tok_acc[(tloc, dc2)] = P.inc(ins, "dv")
                            tok_pY[dc2] = tok_acc[(tloc, dc2)]
                    tok_hT_free = P.tok("pe")
                    if ne_h < 2:
                        tok_w2_free = P.tok("pe")
                        tok_w2 = P.dma(g, "g", "gd", w2_sb[:].rearrange("p (hc d) -> p hc d", hc=HC),
                                       P.w2T[ne_e * H:(ne_e + 1) * H, :].rearrange("(hc p) d -> p hc d", p=128),
                                       toks=(tok_w2_free,))
                        tok_b2 = P.dma(g, "g", "gd", b2e[0:1, :], P.b2[ne_e:ne_e + 1, :])
                tok_xnT_free = P.tok("pe")
                for tloc in range(NT // 2):
                    ti = h * 8 + tloc
                    tok_fs = P.dma(sy, "s", "ld", fs[:].rearrange("p (o d) -> p o d", o=1),
                                   P.sS[:, ti:ti + 1, :],
                                   toks=(tok_fs_free, self.tok_d2_done))
                    P.w(a, "a", tok_acc[(tloc, 3)])
                    asl = acc[:, tloc * D:(tloc + 1) * D]
                    tok_gs = P.inc(a.activation(asl, asl, AF.Silu, bias=0.0, scale=1.0), "ac")
                    P.w(v, "v", tok_gs)
                    P.w(v, "v", tok_fs)
                    P.w(v, "v", tok_fo_free)
                    tok_o = P.inc(v.tensor_tensor(fo[:], fs[:], asl, ALU.mult), "dv")
                    tok_fs_free = tok_o
                    P.w(g, "g", tok_o)
                    P.w(g, "g", tok_xcopy)
                    ins = g.dma_start(P.out.rearrange("(s p) d -> p s d", p=128)[:, ti:ti + 1, :],
                                      fo[:].rearrange("p (o d) -> p o d", o=1),
                                      accum_op=ALU.add)
                    last_store = P.inc(ins, "gd", 16)
                    tok_fo_free = last_store
                tok_acc_free = P.tok("dv")
            if P.debug:
                P.w(g, "g", P.tok("dv"))
                P.inc(g.dma_start(P.dbg_acc, acc[:]), "gd", 16)
            P.w(sy, "s", last_store)
            P.w(sy, "s", P.tok("ld"))


def build_program(debug=False):
    return Prog(debug=debug).build()


def kernel(L0_attn_x, L0_attn_qkv, L0_attn_o_proj, L0_attn_gamma, L0_attn_beta,
           L0_ffn_gate_w, L0_ffn_gate_b, L0_ffn_bias, L0_ffn_w1, L0_ffn_b1,
           L0_ffn_w2, L0_ffn_b2):
    x = np.ascontiguousarray(np.asarray(L0_attn_x, dtype=np.float32))
    qkv = np.ascontiguousarray(np.asarray(L0_attn_qkv, dtype=np.float32))
    o_proj = np.ascontiguousarray(np.asarray(L0_attn_o_proj, dtype=np.float32))
    gamma = np.asarray(L0_attn_gamma, dtype=np.float32)
    beta = np.asarray(L0_attn_beta, dtype=np.float32)
    gate_w = np.asarray(L0_ffn_gate_w, dtype=np.float32)
    gate_b = np.asarray(L0_ffn_gate_b, dtype=np.float32)
    gbias = np.asarray(L0_ffn_bias, dtype=np.float32)
    w1 = np.asarray(L0_ffn_w1, dtype=np.float32)
    b1 = np.ascontiguousarray(np.asarray(L0_ffn_b1, dtype=np.float32))
    w2 = np.asarray(L0_ffn_w2, dtype=np.float32)
    b2 = np.ascontiguousarray(np.asarray(L0_ffn_b2, dtype=np.float32))

    shared = dict(
        qkv=qkv,
        o_proj=o_proj,
        gammaT=np.ascontiguousarray(gamma.reshape(DC, 128).T),
        betaT=np.ascontiguousarray(beta.reshape(DC, 128).T),
        gate_wT=np.ascontiguousarray(gate_w.T),
        gate_b=np.ascontiguousarray(np.broadcast_to(gate_b.reshape(1, E), (128, E))),
        gate_bias=np.ascontiguousarray(np.broadcast_to(gbias.reshape(1, E), (128, E))),
        w1T=np.ascontiguousarray(w1.transpose(0, 2, 1).reshape(E * D, H)),
        w2T=np.ascontiguousarray(w2.transpose(0, 2, 1).reshape(E * H, D)),
        b1=b1, b2=b2,
    )
    in_maps = []
    metas = []
    for core in range(8):
        bi, c = core // 4, core % 4
        cA, cB = c, 7 - c
        xb = np.ascontiguousarray(x[bi])
        xq = np.ascontiguousarray(np.concatenate(
            [xb[cA * 1024:(cA + 1) * 1024], xb[cB * 1024:(cB + 1) * 1024]], axis=0))
        xqT = np.ascontiguousarray(xq.T)
        qpos = np.concatenate([np.arange(cA * 1024, (cA + 1) * 1024),
                               np.arange(cB * 1024, (cB + 1) * 1024)]).astype(np.float32)
        qpos_t = np.ascontiguousarray(qpos.reshape(NT, 128).T)
        m = dict(xb=xb, xq=xq, xqT=xqT, qpos=qpos_t)
        m.update(shared)
        in_maps.append(m)
        metas.append((bi, cA, cB))

    nc = build_program()
    res = run_bass_kernel_spmd(nc, in_maps, core_ids=list(range(8)))

    out = np.empty((2, S, D), dtype=np.float32)
    for core in range(8):
        bi, cA, cB = metas[core]
        oc = res.results[core]["out"]
        out[bi, cA * 1024:(cA + 1) * 1024] = oc[0:1024]
        out[bi, cB * 1024:(cB + 1) * 1024] = oc[1024:2048]
    return out


# version 04
# speedup vs baseline: 1.2992x; 1.2669x over previous
import numpy as np
from contextlib import ExitStack

import concourse.bass as bass
import concourse.mybir as mybir
from concourse.bass_utils import run_bass_kernel_spmd

F32 = mybir.dt.float32
BF16 = mybir.dt.bfloat16
I16 = mybir.dt.int16
AF = mybir.ActivationFunctionType
ALU = mybir.AluOpType
AX = mybir.AxisListType

D = 2048
S = 8192
TOWN = 2048
H = 1408
E = 8
EPS = 1e-5
SCALE = 1.0 / float(np.sqrt(2048.0))

DC = 16
HC = 11
NSB = 16
NT = 16


class Prog:
    def __init__(self, debug=False):
        self.debug = debug
        nc = bass.Bass("TRN2", target_bir_lowering=False)
        self.nc = nc
        self.ctx = ExitStack()
        dt = nc.dram_tensor
        self.xb = dt("xb", [S, D], F32, kind="ExternalInput").ap()
        self.xq = dt("xq", [TOWN, D], F32, kind="ExternalInput").ap()
        self.xqT = dt("xqT", [D, TOWN], F32, kind="ExternalInput").ap()
        self.qpos_i = dt("qpos", [128, NT], F32, kind="ExternalInput").ap()
        self.qkv = dt("qkv", [D, 3 * D], F32, kind="ExternalInput").ap()
        self.o_proj = dt("o_proj", [D, D], F32, kind="ExternalInput").ap()
        self.gammaT_i = dt("gammaT", [128, DC], F32, kind="ExternalInput").ap()
        self.betaT_i = dt("betaT", [128, DC], F32, kind="ExternalInput").ap()
        self.gate_wT = dt("gate_wT", [D, E], F32, kind="ExternalInput").ap()
        self.gate_b = dt("gate_b", [128, E], F32, kind="ExternalInput").ap()
        self.gate_bias = dt("gate_bias", [128, E], F32, kind="ExternalInput").ap()
        self.w1T = dt("w1T", [E * D, H], F32, kind="ExternalInput").ap()
        self.w2T = dt("w2T", [E * H, D], F32, kind="ExternalInput").ap()
        self.b1 = dt("b1", [E, H], F32, kind="ExternalInput").ap()
        self.b2 = dt("b2", [E, D], F32, kind="ExternalInput").ap()
        self.out = dt("out", [TOWN, D], F32, kind="ExternalOutput").ap()
        kind = "ExternalOutput" if debug else "Internal"
        self.xn2TS = dt("xn2TS", [128, DC, S], BF16, kind=kind).ap()
        self.xn2TownS = dt("xn2TownS", [128, DC, TOWN], BF16, kind=kind).ap()
        self.ktS = dt("ktS", [128, DC, S], BF16, kind=kind).ap()
        self.vS = dt("vS", [128, S // 128, D], BF16, kind=kind).ap()
        self.xnS = dt("xnS", [128, NT, D], BF16, kind=kind).ap()
        self.xnTS = dt("xnTS", [128, DC, TOWN], BF16, kind=kind).ap()
        self.qTS = dt("qTS", [128, DC, TOWN], BF16, kind=kind).ap()
        self.oS = dt("oS", [128, NT, D], BF16, kind=kind).ap()
        self.sS = dt("sS", [128, NT, D], BF16, kind=kind).ap()
        if debug:
            self.dbg_comb = dt("dbg_comb", [128, NT * E], F32, kind=kind).ap()
            self.dbg_acc = dt("dbg_acc", [128, 8 * D], F32, kind=kind).ap()
            self.dbg_raw = dt("dbg_raw", [128, NT * E], F32, kind=kind).ap()
            self.dbg_sc = dt("dbg_sc", [128, NT * E], F32, kind=kind).ap()
            self.dbg_gw = dt("dbg_gw", [128, DC * E], F32, kind=kind).ap()
            self.dbg_cons = dt("dbg_cons", [128, 2 * E], F32, kind=kind).ap()

        self.sems = {}
        self.ct = {}
        for name in ["ld", "gd", "pe", "dv", "ac"]:
            self.sems[name] = self.ctx.enter_context(nc.semaphore(name))
            self.ct[name] = 0
        self.lastw = {}

        sb = lambda n, shp, d=F32: self.ctx.enter_context(nc.sbuf_tensor(n, shp, d))
        self.ident = sb("ident", [128, 128], BF16)
        self.ones_row = sb("ones_row", [32, 512], BF16)
        self.ones128f = sb("ones128f", [128, 128], F32)
        self.ones_colb = sb("ones_colb", [128, 1], BF16)
        self.ones_colf = sb("ones_colf", [128, 1], F32)
        self.gammaT = sb("gammaT_s", [128, DC])
        self.betaT = sb("betaT_s", [128, DC])
        self.qpos = sb("qpos_s", [128, NT])
        self.gate_wT_sb = sb("gate_wT_sb", [128, DC * E])
        self.gb_bc = sb("gb_bc", [128, E])
        self.gbias_bc = sb("gbias_bc", [128, E])
        self.rowsum_bc = sb("rowsum_bc", [128, E])
        self.gbb_bc = sb("gbb_bc", [128, E])
        self.comb = sb("comb", [128, NT * E])
        for n in ["sx", "sxx", "mu", "m2", "musq", "var", "a1", "l1", "nh", "r1",
                  "ia1", "var2", "a2", "l2", "nh2", "r2", "rr", "mrr", "mr1n"]:
            setattr(self, n, sb(n, [128, 4]))
        for n in ["g_t1", "g_sc", "g_eq1", "g_msk", "g_eq2", "g_c1"]:
            setattr(self, n, sb(n, [128, E]))
        self.dbg_raw_sb = sb("dbg_raw_sb", [128, NT * E])
        self.dbg_sc_sb = sb("dbg_sc_sb", [128, NT * E])
        for n in ["g_m1", "g_m2", "g_d12", "g_e", "g_den", "g_g1", "g_g2"]:
            setattr(self, n, sb(n, [128, 1]))

        self.g = nc.gpsimd
        self.v = nc.vector
        self.a = nc.scalar
        self.t = nc.tensor
        self.sy = nc.sync

    def inc(self, ins, sem, n=1):
        ins.then_inc(self.sems[sem], n)
        self.ct[sem] += n
        return (sem, self.ct[sem])

    def tok(self, sem):
        return (sem, self.ct[sem])

    def w(self, eng, engname, tok):
        if tok is None:
            return
        sem, val = tok
        if val <= 0:
            return
        key = (engname, sem)
        if self.lastw.get(key, 0) >= val:
            return
        eng.wait_ge(self.sems[sem], val)
        self.lastw[key] = val

    def dma(self, qeng, qname, sem, dst, src, toks=()):
        for t in toks:
            self.w(qeng, qname, t)
        ins = qeng.dma_start(dst, src)
        return self.inc(ins, sem, 16)

    def phase_start(self):
        toks = [self.tok(s) for s in ("pe", "dv", "ac", "ld", "gd")]
        for eng, name in ((self.t, "t"), (self.g, "g"), (self.sy, "s")):
            for tk in toks:
                if (name, tk[0]) not in (("t", "pe"), ("g", "gd"), ("s", "ld")):
                    self.w(eng, name, tk)

    def emit_stats(self, x4, x4_tok, want_mr1n):
        P, v, a = self, self.v, self.a
        t_sx = t_sxx = None
        for j in range(4):
            P.w(v, "v", x4_tok)
            t_sx = P.inc(v.reduce_sum(P.sx[:, j:j + 1], x4[:, j * D:(j + 1) * D], AX.X), "dv")
        for j in range(4):
            P.w(a, "a", x4_tok)
            t_sxx = P.inc(a.activation(P.sqdump[:], x4[:, j * D:(j + 1) * D], AF.Square,
                                       bias=0.0, scale=1.0, accum_out=P.sxx[:, j:j + 1]), "ac")
        P.w(v, "v", t_sx)
        tk = P.inc(v.tensor_scalar_mul(P.mu[:], P.sx[:], 1.0 / D), "dv")
        P.w(v, "v", t_sxx)
        P.w(v, "v", tk)
        tk = P.inc(v.tensor_scalar_mul(P.m2[:], P.sxx[:], 1.0 / D), "dv")
        P.w(v, "v", tk)
        tk = P.inc(v.tensor_tensor(P.musq[:], P.mu[:], P.mu[:], ALU.mult), "dv")
        P.w(v, "v", tk)
        tk = P.inc(v.tensor_tensor(P.var[:], P.m2[:], P.musq[:], ALU.subtract), "dv")
        P.w(v, "v", tk)
        t_a1 = P.inc(v.tensor_scalar_add(P.a1[:], P.var[:], EPS), "dv")
        P.w(a, "a", t_a1)
        tk = P.inc(a.activation(P.l1[:], P.a1[:], AF.Ln), "ac")
        P.w(v, "v", tk)
        tk = P.inc(v.tensor_scalar_mul(P.nh[:], P.l1[:], -0.5), "dv")
        P.w(a, "a", tk)
        t_r1 = P.inc(a.activation(P.r1[:], P.nh[:], AF.Exp), "ac")
        P.w(v, "v", t_a1)
        tk = P.inc(v.reciprocal(P.ia1[:], P.a1[:]), "dv")
        P.w(v, "v", tk)
        tk = P.inc(v.tensor_tensor(P.var2[:], P.var[:], P.ia1[:], ALU.mult), "dv")
        P.w(v, "v", tk)
        tk = P.inc(v.tensor_scalar_add(P.a2[:], P.var2[:], EPS), "dv")
        P.w(a, "a", tk)
        tk = P.inc(a.activation(P.l2[:], P.a2[:], AF.Ln), "ac")
        P.w(v, "v", tk)
        tk = P.inc(v.tensor_scalar_mul(P.nh2[:], P.l2[:], -0.5), "dv")
        P.w(a, "a", tk)
        tk = P.inc(a.activation(P.r2[:], P.nh2[:], AF.Exp), "ac")
        P.w(v, "v", tk)
        P.w(v, "v", t_r1)
        tk = P.inc(v.tensor_tensor(P.rr[:], P.r1[:], P.r2[:], ALU.mult), "dv")
        P.w(v, "v", tk)
        t_mrr = P.inc(v.scalar_tensor_tensor(P.mrr[:], P.mu[:], -1.0, P.rr[:], ALU.mult, ALU.mult), "dv")
        t_r1mu = None
        if want_mr1n:
            P.w(v, "v", t_mrr)
            t_r1mu = P.inc(v.scalar_tensor_tensor(P.mr1n[:], P.mu[:], -1.0, P.r1[:], ALU.mult, ALU.mult), "dv")
        return t_mrr, t_r1mu

    def emit_transposes(self, src_sb, src_tok, pT, evac_tok, dst, dst_dc_stride,
                        gamma, beta, n_tiles=4, free_tok=None):
        P, t, v = self, self.t, self.v
        last = None
        for dc in range(DC):
            bank = dc % 2
            P.w(t, "t", evac_tok[bank])
            P.w(t, "t", src_tok)
            if free_tok is not None:
                P.w(t, "t", free_tok)
            for j in range(n_tiles):
                ins = t.transpose(pT[bank][:, j * 128:(j + 1) * 128],
                                  src_sb[:, j * D + dc * 128: j * D + (dc + 1) * 128],
                                  P.ident[:])
            tk_pe = P.inc(ins, "pe")
            P.w(v, "v", tk_pe)
            dsl = dst[:, dc * dst_dc_stride: dc * dst_dc_stride + n_tiles * 128]
            if gamma is not None:
                ins = v.tensor_scalar(dsl, pT[bank][:, 0:n_tiles * 128],
                                      gamma[:, dc:dc + 1], beta[:, dc:dc + 1],
                                      ALU.mult, ALU.add)
            else:
                ins = v.tensor_copy(dsl, pT[bank][:, 0:n_tiles * 128])
            last = P.inc(ins, "dv")
            evac_tok[bank] = last
        return last

    def emit_proj_kc(self, w_sb, src, pX, evac_toks, dstbuf, dst_free_tok, ready_tok):
        P, t, a = self, self.t, self.a
        evac = evac_toks
        for kc in range(DC):
            bank = kc % 2
            P.w(t, "t", evac[bank])
            P.w(t, "t", ready_tok)
            for dc in range(DC):
                ins = t.matmul(pX[:, bank * 512:(bank + 1) * 512],
                               w_sb[:, dc * D + kc * 128: dc * D + (kc + 1) * 128],
                               src[:, dc * 512:(dc + 1) * 512],
                               start=(dc == 0), stop=(dc == DC - 1))
            tk_pe = P.inc(ins, "pe")
            P.w(a, "a", tk_pe)
            P.w(a, "a", dst_free_tok)
            ins = a.activation(dstbuf[:, kc * 512:(kc + 1) * 512],
                               pX[:, bank * 512:(bank + 1) * 512], AF.Copy,
                               bias=0.0, scale=1.0)
            evac[bank] = P.inc(ins, "ac")
        return evac[(DC - 1) % 2]

    def build(self):
        self.emit_prep()
        self.emit_a()
        self.emit_b()
        self.emit_c1()
        self.emit_c2()
        self.emit_d()
        self.emit_d2()
        self.emit_e_f()
        self.ctx.close()
        return self.nc

    def emit_prep(self):
        P = self
        g, v, a, t, sy = self.g, self.v, self.a, self.t, self.sy
        g.memset(P.ident[:], 0.0)
        g.affine_select(P.ident[:], P.ident[:], [[1, 128]], ALU.not_equal, 1.0,
                        base=0, channel_multiplier=-1)
        g.memset(P.ones_row[:], 0.0)
        g.memset(P.ones_row[0:1, :], 1.0)
        g.memset(P.ones128f[:], 1.0)
        g.memset(P.ones_colb[:], 1.0)
        tok_consts = P.inc(g.memset(P.ones_colf[:], 1.0), "gd")

        P.dma(sy, "s", "ld", P.gammaT[:], P.gammaT_i)
        P.dma(sy, "s", "ld", P.betaT[:], P.betaT_i)
        P.tok_qpos = P.dma(sy, "s", "ld", P.qpos[:], P.qpos_i)
        tok_gw = P.dma(sy, "s", "ld",
                       P.gate_wT_sb[:].rearrange("p (dc e) -> p dc e", dc=DC),
                       P.gate_wT.rearrange("(dc p) e -> p dc e", p=128))
        tok_gb = P.dma(sy, "s", "ld", P.gb_bc[:], P.gate_b)
        tok_gbias = P.dma(sy, "s", "ld", P.gbias_bc[:], P.gate_bias)
        P.tok_gamma_ready = P.tok_qpos

        with self.nc.psum_tensor("pPrep", [128, 512], F32) as pP:
            P.w(t, "t", tok_gw)
            P.w(t, "t", tok_consts)
            for dc in range(DC):
                ins = t.matmul(pP[:, 0:E], P.ones128f[:], P.gate_wT_sb[:, dc * E:(dc + 1) * E],
                               start=(dc == 0), stop=(dc == DC - 1))
            tk_pe = P.inc(ins, "pe")
            P.w(v, "v", tk_pe)
            P.inc(v.tensor_copy(P.rowsum_bc[:], pP[:, 0:E]), "dv")
            P.w(v, "v", tok_gb)
            P.w(v, "v", tok_gbias)
            P.w(v, "v", P.tok("dv"))
            P.tok_gate_ready = P.inc(v.tensor_tensor(P.gbb_bc[:], P.gb_bc[:], P.gbias_bc[:], ALU.add), "dv")

    def emit_a(self):
        P = self
        nc = self.nc
        g, v, a, t, sy = self.g, self.v, self.a, self.t, self.sy
        with ExitStack() as phase_ctx:
            x4 = phase_ctx.enter_context(nc.sbuf_tensor("ax4", [128, 4 * D], F32))
            xnl2 = phase_ctx.enter_context(nc.sbuf_tensor("axnl2", [128, 4 * D], BF16))
            xn2T = phase_ctx.enter_context(nc.sbuf_tensor("axn2T", [128, DC * 512], BF16))
            ktbuf = phase_ctx.enter_context(nc.sbuf_tensor("aktbuf", [128, DC * 512], BF16))
            wk_sb = phase_ctx.enter_context(nc.sbuf_tensor("awk", [128, DC * D], BF16))
            self.sqdump = phase_ctx.enter_context(nc.sbuf_tensor("asq", [128, D], BF16))
            pK = phase_ctx.enter_context(nc.psum_tensor("apK", [128, 2 * 512], F32))
            pT0 = phase_ctx.enter_context(nc.psum_tensor("apT0", [128, 512], BF16))
            pT1 = phase_ctx.enter_context(nc.psum_tensor("apT1", [128, 512], BF16))
            pT = [pT0, pT1]
            P.phase_start()
            tok_wk = P.dma(g, "g", "gd", wk_sb[:].rearrange("p (dc f) -> p dc f", dc=DC),
                           P.qkv[:, D:2 * D].rearrange("(dc p) f -> p dc f", p=128))
            tok_x4_free = None
            tok_sp_xn2T = tok_sp_kt = None
            evac_tok = [None, None]
            evacK = [None, None]
            for sb_i in range(NSB):
                t0 = sb_i * 512
                tok_x4 = P.dma(sy, "s", "ld",
                               x4[:].rearrange("p (s d) -> p s d", s=4),
                               P.xb.rearrange("(s p) d -> p s d", p=128)[:, sb_i * 4:(sb_i + 1) * 4, :],
                               toks=(tok_x4_free, P.tok_gamma_ready))
                t_mrr, _ = P.emit_stats(x4, tok_x4, want_mr1n=False)
                last_xnl2 = None
                for j in range(4):
                    P.w(v, "v", t_mrr)
                    ins = v.tensor_scalar(xnl2[:, j * D:(j + 1) * D], x4[:, j * D:(j + 1) * D],
                                          P.rr[:, j:j + 1], P.mrr[:, j:j + 1], ALU.mult, ALU.add)
                    last_xnl2 = P.inc(ins, "dv")
                tok_x4_free = last_xnl2
                tok_xn2T_ready = P.emit_transposes(xnl2, last_xnl2, pT, evac_tok, xn2T, 512,
                                                  P.gammaT, P.betaT, free_tok=tok_sp_xn2T)
                P.w(t, "t", tok_wk)
                tok_kt_ready = P.emit_proj_kc(wk_sb, xn2T, pK, evacK, ktbuf,
                                              tok_sp_kt, tok_xn2T_ready)
                tok_sp_xn2T = P.dma(sy, "s", "ld", P.xn2TS[:, :, t0:t0 + 512],
                                    xn2T[:].rearrange("p (dc t) -> p dc t", dc=DC),
                                    toks=(tok_xn2T_ready,))
                tok_sp_kt = P.dma(sy, "s", "ld", P.ktS[:, :, t0:t0 + 512],
                                  ktbuf[:].rearrange("p (kc t) -> p kc t", kc=DC),
                                  toks=(tok_kt_ready,))
            self.tok_a_done = tok_sp_kt

    def emit_b(self):
        P = self
        nc = self.nc
        g, v, a, t, sy = self.g, self.v, self.a, self.t, self.sy
        with ExitStack() as phase_ctx:
            xt0 = phase_ctx.enter_context(nc.sbuf_tensor("bxt0", [128, DC * 512], BF16))
            xt1 = phase_ctx.enter_context(nc.sbuf_tensor("bxt1", [128, DC * 512], BF16))
            vbuf = phase_ctx.enter_context(nc.sbuf_tensor("bvbuf", [128, 4 * D], BF16))
            wv_sb = phase_ctx.enter_context(nc.sbuf_tensor("bwv", [128, DC * D], BF16))
            pV = phase_ctx.enter_context(nc.psum_tensor("bpV", [128, 2 * 512], F32))
            xts = [xt0, xt1]
            P.phase_start()
            tok_wv = P.dma(g, "g", "gd", wv_sb[:].rearrange("p (dc f) -> p dc f", dc=DC),
                           P.qkv[:, 2 * D:3 * D].rearrange("(dc p) f -> p dc f", p=128))
            tok_xt_free = [None, None]
            tok_sp_v = None
            evacV = [None, None]
            for sb_i in range(NSB):
                t0 = sb_i * 512
                bk = sb_i % 2
                tok_xt = P.dma(sy, "s", "ld", xts[bk][:].rearrange("p (dc t) -> p dc t", dc=DC),
                               P.xn2TS[:, :, t0:t0 + 512],
                               toks=(tok_xt_free[bk], self.tok_a_done))
                P.w(t, "t", tok_wv)
                nvb = 0
                tok_v_ready = None
                for j in range(4):
                    for dvc in range(4):
                        bank = nvb % 2
                        nvb += 1
                        P.w(t, "t", evacV[bank])
                        P.w(t, "t", tok_xt)
                        for dc in range(DC):
                            ins = t.matmul(pV[:, bank * 512:(bank + 1) * 512],
                                           xts[bk][:, dc * 512 + j * 128: dc * 512 + (j + 1) * 128],
                                           wv_sb[:, dc * D + dvc * 512: dc * D + (dvc + 1) * 512],
                                           start=(dc == 0), stop=(dc == DC - 1))
                        tk_pe = P.inc(ins, "pe")
                        P.w(a, "a", tk_pe)
                        P.w(a, "a", tok_sp_v)
                        ins = a.activation(vbuf[:, j * D + dvc * 512: j * D + (dvc + 1) * 512],
                                           pV[:, bank * 512:(bank + 1) * 512], AF.Copy,
                                           bias=0.0, scale=1.0)
                        evacV[bank] = P.inc(ins, "ac")
                        tok_v_ready = evacV[bank]
                tok_xt_free[bk] = P.tok("pe")
                tok_sp_v = P.dma(sy, "s", "ld", P.vS[:, sb_i * 4:(sb_i + 1) * 4, :],
                                 vbuf[:].rearrange("p (s d) -> p s d", s=4),
                                 toks=(tok_v_ready,))
            self.tok_b_done = tok_sp_v

    def emit_c1(self):
        P = self
        nc = self.nc
        g, v, a, t, sy = self.g, self.v, self.a, self.t, self.sy
        with ExitStack() as phase_ctx:
            x4 = phase_ctx.enter_context(nc.sbuf_tensor("cx4", [128, 4 * D], F32))
            xqT4 = phase_ctx.enter_context(nc.sbuf_tensor("cxqT4", [128, DC * 512], F32))
            xnl2 = phase_ctx.enter_context(nc.sbuf_tensor("cxnl2", [128, 4 * D], BF16))
            xn2T = phase_ctx.enter_context(nc.sbuf_tensor("cxn2T", [128, DC * 512], BF16))
            xn_b = phase_ctx.enter_context(nc.sbuf_tensor("cxn", [128, 4 * D], BF16))
            xnT = phase_ctx.enter_context(nc.sbuf_tensor("cxnT", [128, DC * 512], BF16))
            self.sqdump = phase_ctx.enter_context(nc.sbuf_tensor("csq", [128, D], BF16))
            pRaw = phase_ctx.enter_context(nc.psum_tensor("cpRaw", [128, 512], F32))
            pT0 = phase_ctx.enter_context(nc.psum_tensor("cpT0", [128, 512], BF16))
            pT1 = phase_ctx.enter_context(nc.psum_tensor("cpT1", [128, 512], BF16))
            pT = [pT0, pT1]
            P.phase_start()
            tok_x4_free = None
            tok_sp_xn = tok_sp_xnT = tok_sp_x2 = None
            tok_raw_read = None
            evac_tok = [None, None]
            for c in range(4):
                t0 = c * 512
                tok_x4 = P.dma(sy, "s", "ld",
                               x4[:].rearrange("p (s d) -> p s d", s=4),
                               P.xq.rearrange("(s p) d -> p s d", p=128)[:, c * 4:(c + 1) * 4, :],
                               toks=(tok_x4_free,))
                tok_xqT = P.dma(sy, "s", "ld",
                                xqT4[:].rearrange("p (dc t) -> p dc t", dc=DC),
                                P.xqT.rearrange("(dc p) t -> p dc t", p=128)[:, :, c * 512:(c + 1) * 512],
                                toks=(tok_raw_read,))
                t_mrr, t_r1mu = P.emit_stats(x4, tok_x4, want_mr1n=True)
                for j in range(4):
                    ti = c * 4 + j
                    P.w(t, "t", tok_xqT)
                    P.w(t, "t", P.tok_gate_ready)
                    P.w(t, "t", tok_raw_read)
                    for dc in range(DC):
                        ins = t.matmul(pRaw[:, 0:E],
                                       xqT4[:, dc * 512 + j * 128: dc * 512 + (j + 1) * 128],
                                       P.gate_wT_sb[:, dc * E:(dc + 1) * E],
                                       start=(dc == 0), stop=(dc == DC - 1))
                    tk_raw = P.inc(ins, "pe")
                    P.w(v, "v", tk_raw)
                    P.w(v, "v", t_r1mu)
                    if P.debug:
                        P.inc(v.tensor_copy(P.dbg_raw_sb[:, ti * E:(ti + 1) * E], pRaw[:, 0:E]), "dv")
                    tk = P.inc(v.scalar_tensor_tensor(P.g_t1[:], pRaw[:, 0:E], P.r1[:, j:j + 1],
                                                      P.gbb_bc[:], ALU.mult, ALU.add), "dv")
                    tok_raw_read = tk
                    P.w(v, "v", tk)
                    tk = P.inc(v.scalar_tensor_tensor(P.g_sc[:], P.rowsum_bc[:], P.mr1n[:, j:j + 1],
                                                      P.g_t1[:], ALU.mult, ALU.add), "dv")
                    P.w(v, "v", tk)
                    if P.debug:
                        P.inc(v.tensor_copy(P.dbg_sc_sb[:, ti * E:(ti + 1) * E], P.g_sc[:]), "dv")
                        P.w(v, "v", P.tok("dv"))
                    tk = P.inc(v.reduce_max(P.g_m1[:], P.g_sc[:], AX.X), "dv")
                    P.w(v, "v", tk)
                    tk = P.inc(v.tensor_scalar(P.g_eq1[:], P.g_sc[:], P.g_m1[:, 0:1], None, ALU.is_ge), "dv")
                    P.w(v, "v", tk)
                    tk = P.inc(v.scalar_tensor_tensor(P.g_msk[:], P.g_eq1[:], -1e30, P.g_sc[:],
                                                      ALU.mult, ALU.add), "dv")
                    P.w(v, "v", tk)
                    tk = P.inc(v.reduce_max(P.g_m2[:], P.g_msk[:], AX.X), "dv")
                    P.w(v, "v", tk)
                    tk = P.inc(v.tensor_scalar(P.g_eq2[:], P.g_msk[:], P.g_m2[:, 0:1], None, ALU.is_ge), "dv")
                    P.w(v, "v", tk)
                    tk = P.inc(v.tensor_tensor(P.g_d12[:], P.g_m1[:], P.g_m2[:], ALU.subtract), "dv")
                    P.w(a, "a", tk)
                    tk = P.inc(a.activation(P.g_e[:], P.g_d12[:], AF.Exp, bias=0.0, scale=-1.0), "ac")
                    P.w(v, "v", tk)
                    tk = P.inc(v.tensor_scalar_add(P.g_den[:], P.g_e[:], 1.0), "dv")
                    P.w(v, "v", tk)
                    tk = P.inc(v.reciprocal(P.g_g1[:], P.g_den[:]), "dv")
                    P.w(v, "v", tk)
                    tk = P.inc(v.tensor_scalar(P.g_g2[:], P.g_g1[:], -1.0, 1.0, ALU.mult, ALU.add), "dv")
                    P.w(v, "v", tk)
                    tk = P.inc(v.tensor_scalar_mul(P.g_c1[:], P.g_eq1[:], P.g_g1[:, 0:1]), "dv")
                    P.w(v, "v", tk)
                    P.inc(v.scalar_tensor_tensor(P.comb[:, ti * E:(ti + 1) * E], P.g_eq2[:],
                                                 P.g_g2[:, 0:1], P.g_c1[:], ALU.mult, ALU.add), "dv")
                last_xn = last_xnl2 = None
                for j in range(4):
                    P.w(v, "v", t_mrr)
                    P.w(v, "v", t_r1mu)
                    ins = v.tensor_scalar(xn_b[:, j * D:(j + 1) * D], x4[:, j * D:(j + 1) * D],
                                          P.r1[:, j:j + 1], P.mr1n[:, j:j + 1], ALU.mult, ALU.add)
                    last_xn = P.inc(ins, "dv")
                    ins = v.tensor_scalar(xnl2[:, j * D:(j + 1) * D], x4[:, j * D:(j + 1) * D],
                                          P.rr[:, j:j + 1], P.mrr[:, j:j + 1], ALU.mult, ALU.add)
                    last_xnl2 = P.inc(ins, "dv")
                tok_x4_free = last_xnl2
                tok_xnT_ready = P.emit_transposes(xn_b, last_xn, pT, evac_tok, xnT, 512,
                                                 None, None, free_tok=tok_sp_xnT)
                tok_x2_ready = P.emit_transposes(xnl2, last_xnl2, pT, evac_tok, xn2T, 512,
                                                 P.gammaT, P.betaT, free_tok=tok_sp_x2)
                tok_sp_xn = P.dma(sy, "s", "ld", P.xnS[:, c * 4:(c + 1) * 4, :],
                                  xn_b[:].rearrange("p (s d) -> p s d", s=4),
                                  toks=(last_xn,))
                tok_sp_xnT = P.dma(sy, "s", "ld", P.xnTS[:, :, t0:t0 + 512],
                                   xnT[:].rearrange("p (dc t) -> p dc t", dc=DC),
                                   toks=(tok_xnT_ready,))
                tok_sp_x2 = P.dma(sy, "s", "ld", P.xn2TownS[:, :, t0:t0 + 512],
                                  xn2T[:].rearrange("p (dc t) -> p dc t", dc=DC),
                                  toks=(tok_x2_ready,))
            self.tok_c1_done = tok_sp_x2
            if P.debug:
                P.w(g, "g", P.tok("dv"))
                P.inc(g.dma_start(P.dbg_comb, P.comb[:]), "gd", 16)
                P.inc(g.dma_start(P.dbg_raw, P.dbg_raw_sb[:]), "gd", 16)
                P.inc(g.dma_start(P.dbg_sc, P.dbg_sc_sb[:]), "gd", 16)
                P.inc(g.dma_start(P.dbg_gw, P.gate_wT_sb[:]), "gd", 16)
                P.inc(g.dma_start(P.dbg_cons[:, 0:E], P.rowsum_bc[:]), "gd", 16)
                P.inc(g.dma_start(P.dbg_cons[:, E:2 * E], P.gbb_bc[:]), "gd", 16)

    def emit_c2(self):
        P = self
        nc = self.nc
        g, v, a, t, sy = self.g, self.v, self.a, self.t, self.sy
        with ExitStack() as phase_ctx:
            xt0 = phase_ctx.enter_context(nc.sbuf_tensor("qxt0", [128, DC * 512], BF16))
            xt1 = phase_ctx.enter_context(nc.sbuf_tensor("qxt1", [128, DC * 512], BF16))
            qtbuf = phase_ctx.enter_context(nc.sbuf_tensor("qqt", [128, DC * 512], BF16))
            wq_sb = phase_ctx.enter_context(nc.sbuf_tensor("qwq", [128, DC * D], BF16))
            pQ = phase_ctx.enter_context(nc.psum_tensor("qpQ", [128, 2 * 512], F32))
            xts = [xt0, xt1]
            P.phase_start()
            tok_wq = P.dma(g, "g", "gd", wq_sb[:].rearrange("p (dc f) -> p dc f", dc=DC),
                           P.qkv[:, 0:D].rearrange("(dc p) f -> p dc f", p=128))
            tok_xt_free = [None, None]
            tok_sp_qt = None
            evacQ = [None, None]
            for c in range(4):
                t0 = c * 512
                bk = c % 2
                tok_xt = P.dma(sy, "s", "ld", xts[bk][:].rearrange("p (dc t) -> p dc t", dc=DC),
                               P.xn2TownS[:, :, t0:t0 + 512],
                               toks=(tok_xt_free[bk], self.tok_c1_done))
                P.w(t, "t", tok_wq)
                tok_qt_ready = P.emit_proj_kc(wq_sb, xts[bk], pQ, evacQ, qtbuf,
                                              tok_sp_qt, tok_xt)
                tok_xt_free[bk] = P.tok("pe")
                tok_sp_qt = P.dma(sy, "s", "ld", P.qTS[:, :, t0:t0 + 512],
                                  qtbuf[:].rearrange("p (kc t) -> p kc t", kc=DC),
                                  toks=(tok_qt_ready,))
            self.tok_c2_done = tok_sp_qt

    def emit_d(self):
        P = self
        nc = self.nc
        g, v, a, t, sy = self.g, self.v, self.a, self.t, self.sy
        with ExitStack() as phase_ctx:
            arena = phase_ctx.enter_context(nc.sbuf_tensor("dar", [128, 4 * S], BF16))
            Pb = phase_ctx.enter_context(nc.sbuf_tensor("dP", [128, S], BF16))
            mask = phase_ctx.enter_context(nc.sbuf_tensor("dmask", [128, S], BF16))
            kvidx = phase_ctx.enter_context(nc.sbuf_tensor("dkvidx", [128, S], I16))
            kt0 = phase_ctx.enter_context(nc.sbuf_tensor("dkt0", [128, DC * 512], BF16))
            kt1 = phase_ctx.enter_context(nc.sbuf_tensor("dkt1", [128, DC * 512], BF16))
            vb0 = phase_ctx.enter_context(nc.sbuf_tensor("dvb0", [128, 4 * 1024], BF16))
            vb1 = phase_ctx.enter_context(nc.sbuf_tensor("dvb1", [128, 4 * 1024], BF16))
            qt4 = phase_ctx.enter_context(nc.sbuf_tensor("dqt4", [128, DC * 512], BF16))
            ob4 = phase_ctx.enter_context(nc.sbuf_tensor("dob4", [128, 4 * D], BF16))
            bm4 = phase_ctx.enter_context(nc.sbuf_tensor("dbm4", [128, 64], F32))
            negm4 = phase_ctx.enter_context(nc.sbuf_tensor("dnegm4", [128, 4], F32))
            negms4 = phase_ctx.enter_context(nc.sbuf_tensor("dnegms4", [128, 4], F32))
            l4 = phase_ctx.enter_context(nc.sbuf_tensor("dl4", [32, 512], F32))
            lcol4 = phase_ctx.enter_context(nc.sbuf_tensor("dlcol4", [128, 4], F32))
            linv4 = phase_ctx.enter_context(nc.sbuf_tensor("dlinv4", [128, 4], F32))
            kts = [kt0, kt1]
            vbs = [vb0, vb1]
            P.phase_start()
            g.memset(l4[:], 0.0)
            tok_kvidx = P.inc(g.iota(kvidx[:], [[1, S]], base=0, channel_multiplier=0,
                                     allow_small_or_imprecise_dtypes=True), "gd")
            tok_kt_free = [None, None]
            tok_vb_free = [None, None]
            tok_qt_free = None
            tok_arena_free = None
            tok_psum_free = None
            tok_mask_free = None
            tok_P_free = None
            tok_ob_spill = None
            tok_sp_ob = None
            for gi in range(4):
                L = 4096 if gi < 2 else S
                nblk = L // 512
                nch = L // 128
                tok_qt = P.dma(sy, "s", "ld", qt4[:].rearrange("p (kc t) -> p kc t", kc=DC),
                               P.qTS[:, :, gi * 512:(gi + 1) * 512],
                               toks=(tok_qt_free, self.tok_c2_done))
                with nc.psum_tensor(f"dpS{gi}", [128, 2 * 512], F32) as pS:
                    tok_sc_evac = {}
                    nb_mm = 0
                    for b in range(nblk):
                        bk = b % 2
                        tok_kt = P.dma(sy, "s", "ld",
                                       kts[bk][:].rearrange("p (kc t) -> p kc t", kc=DC),
                                       P.ktS[:, :, b * 512:(b + 1) * 512],
                                       toks=(tok_kt_free[bk], self.tok_b_done))
                        for tl in range(4):
                            bank = nb_mm % 2
                            nb_mm += 1
                            P.w(t, "t", tok_kt)
                            P.w(t, "t", tok_qt)
                            P.w(t, "t", tok_sc_evac.get(nb_mm - 3))
                            P.w(t, "t", tok_psum_free)
                            P.w(t, "t", tok_arena_free)
                            for kc in range(DC):
                                ins = t.matmul(pS[:, bank * 512:(bank + 1) * 512],
                                               qt4[:, kc * 512 + tl * 128: kc * 512 + (tl + 1) * 128],
                                               kts[bk][:, kc * 512:(kc + 1) * 512],
                                               start=(kc == 0), stop=(kc == DC - 1))
                            tk_pe = P.inc(ins, "pe")
                            if tl == 3:
                                tok_kt_free[bk] = tk_pe
                            P.w(a, "a", tk_pe)
                            P.w(a, "a", tok_arena_free)
                            P.w(a, "a", tok_P_free)
                            ins = a.activation(arena[:, tl * S + b * 512: tl * S + (b + 1) * 512],
                                               pS[:, bank * 512:(bank + 1) * 512], AF.Copy,
                                               bias=0.0, scale=1.0)
                            tok_sc_evac[nb_mm - 1] = P.inc(ins, "ac")
                            P.w(v, "v", tok_sc_evac[nb_mm - 1])
                            P.inc(v.reduce_max(bm4[:, tl * 16 + b: tl * 16 + b + 1],
                                               arena[:, tl * S + b * 512: tl * S + (b + 1) * 512],
                                               AX.X), "dv")
                    P.w(v, "v", P.tok("dv"))
                    for tl in range(4):
                        P.inc(v.reduce_max(negm4[:, tl:tl + 1], bm4[:, tl * 16: tl * 16 + nblk],
                                           AX.X, negate=True), "dv")
                    P.w(v, "v", P.tok("dv"))
                    tok_negms = P.inc(v.tensor_scalar_mul(negms4[:], negm4[:], SCALE), "dv")
                    tok_pass1_sc = P.tok("ac")
                with nc.psum_tensor(f"dpT{gi}", [128, 512], BF16) as pTr, \
                     nc.psum_tensor(f"dpL{gi}", [128, 512], F32) as pL:
                    tok_tr_evac = None
                    tok_l_evac = None
                    tok_ptb_chunk = {}
                    for tl in range(4):
                        ti = gi * 4 + tl
                        P.w(v, "v", tok_kvidx)
                        P.w(v, "v", tok_mask_free)
                        P.w(v, "v", P.tok_qpos)
                        tok_mask = P.inc(v.tensor_scalar(mask[:, 0:L], kvidx[:, 0:L],
                                                         P.qpos[:, ti:ti + 1], None, ALU.is_le), "dv")
                        P.w(a, "a", tok_negms)
                        P.w(a, "a", tok_P_free)
                        tok_exp = P.inc(a.activation(Pb[:, 0:L], arena[:, tl * S: tl * S + L], AF.Exp,
                                                     bias=negms4[:, tl:tl + 1], scale=SCALE), "ac")
                        P.w(v, "v", tok_exp)
                        P.w(v, "v", tok_mask)
                        tok_Pm = P.inc(v.tensor_tensor(Pb[:, 0:L], Pb[:, 0:L], mask[:, 0:L], ALU.mult), "dv")
                        tok_mask_free = tok_Pm
                        for g4 in range(nch // 4):
                            P.w(t, "t", tok_tr_evac)
                            P.w(t, "t", tok_Pm)
                            if tl == 0 and g4 == 0:
                                P.w(t, "t", tok_pass1_sc)
                            for j in range(4):
                                k = g4 * 4 + j
                                ins = t.transpose(pTr[:, j * 128:(j + 1) * 128],
                                                  Pb[:, k * 128:(k + 1) * 128], P.ident[:])
                            tk_pe = P.inc(ins, "pe")
                            P.w(v, "v", tk_pe)
                            ins = v.tensor_copy(arena[:, tl * S + g4 * 512: tl * S + (g4 + 1) * 512],
                                                pTr[:, 0:512])
                            tok_tr_evac = P.inc(ins, "dv")
                            for j in range(4):
                                tok_ptb_chunk[(tl, g4 * 4 + j)] = tok_tr_evac
                        tok_P_free = P.tok("pe")
                        for k in range(nch):
                            P.w(t, "t", tok_ptb_chunk[(tl, k)])
                            if tl == 0 and k == 0:
                                P.w(t, "t", tok_l_evac)
                            ins = t.matmul(pL[0:1, tl * 128:(tl + 1) * 128], P.ones_colb[:],
                                           arena[:, tl * S + k * 128: tl * S + (k + 1) * 128],
                                           start=(k == 0), stop=(k == nch - 1))
                        tk_pe = P.inc(ins, "pe")
                        P.w(v, "v", tk_pe)
                        tok_l_evac = P.inc(v.tensor_copy(l4[0:1, tl * 128:(tl + 1) * 128],
                                                         pL[0:1, tl * 128:(tl + 1) * 128]), "dv")
                    for tl in range(4):
                        P.w(t, "t", tok_l_evac)
                        ins = t.matmul(pL[:, 256 + 2 * tl: 258 + 2 * tl], l4[0:32, tl * 128:(tl + 1) * 128],
                                       P.ones128f[0:32, 0:2], start=True, stop=True)
                    tk_pe = P.inc(ins, "pe")
                    P.w(v, "v", tk_pe)
                    for tl in range(4):
                        P.inc(v.tensor_copy(lcol4[:, tl:tl + 1], pL[:, 256 + 2 * tl: 257 + 2 * tl]), "dv")
                    P.w(v, "v", P.tok("dv"))
                    tok_linv = P.inc(v.reciprocal(linv4[:], lcol4[:]), "dv")
                    tok_pass2_pe = P.tok("pe")
                with nc.psum_tensor(f"dpO{gi}", [128, 8 * 512], F32) as pO:
                    tok_ob_evac = None
                    for dvh in range(2):
                        last_pv = None
                        for b in range(nblk):
                            bk = b % 2
                            tok_vb = P.dma(sy, "s", "ld",
                                           vbs[bk][:].rearrange("p (s d) -> p s d", s=4),
                                           P.vS[:, b * 4:(b + 1) * 4, dvh * 1024:(dvh + 1) * 1024],
                                           toks=(tok_vb_free[bk],))
                            P.w(t, "t", tok_vb)
                            if dvh == 0 and b == 0:
                                P.w(t, "t", tok_linv)
                                P.w(t, "t", tok_ob_spill)
                            if dvh == 1 and b == 0:
                                P.w(t, "t", tok_ob_evac)
                            for s4 in range(4):
                                k = b * 4 + s4
                                for tl in range(4):
                                    for dvc in range(2):
                                        ins = t.matmul(
                                            pO[:, (tl * 2 + dvc) * 512:(tl * 2 + dvc + 1) * 512],
                                            arena[:, tl * S + k * 128: tl * S + (k + 1) * 128],
                                            vbs[bk][:, s4 * 1024 + dvc * 512: s4 * 1024 + (dvc + 1) * 512],
                                            start=(k == 0), stop=(k == nch - 1))
                            last_pv = P.inc(ins, "pe")
                            tok_vb_free[bk] = last_pv
                        for tl in range(4):
                            P.w(a, "a", last_pv)
                            P.w(a, "a", tok_linv)
                            P.w(a, "a", tok_sp_ob)
                            ins = a.activation(ob4[:, tl * D + dvh * 1024: tl * D + (dvh + 1) * 1024],
                                               pO[:, tl * 1024: tl * 1024 + 1024], AF.Copy,
                                               bias=0.0, scale=linv4[:, tl:tl + 1])
                            tok_ob_evac = P.inc(ins, "ac")
                    tok_arena_free = P.tok("pe")
                    tok_psum_free = tok_ob_evac
                tok_sp_ob = P.dma(sy, "s", "ld", P.oS[:, gi * 4:(gi + 1) * 4, :],
                                  ob4[:].rearrange("p (s d) -> p s d", s=4),
                                  toks=(tok_ob_evac,))
                tok_ob_spill = tok_sp_ob
                tok_qt_free = P.tok("pe")
            self.tok_d_done = tok_sp_ob

    def emit_d2(self):
        P = self
        nc = self.nc
        g, v, a, t, sy = self.g, self.v, self.a, self.t, self.sy
        with ExitStack() as phase_ctx:
            wo_sb = phase_ctx.enter_context(nc.sbuf_tensor("ewo", [128, DC * D], BF16))
            ob = phase_ctx.enter_context(nc.sbuf_tensor("eob", [128, D], BF16))
            otb = phase_ctx.enter_context(nc.sbuf_tensor("eotb", [128, DC * 128], BF16))
            xnt = phase_ctx.enter_context(nc.sbuf_tensor("exn", [128, D], BF16))
            s_bf = phase_ctx.enter_context(nc.sbuf_tensor("esbf", [128, D], BF16))
            pS = phase_ctx.enter_context(nc.psum_tensor("epS2", [128, 2 * 512], F32))
            pTr = phase_ctx.enter_context(nc.psum_tensor("epT2", [128, 512], BF16))
            P.phase_start()
            tok_wo = P.dma(g, "g", "gd", wo_sb[:].rearrange("p (dc f) -> p dc f", dc=DC),
                           P.o_proj.rearrange("(dc p) f -> p dc f", p=128))
            tok_ob_free = None
            tok_xn_free = None
            tok_sbf_free = None
            tok_otb_free = None
            tok_pTr_free = None
            tok_sp_s = None
            tok_pS_free = [None, None]
            for qi in range(NT):
                tok_ob = P.dma(sy, "s", "ld", ob[:].rearrange("p (o d) -> p o d", o=1),
                               P.oS[:, qi:qi + 1, :], toks=(tok_ob_free, self.tok_d_done))
                tok_xn = P.dma(sy, "s", "ld", xnt[:].rearrange("p (o d) -> p o d", o=1),
                               P.xnS[:, qi:qi + 1, :], toks=(tok_xn_free,))
                tok_tr2 = tok_pTr_free
                P.w(t, "t", tok_otb_free)
                for g4 in range(4):
                    P.w(t, "t", tok_tr2)
                    P.w(t, "t", tok_ob)
                    for j in range(4):
                        dvc = g4 * 4 + j
                        ins = t.transpose(pTr[:, j * 128:(j + 1) * 128],
                                          ob[:, dvc * 128:(dvc + 1) * 128], P.ident[:])
                    tk_pe = P.inc(ins, "pe")
                    P.w(v, "v", tk_pe)
                    ins = v.tensor_copy(otb[:, g4 * 512:(g4 + 1) * 512], pTr[:, 0:512])
                    tok_tr2 = P.inc(ins, "dv")
                tok_pTr_free = tok_tr2
                P.w(t, "t", tok_wo)
                tok_s_parts = []
                for fc in range(4):
                    bank = fc % 2
                    P.w(t, "t", tok_s_parts[-2] if fc >= 2 else tok_pS_free[bank])
                    P.w(t, "t", tok_tr2)
                    for dvc in range(DC):
                        ins = t.matmul(pS[:, bank * 512:(bank + 1) * 512],
                                       otb[:, dvc * 128:(dvc + 1) * 128],
                                       wo_sb[:, dvc * D + fc * 512: dvc * D + (fc + 1) * 512],
                                       start=(dvc == 0), stop=(dvc == DC - 1))
                    tk_pe = P.inc(ins, "pe")
                    P.w(v, "v", tk_pe)
                    P.w(v, "v", tok_xn)
                    P.w(v, "v", tok_sbf_free)
                    ins = v.scalar_tensor_tensor(s_bf[:, fc * 512:(fc + 1) * 512],
                                                 pS[:, bank * 512:(bank + 1) * 512], 1.0,
                                                 xnt[:, fc * 512:(fc + 1) * 512],
                                                 ALU.mult, ALU.add)
                    tok_s_parts.append(P.inc(ins, "dv"))
                tok_pS_free = [tok_s_parts[2], tok_s_parts[3]]
                tok_sp_s = P.dma(sy, "s", "ld", P.sS[:, qi:qi + 1, :],
                                 s_bf[:].rearrange("p (o d) -> p o d", o=1),
                                 toks=(tok_s_parts[-1],))
                tok_sbf_free = tok_sp_s
                tok_ob_free = P.tok("pe")
                tok_xn_free = tok_s_parts[-1]
                tok_otb_free = P.tok("pe")
            self.tok_d2_done = tok_sp_s

    def emit_e_f(self):
        P = self
        nc = self.nc
        g, v, a, t, sy = self.g, self.v, self.a, self.t, self.sy
        with ExitStack() as phase_ctx:
            acc = phase_ctx.enter_context(nc.sbuf_tensor("eacc", [128, 8 * D], BF16))
            w1_sb = phase_ctx.enter_context(nc.sbuf_tensor("ew1", [128, DC * H], BF16))
            w2_sb = phase_ctx.enter_context(nc.sbuf_tensor("ew2", [128, HC * D], BF16))
            hT = phase_ctx.enter_context(nc.sbuf_tensor("ehT", [128, HC * 1024], BF16))
            xnTh = phase_ctx.enter_context(nc.sbuf_tensor("exnTh", [128, DC * 1024], BF16))
            b1e = phase_ctx.enter_context(nc.sbuf_tensor("eb1", [32, H], BF16))
            b2e = phase_ctx.enter_context(nc.sbuf_tensor("eb2", [32, D], BF16))
            fs = phase_ctx.enter_context(nc.sbuf_tensor("fs", [128, D], BF16))
            fo = phase_ctx.enter_context(nc.sbuf_tensor("fo", [128, D], F32))
            pH = phase_ctx.enter_context(nc.psum_tensor("epH", [128, 2 * 512], F32))
            pY = phase_ctx.enter_context(nc.psum_tensor("epY", [128, 4 * 512], F32))
            tok_w_free = None
            P.phase_start()
            tok_hT_free = None
            tok_xnT_free = None
            evacH = [None, None]
            tok_pY = {}
            g.memset(b1e[:], 0.0)
            P.inc(g.memset(b2e[:], 0.0), "gd")
            tok_fs_free = None
            tok_fo_free = None
            last_store = None
            tok_acc_free = None
            tok_xcopy = P.dma(sy, "s", "ld", P.out, P.xq)
            tok_w1 = tok_w2 = tok_b1 = tok_b2 = None
            tok_w1_free = tok_w2_free = None
            first = True
            for h in range(2):
                tok_acc = {}
                tok_xnT = P.dma(sy, "s", "ld",
                                xnTh[:].rearrange("p (dc t) -> p dc t", dc=DC),
                                P.xnTS[:, :, h * 1024:(h + 1) * 1024],
                                toks=(tok_xnT_free, self.tok_c1_done))
                for e in range(E):
                    if first:
                        tok_w1 = P.dma(g, "g", "gd", w1_sb[:].rearrange("p (dc h) -> p dc h", dc=DC),
                                       P.w1T[e * D:(e + 1) * D, :].rearrange("(dc p) h -> p dc h", p=128))
                        tok_b1 = P.dma(g, "g", "gd", b1e[0:1, :], P.b1[e:e + 1, :])
                        tok_w2 = P.dma(g, "g", "gd", w2_sb[:].rearrange("p (hc d) -> p hc d", hc=HC),
                                       P.w2T[e * H:(e + 1) * H, :].rearrange("(hc p) d -> p hc d", p=128))
                        tok_b2 = P.dma(g, "g", "gd", b2e[0:1, :], P.b2[e:e + 1, :])
                        first = False
                    for s in range(2):
                        for hc in range(HC):
                            bank = hc % 2
                            P.w(t, "t", tok_w1)
                            P.w(t, "t", tok_b1)
                            P.w(t, "t", tok_xnT)
                            P.w(t, "t", tok_hT_free)
                            P.w(t, "t", evacH[bank])
                            for dc in range(DC):
                                t.matmul(pH[:, bank * 512:(bank + 1) * 512],
                                         w1_sb[:, dc * H + hc * 128: dc * H + (hc + 1) * 128],
                                         xnTh[:, dc * 1024 + s * 512: dc * 1024 + (s + 1) * 512],
                                         start=(dc == 0), stop=False)
                            ins = t.matmul(pH[:, bank * 512:(bank + 1) * 512],
                                           b1e[0:32, hc * 128:(hc + 1) * 128],
                                           P.ones_row[0:32, 0:512], start=False, stop=True)
                            tk_pe = P.inc(ins, "pe")
                            P.w(a, "a", tk_pe)
                            ins = a.activation(hT[:, hc * 1024 + s * 512: hc * 1024 + (s + 1) * 512],
                                               pH[:, bank * 512:(bank + 1) * 512], AF.Silu,
                                               bias=0.0, scale=1.0)
                            evacH[bank] = P.inc(ins, "ac")
                    tok_hT_ready = P.tok("ac")
                    ne_h, ne_e = (h, e + 1) if e + 1 < E else (h + 1, 0)
                    if ne_h < 2:
                        tok_w1_free = P.tok("pe")
                        tok_w1 = P.dma(g, "g", "gd", w1_sb[:].rearrange("p (dc h) -> p dc h", dc=DC),
                                       P.w1T[ne_e * D:(ne_e + 1) * D, :].rearrange("(dc p) h -> p dc h", p=128),
                                       toks=(tok_w1_free,))
                        tok_b1 = P.dma(g, "g", "gd", b1e[0:1, :], P.b1[ne_e:ne_e + 1, :])
                    for tloc in range(8):
                        ti = h * 8 + tloc
                        for dc2 in range(4):
                            P.w(t, "t", tok_w2)
                            P.w(t, "t", tok_b2)
                            P.w(t, "t", tok_hT_ready)
                            P.w(t, "t", tok_pY.get(dc2))
                            for hc in range(HC):
                                t.matmul(pY[:, dc2 * 512:(dc2 + 1) * 512],
                                         hT[:, hc * 1024 + tloc * 128: hc * 1024 + (tloc + 1) * 128],
                                         w2_sb[:, hc * D + dc2 * 512: hc * D + (dc2 + 1) * 512],
                                         start=(hc == 0), stop=False)
                            ins = t.matmul(pY[:, dc2 * 512:(dc2 + 1) * 512],
                                           P.ones_row[0:32, 0:128],
                                           b2e[0:32, dc2 * 512:(dc2 + 1) * 512],
                                           start=False, stop=True)
                            tk_pe = P.inc(ins, "pe")
                            P.w(v, "v", tk_pe)
                            cslice = P.comb[:, ti * E + e: ti * E + e + 1]
                            dst = acc[:, tloc * D + dc2 * 512: tloc * D + (dc2 + 1) * 512]
                            if e == 0:
                                P.w(v, "v", tok_acc_free)
                                ins = v.tensor_scalar_mul(dst, pY[:, dc2 * 512:(dc2 + 1) * 512], cslice)
                            else:
                                P.w(v, "v", tok_acc.get((tloc, dc2)))
                                ins = v.scalar_tensor_tensor(dst, pY[:, dc2 * 512:(dc2 + 1) * 512],
                                                             cslice, dst, ALU.mult, ALU.add)
                            tok_acc[(tloc, dc2)] = P.inc(ins, "dv")
                            tok_pY[dc2] = tok_acc[(tloc, dc2)]
                    tok_hT_free = P.tok("pe")
                    if ne_h < 2:
                        tok_w2_free = P.tok("pe")
                        tok_w2 = P.dma(g, "g", "gd", w2_sb[:].rearrange("p (hc d) -> p hc d", hc=HC),
                                       P.w2T[ne_e * H:(ne_e + 1) * H, :].rearrange("(hc p) d -> p hc d", p=128),
                                       toks=(tok_w2_free,))
                        tok_b2 = P.dma(g, "g", "gd", b2e[0:1, :], P.b2[ne_e:ne_e + 1, :])
                tok_xnT_free = P.tok("pe")
                for tloc in range(NT // 2):
                    ti = h * 8 + tloc
                    tok_fs = P.dma(sy, "s", "ld", fs[:].rearrange("p (o d) -> p o d", o=1),
                                   P.sS[:, ti:ti + 1, :],
                                   toks=(tok_fs_free, self.tok_d2_done))
                    P.w(a, "a", tok_acc[(tloc, 3)])
                    asl = acc[:, tloc * D:(tloc + 1) * D]
                    tok_gs = P.inc(a.activation(asl, asl, AF.Silu, bias=0.0, scale=1.0), "ac")
                    P.w(v, "v", tok_gs)
                    P.w(v, "v", tok_fs)
                    P.w(v, "v", tok_fo_free)
                    tok_o = P.inc(v.tensor_tensor(fo[:], fs[:], asl, ALU.mult), "dv")
                    tok_fs_free = tok_o
                    P.w(g, "g", tok_o)
                    P.w(g, "g", tok_xcopy)
                    ins = g.dma_start(P.out.rearrange("(s p) d -> p s d", p=128)[:, ti:ti + 1, :],
                                      fo[:].rearrange("p (o d) -> p o d", o=1),
                                      accum_op=ALU.add)
                    last_store = P.inc(ins, "gd", 16)
                    tok_fo_free = last_store
                tok_acc_free = P.tok("dv")
            if P.debug:
                P.w(g, "g", P.tok("dv"))
                P.inc(g.dma_start(P.dbg_acc, acc[:]), "gd", 16)
            P.w(sy, "s", last_store)
            P.w(sy, "s", P.tok("ld"))


def build_program(debug=False):
    return Prog(debug=debug).build()


def kernel(L0_attn_x, L0_attn_qkv, L0_attn_o_proj, L0_attn_gamma, L0_attn_beta,
           L0_ffn_gate_w, L0_ffn_gate_b, L0_ffn_bias, L0_ffn_w1, L0_ffn_b1,
           L0_ffn_w2, L0_ffn_b2):
    x = np.ascontiguousarray(np.asarray(L0_attn_x, dtype=np.float32))
    qkv = np.ascontiguousarray(np.asarray(L0_attn_qkv, dtype=np.float32))
    o_proj = np.ascontiguousarray(np.asarray(L0_attn_o_proj, dtype=np.float32))
    gamma = np.asarray(L0_attn_gamma, dtype=np.float32)
    beta = np.asarray(L0_attn_beta, dtype=np.float32)
    gate_w = np.asarray(L0_ffn_gate_w, dtype=np.float32)
    gate_b = np.asarray(L0_ffn_gate_b, dtype=np.float32)
    gbias = np.asarray(L0_ffn_bias, dtype=np.float32)
    w1 = np.asarray(L0_ffn_w1, dtype=np.float32)
    b1 = np.ascontiguousarray(np.asarray(L0_ffn_b1, dtype=np.float32))
    w2 = np.asarray(L0_ffn_w2, dtype=np.float32)
    b2 = np.ascontiguousarray(np.asarray(L0_ffn_b2, dtype=np.float32))

    shared = dict(
        qkv=qkv,
        o_proj=o_proj,
        gammaT=np.ascontiguousarray(gamma.reshape(DC, 128).T),
        betaT=np.ascontiguousarray(beta.reshape(DC, 128).T),
        gate_wT=np.ascontiguousarray(gate_w.T),
        gate_b=np.ascontiguousarray(np.broadcast_to(gate_b.reshape(1, E), (128, E))),
        gate_bias=np.ascontiguousarray(np.broadcast_to(gbias.reshape(1, E), (128, E))),
        w1T=np.ascontiguousarray(w1.transpose(0, 2, 1).reshape(E * D, H)),
        w2T=np.ascontiguousarray(w2.transpose(0, 2, 1).reshape(E * H, D)),
        b1=b1, b2=b2,
    )
    in_maps = []
    metas = []
    for core in range(8):
        bi, c = core // 4, core % 4
        cA, cB = c, 7 - c
        xb = np.ascontiguousarray(x[bi])
        xq = np.ascontiguousarray(np.concatenate(
            [xb[cA * 1024:(cA + 1) * 1024], xb[cB * 1024:(cB + 1) * 1024]], axis=0))
        xqT = np.ascontiguousarray(xq.T)
        qpos = np.concatenate([np.arange(cA * 1024, (cA + 1) * 1024),
                               np.arange(cB * 1024, (cB + 1) * 1024)]).astype(np.float32)
        qpos_t = np.ascontiguousarray(qpos.reshape(NT, 128).T)
        m = dict(xb=xb, xq=xq, xqT=xqT, qpos=qpos_t)
        m.update(shared)
        in_maps.append(m)
        metas.append((bi, cA, cB))

    nc = build_program()
    res = run_bass_kernel_spmd(nc, in_maps, core_ids=list(range(8)))

    out = np.empty((2, S, D), dtype=np.float32)
    for core in range(8):
        bi, cA, cB = metas[core]
        oc = res.results[core]["out"]
        out[bi, cA * 1024:(cA + 1) * 1024] = oc[0:1024]
        out[bi, cB * 1024:(cB + 1) * 1024] = oc[1024:2048]
    return out


# version 05
# speedup vs baseline: 1.3582x; 1.0454x over previous
import numpy as np
from contextlib import ExitStack

import concourse.bass as bass
import concourse.mybir as mybir
from concourse.bass_utils import run_bass_kernel_spmd

F32 = mybir.dt.float32
BF16 = mybir.dt.bfloat16
I16 = mybir.dt.int16
AF = mybir.ActivationFunctionType
ALU = mybir.AluOpType
AX = mybir.AxisListType

D = 2048
S = 8192
TOWN = 2048
H = 1408
E = 8
EPS = 1e-5
SCALE = 1.0 / float(np.sqrt(2048.0))

DC = 16
HC = 11
NSB = 16
NT = 16


class Prog:
    def __init__(self, debug=False):
        self.debug = debug
        nc = bass.Bass("TRN2", target_bir_lowering=False)
        self.nc = nc
        self.ctx = ExitStack()
        dt = nc.dram_tensor
        self.xb = dt("xb", [S, D], F32, kind="ExternalInput").ap()
        self.xq = dt("xq", [TOWN, D], F32, kind="ExternalInput").ap()
        self.xqT = dt("xqT", [D, TOWN], F32, kind="ExternalInput").ap()
        self.qpos_i = dt("qpos", [128, NT], F32, kind="ExternalInput").ap()
        self.qkv = dt("qkv", [D, 3 * D], F32, kind="ExternalInput").ap()
        self.o_proj = dt("o_proj", [D, D], F32, kind="ExternalInput").ap()
        self.gammaT_i = dt("gammaT", [128, DC], F32, kind="ExternalInput").ap()
        self.betaT_i = dt("betaT", [128, DC], F32, kind="ExternalInput").ap()
        self.gate_wT = dt("gate_wT", [D, E], F32, kind="ExternalInput").ap()
        self.gate_b = dt("gate_b", [128, E], F32, kind="ExternalInput").ap()
        self.gate_bias = dt("gate_bias", [128, E], F32, kind="ExternalInput").ap()
        self.w1T = dt("w1T", [E * D, H], F32, kind="ExternalInput").ap()
        self.w2T = dt("w2T", [E * H, D], F32, kind="ExternalInput").ap()
        self.b1col = dt("b1col", [128, E * HC], F32, kind="ExternalInput").ap()
        self.b2 = dt("b2", [E, D], F32, kind="ExternalInput").ap()
        self.out = dt("out", [TOWN, D], F32, kind="ExternalOutput").ap()
        kind = "ExternalOutput" if debug else "Internal"
        self.xn2TS = dt("xn2TS", [128, DC, S], BF16, kind=kind).ap()
        self.xn2TownS = dt("xn2TownS", [128, DC, TOWN], BF16, kind=kind).ap()
        self.ktS = dt("ktS", [128, DC, S], BF16, kind=kind).ap()
        self.vS = dt("vS", [128, S // 128, D], BF16, kind=kind).ap()
        self.xnS = dt("xnS", [128, NT, D], BF16, kind=kind).ap()
        self.xnTS = dt("xnTS", [128, DC, TOWN], BF16, kind=kind).ap()
        self.qTS = dt("qTS", [128, DC, TOWN], BF16, kind=kind).ap()
        self.oS = dt("oS", [128, NT, D], BF16, kind=kind).ap()
        self.sS = dt("sS", [128, NT, D], BF16, kind=kind).ap()
        if debug:
            self.dbg_comb = dt("dbg_comb", [128, NT * E], F32, kind=kind).ap()
            self.dbg_acc = dt("dbg_acc", [128, 8 * D], F32, kind=kind).ap()
            self.dbg_raw = dt("dbg_raw", [128, NT * E], F32, kind=kind).ap()
            self.dbg_sc = dt("dbg_sc", [128, NT * E], F32, kind=kind).ap()
            self.dbg_gw = dt("dbg_gw", [128, DC * E], F32, kind=kind).ap()
            self.dbg_cons = dt("dbg_cons", [128, 2 * E], F32, kind=kind).ap()

        self.sems = {}
        self.ct = {}
        for name in ["ld", "gd", "pe", "dv", "ac"]:
            self.sems[name] = self.ctx.enter_context(nc.semaphore(name))
            self.ct[name] = 0
        self.lastw = {}

        sb = lambda n, shp, d=F32: self.ctx.enter_context(nc.sbuf_tensor(n, shp, d))
        self.ident = sb("ident", [128, 128], BF16)
        self.ones_row = sb("ones_row", [32, 512], BF16)
        self.ones128f = sb("ones128f", [128, 128], F32)
        self.ones_colb = sb("ones_colb", [128, 1], BF16)
        self.ones_colf = sb("ones_colf", [128, 1], F32)
        self.gammaT = sb("gammaT_s", [128, DC])
        self.betaT = sb("betaT_s", [128, DC])
        self.qpos = sb("qpos_s", [128, NT])
        self.gate_wT_sb = sb("gate_wT_sb", [128, DC * E])
        self.b1col_sb = sb("b1col_sb", [128, E * HC])
        self.gb_bc = sb("gb_bc", [128, E])
        self.gbias_bc = sb("gbias_bc", [128, E])
        self.rowsum_bc = sb("rowsum_bc", [128, E])
        self.gbb_bc = sb("gbb_bc", [128, E])
        self.comb = sb("comb", [128, NT * E])
        for n in ["sx", "sxx", "mu", "m2", "musq", "var", "a1", "l1", "nh", "r1",
                  "ia1", "var2", "a2", "l2", "nh2", "r2", "rr", "mrr", "mr1n"]:
            setattr(self, n, sb(n, [128, 4]))
        for n in ["g_t1", "g_sc", "g_eq1", "g_msk", "g_eq2", "g_c1"]:
            setattr(self, n, sb(n, [128, E]))
        self.dbg_raw_sb = sb("dbg_raw_sb", [128, NT * E])
        self.dbg_sc_sb = sb("dbg_sc_sb", [128, NT * E])
        for n in ["g_m1", "g_m2", "g_d12", "g_e", "g_den", "g_g1", "g_g2"]:
            setattr(self, n, sb(n, [128, 1]))

        self.g = nc.gpsimd
        self.v = nc.vector
        self.a = nc.scalar
        self.t = nc.tensor
        self.sy = nc.sync

    def inc(self, ins, sem, n=1):
        ins.then_inc(self.sems[sem], n)
        self.ct[sem] += n
        return (sem, self.ct[sem])

    def tok(self, sem):
        return (sem, self.ct[sem])

    def w(self, eng, engname, tok):
        if tok is None:
            return
        sem, val = tok
        if val <= 0:
            return
        key = (engname, sem)
        if self.lastw.get(key, 0) >= val:
            return
        eng.wait_ge(self.sems[sem], val)
        self.lastw[key] = val

    def dma(self, qeng, qname, sem, dst, src, toks=()):
        for t in toks:
            self.w(qeng, qname, t)
        ins = qeng.dma_start(dst, src)
        return self.inc(ins, sem, 16)

    def phase_start(self):
        toks = [self.tok(s) for s in ("pe", "dv", "ac", "ld", "gd")]
        for eng, name in ((self.t, "t"), (self.g, "g"), (self.sy, "s")):
            for tk in toks:
                if (name, tk[0]) not in (("t", "pe"), ("g", "gd"), ("s", "ld")):
                    self.w(eng, name, tk)

    def emit_stats(self, x4, x4_tok, want_mr1n):
        P, v, a = self, self.v, self.a
        t_sx = t_sxx = None
        for j in range(4):
            P.w(v, "v", x4_tok)
            t_sx = P.inc(v.reduce_sum(P.sx[:, j:j + 1], x4[:, j * D:(j + 1) * D], AX.X), "dv")
        for j in range(4):
            P.w(a, "a", x4_tok)
            t_sxx = P.inc(a.activation(P.sqdump[:], x4[:, j * D:(j + 1) * D], AF.Square,
                                       bias=0.0, scale=1.0, accum_out=P.sxx[:, j:j + 1]), "ac")
        P.w(v, "v", t_sx)
        tk = P.inc(v.tensor_scalar_mul(P.mu[:], P.sx[:], 1.0 / D), "dv")
        P.w(v, "v", t_sxx)
        P.w(v, "v", tk)
        tk = P.inc(v.tensor_scalar_mul(P.m2[:], P.sxx[:], 1.0 / D), "dv")
        P.w(v, "v", tk)
        tk = P.inc(v.tensor_tensor(P.musq[:], P.mu[:], P.mu[:], ALU.mult), "dv")
        P.w(v, "v", tk)
        tk = P.inc(v.tensor_tensor(P.var[:], P.m2[:], P.musq[:], ALU.subtract), "dv")
        P.w(v, "v", tk)
        t_a1 = P.inc(v.tensor_scalar_add(P.a1[:], P.var[:], EPS), "dv")
        P.w(a, "a", t_a1)
        tk = P.inc(a.activation(P.l1[:], P.a1[:], AF.Ln), "ac")
        P.w(v, "v", tk)
        tk = P.inc(v.tensor_scalar_mul(P.nh[:], P.l1[:], -0.5), "dv")
        P.w(a, "a", tk)
        t_r1 = P.inc(a.activation(P.r1[:], P.nh[:], AF.Exp), "ac")
        P.w(v, "v", t_a1)
        tk = P.inc(v.reciprocal(P.ia1[:], P.a1[:]), "dv")
        P.w(v, "v", tk)
        tk = P.inc(v.tensor_tensor(P.var2[:], P.var[:], P.ia1[:], ALU.mult), "dv")
        P.w(v, "v", tk)
        tk = P.inc(v.tensor_scalar_add(P.a2[:], P.var2[:], EPS), "dv")
        P.w(a, "a", tk)
        tk = P.inc(a.activation(P.l2[:], P.a2[:], AF.Ln), "ac")
        P.w(v, "v", tk)
        tk = P.inc(v.tensor_scalar_mul(P.nh2[:], P.l2[:], -0.5), "dv")
        P.w(a, "a", tk)
        tk = P.inc(a.activation(P.r2[:], P.nh2[:], AF.Exp), "ac")
        P.w(v, "v", tk)
        P.w(v, "v", t_r1)
        tk = P.inc(v.tensor_tensor(P.rr[:], P.r1[:], P.r2[:], ALU.mult), "dv")
        P.w(v, "v", tk)
        t_mrr = P.inc(v.scalar_tensor_tensor(P.mrr[:], P.mu[:], -1.0, P.rr[:], ALU.mult, ALU.mult), "dv")
        t_r1mu = None
        if want_mr1n:
            P.w(v, "v", t_mrr)
            t_r1mu = P.inc(v.scalar_tensor_tensor(P.mr1n[:], P.mu[:], -1.0, P.r1[:], ALU.mult, ALU.mult), "dv")
        return t_mrr, t_r1mu

    def emit_transposes(self, src_sb, src_tok, pT, evac_tok, dst, dst_dc_stride,
                        gamma, beta, n_tiles=4, free_tok=None):
        P, t, v = self, self.t, self.v
        last = None
        for dc in range(DC):
            bank = dc % 2
            P.w(t, "t", evac_tok[bank])
            P.w(t, "t", src_tok)
            if free_tok is not None:
                P.w(t, "t", free_tok)
            for j in range(n_tiles):
                ins = t.transpose(pT[bank][:, j * 128:(j + 1) * 128],
                                  src_sb[:, j * D + dc * 128: j * D + (dc + 1) * 128],
                                  P.ident[:])
            tk_pe = P.inc(ins, "pe")
            P.w(v, "v", tk_pe)
            dsl = dst[:, dc * dst_dc_stride: dc * dst_dc_stride + n_tiles * 128]
            if gamma is not None:
                ins = v.tensor_scalar(dsl, pT[bank][:, 0:n_tiles * 128],
                                      gamma[:, dc:dc + 1], beta[:, dc:dc + 1],
                                      ALU.mult, ALU.add)
            else:
                ins = v.tensor_copy(dsl, pT[bank][:, 0:n_tiles * 128])
            last = P.inc(ins, "dv")
            evac_tok[bank] = last
        return last

    def emit_proj_kc(self, w_sb, src, pX, evac_toks, dstbuf, dst_free_tok, ready_tok):
        P, t, a = self, self.t, self.a
        evac = evac_toks
        for kc in range(DC):
            bank = kc % 2
            P.w(t, "t", evac[bank])
            P.w(t, "t", ready_tok)
            for dc in range(DC):
                ins = t.matmul(pX[:, bank * 512:(bank + 1) * 512],
                               w_sb[:, dc * D + kc * 128: dc * D + (kc + 1) * 128],
                               src[:, dc * 512:(dc + 1) * 512],
                               start=(dc == 0), stop=(dc == DC - 1))
            tk_pe = P.inc(ins, "pe")
            P.w(a, "a", tk_pe)
            P.w(a, "a", dst_free_tok)
            ins = a.activation(dstbuf[:, kc * 512:(kc + 1) * 512],
                               pX[:, bank * 512:(bank + 1) * 512], AF.Copy,
                               bias=0.0, scale=1.0)
            evac[bank] = P.inc(ins, "ac")
        return evac[(DC - 1) % 2]

    def build(self):
        self.emit_prep()
        self.emit_a()
        self.emit_b()
        self.emit_c1()
        self.emit_c2()
        self.emit_d()
        self.emit_d2()
        self.emit_e_f()
        self.ctx.close()
        return self.nc

    def emit_prep(self):
        P = self
        g, v, a, t, sy = self.g, self.v, self.a, self.t, self.sy
        g.memset(P.ident[:], 0.0)
        g.affine_select(P.ident[:], P.ident[:], [[1, 128]], ALU.not_equal, 1.0,
                        base=0, channel_multiplier=-1)
        g.memset(P.ones_row[:], 0.0)
        g.memset(P.ones_row[0:1, :], 1.0)
        g.memset(P.ones128f[:], 1.0)
        g.memset(P.ones_colb[:], 1.0)
        tok_consts = P.inc(g.memset(P.ones_colf[:], 1.0), "gd")

        P.dma(sy, "s", "ld", P.gammaT[:], P.gammaT_i)
        P.dma(sy, "s", "ld", P.betaT[:], P.betaT_i)
        P.tok_qpos = P.dma(sy, "s", "ld", P.qpos[:], P.qpos_i)
        P.tok_b1col = P.dma(sy, "s", "ld", P.b1col_sb[:], P.b1col)
        tok_gw = P.dma(sy, "s", "ld",
                       P.gate_wT_sb[:].rearrange("p (dc e) -> p dc e", dc=DC),
                       P.gate_wT.rearrange("(dc p) e -> p dc e", p=128))
        tok_gb = P.dma(sy, "s", "ld", P.gb_bc[:], P.gate_b)
        tok_gbias = P.dma(sy, "s", "ld", P.gbias_bc[:], P.gate_bias)
        P.tok_gamma_ready = P.tok_qpos

        with self.nc.psum_tensor("pPrep", [128, 512], F32) as pP:
            P.w(t, "t", tok_gw)
            P.w(t, "t", tok_consts)
            for dc in range(DC):
                ins = t.matmul(pP[:, 0:E], P.ones128f[:], P.gate_wT_sb[:, dc * E:(dc + 1) * E],
                               start=(dc == 0), stop=(dc == DC - 1))
            tk_pe = P.inc(ins, "pe")
            P.w(v, "v", tk_pe)
            P.inc(v.tensor_copy(P.rowsum_bc[:], pP[:, 0:E]), "dv")
            P.w(v, "v", tok_gb)
            P.w(v, "v", tok_gbias)
            P.w(v, "v", P.tok("dv"))
            P.tok_gate_ready = P.inc(v.tensor_tensor(P.gbb_bc[:], P.gb_bc[:], P.gbias_bc[:], ALU.add), "dv")

    def emit_a(self):
        P = self
        nc = self.nc
        g, v, a, t, sy = self.g, self.v, self.a, self.t, self.sy
        with ExitStack() as phase_ctx:
            x4 = phase_ctx.enter_context(nc.sbuf_tensor("ax4", [128, 4 * D], F32))
            xnl2 = phase_ctx.enter_context(nc.sbuf_tensor("axnl2", [128, 4 * D], BF16))
            xn2T = phase_ctx.enter_context(nc.sbuf_tensor("axn2T", [128, DC * 512], BF16))
            ktbuf = phase_ctx.enter_context(nc.sbuf_tensor("aktbuf", [128, DC * 512], BF16))
            wk_sb = phase_ctx.enter_context(nc.sbuf_tensor("awk", [128, DC * D], BF16))
            self.sqdump = phase_ctx.enter_context(nc.sbuf_tensor("asq", [128, D], BF16))
            pK = phase_ctx.enter_context(nc.psum_tensor("apK", [128, 2 * 512], F32))
            pT0 = phase_ctx.enter_context(nc.psum_tensor("apT0", [128, 512], BF16))
            pT1 = phase_ctx.enter_context(nc.psum_tensor("apT1", [128, 512], BF16))
            pT = [pT0, pT1]
            P.phase_start()
            tok_wk = P.dma(g, "g", "gd", wk_sb[:].rearrange("p (dc f) -> p dc f", dc=DC),
                           P.qkv[:, D:2 * D].rearrange("(dc p) f -> p dc f", p=128))
            tok_x4_free = None
            tok_sp_xn2T = tok_sp_kt = None
            evac_tok = [None, None]
            evacK = [None, None]
            for sb_i in range(NSB):
                t0 = sb_i * 512
                tok_x4 = P.dma(sy, "s", "ld",
                               x4[:].rearrange("p (s d) -> p s d", s=4),
                               P.xb.rearrange("(s p) d -> p s d", p=128)[:, sb_i * 4:(sb_i + 1) * 4, :],
                               toks=(tok_x4_free, P.tok_gamma_ready))
                t_mrr, _ = P.emit_stats(x4, tok_x4, want_mr1n=False)
                last_xnl2 = None
                for j in range(4):
                    P.w(v, "v", t_mrr)
                    ins = v.tensor_scalar(xnl2[:, j * D:(j + 1) * D], x4[:, j * D:(j + 1) * D],
                                          P.rr[:, j:j + 1], P.mrr[:, j:j + 1], ALU.mult, ALU.add)
                    last_xnl2 = P.inc(ins, "dv")
                tok_x4_free = last_xnl2
                tok_xn2T_ready = P.emit_transposes(xnl2, last_xnl2, pT, evac_tok, xn2T, 512,
                                                  P.gammaT, P.betaT, free_tok=tok_sp_xn2T)
                P.w(t, "t", tok_wk)
                tok_kt_ready = P.emit_proj_kc(wk_sb, xn2T, pK, evacK, ktbuf,
                                              tok_sp_kt, tok_xn2T_ready)
                tok_sp_xn2T = P.dma(sy, "s", "ld", P.xn2TS[:, :, t0:t0 + 512],
                                    xn2T[:].rearrange("p (dc t) -> p dc t", dc=DC),
                                    toks=(tok_xn2T_ready,))
                tok_sp_kt = P.dma(sy, "s", "ld", P.ktS[:, :, t0:t0 + 512],
                                  ktbuf[:].rearrange("p (kc t) -> p kc t", kc=DC),
                                  toks=(tok_kt_ready,))
            self.tok_a_done = tok_sp_kt

    def emit_b(self):
        P = self
        nc = self.nc
        g, v, a, t, sy = self.g, self.v, self.a, self.t, self.sy
        with ExitStack() as phase_ctx:
            xt0 = phase_ctx.enter_context(nc.sbuf_tensor("bxt0", [128, DC * 512], BF16))
            xt1 = phase_ctx.enter_context(nc.sbuf_tensor("bxt1", [128, DC * 512], BF16))
            vbuf = phase_ctx.enter_context(nc.sbuf_tensor("bvbuf", [128, 4 * D], BF16))
            wv_sb = phase_ctx.enter_context(nc.sbuf_tensor("bwv", [128, DC * D], BF16))
            pV = phase_ctx.enter_context(nc.psum_tensor("bpV", [128, 2 * 512], F32))
            xts = [xt0, xt1]
            P.phase_start()
            tok_wv = P.dma(g, "g", "gd", wv_sb[:].rearrange("p (dc f) -> p dc f", dc=DC),
                           P.qkv[:, 2 * D:3 * D].rearrange("(dc p) f -> p dc f", p=128))
            tok_xt_free = [None, None]
            tok_sp_v = None
            evacV = [None, None]
            for sb_i in range(NSB):
                t0 = sb_i * 512
                bk = sb_i % 2
                tok_xt = P.dma(sy, "s", "ld", xts[bk][:].rearrange("p (dc t) -> p dc t", dc=DC),
                               P.xn2TS[:, :, t0:t0 + 512],
                               toks=(tok_xt_free[bk], self.tok_a_done))
                P.w(t, "t", tok_wv)
                nvb = 0
                tok_v_ready = None
                for j in range(4):
                    for dvc in range(4):
                        bank = nvb % 2
                        nvb += 1
                        P.w(t, "t", evacV[bank])
                        P.w(t, "t", tok_xt)
                        for dc in range(DC):
                            ins = t.matmul(pV[:, bank * 512:(bank + 1) * 512],
                                           xts[bk][:, dc * 512 + j * 128: dc * 512 + (j + 1) * 128],
                                           wv_sb[:, dc * D + dvc * 512: dc * D + (dvc + 1) * 512],
                                           start=(dc == 0), stop=(dc == DC - 1))
                        tk_pe = P.inc(ins, "pe")
                        P.w(a, "a", tk_pe)
                        P.w(a, "a", tok_sp_v)
                        ins = a.activation(vbuf[:, j * D + dvc * 512: j * D + (dvc + 1) * 512],
                                           pV[:, bank * 512:(bank + 1) * 512], AF.Copy,
                                           bias=0.0, scale=1.0)
                        evacV[bank] = P.inc(ins, "ac")
                        tok_v_ready = evacV[bank]
                tok_xt_free[bk] = P.tok("pe")
                tok_sp_v = P.dma(sy, "s", "ld", P.vS[:, sb_i * 4:(sb_i + 1) * 4, :],
                                 vbuf[:].rearrange("p (s d) -> p s d", s=4),
                                 toks=(tok_v_ready,))
            self.tok_b_done = tok_sp_v

    def emit_c1(self):
        P = self
        nc = self.nc
        g, v, a, t, sy = self.g, self.v, self.a, self.t, self.sy
        with ExitStack() as phase_ctx:
            x4 = phase_ctx.enter_context(nc.sbuf_tensor("cx4", [128, 4 * D], F32))
            xqT4 = phase_ctx.enter_context(nc.sbuf_tensor("cxqT4", [128, DC * 512], F32))
            xnl2 = phase_ctx.enter_context(nc.sbuf_tensor("cxnl2", [128, 4 * D], BF16))
            xn2T = phase_ctx.enter_context(nc.sbuf_tensor("cxn2T", [128, DC * 512], BF16))
            xn_b = phase_ctx.enter_context(nc.sbuf_tensor("cxn", [128, 4 * D], BF16))
            xnT = phase_ctx.enter_context(nc.sbuf_tensor("cxnT", [128, DC * 512], BF16))
            self.sqdump = phase_ctx.enter_context(nc.sbuf_tensor("csq", [128, D], BF16))
            pRaw = phase_ctx.enter_context(nc.psum_tensor("cpRaw", [128, 512], F32))
            pT0 = phase_ctx.enter_context(nc.psum_tensor("cpT0", [128, 512], BF16))
            pT1 = phase_ctx.enter_context(nc.psum_tensor("cpT1", [128, 512], BF16))
            pT = [pT0, pT1]
            P.phase_start()
            tok_x4_free = None
            tok_sp_xn = tok_sp_xnT = tok_sp_x2 = None
            tok_raw_read = None
            evac_tok = [None, None]
            for c in range(4):
                t0 = c * 512
                tok_x4 = P.dma(sy, "s", "ld",
                               x4[:].rearrange("p (s d) -> p s d", s=4),
                               P.xq.rearrange("(s p) d -> p s d", p=128)[:, c * 4:(c + 1) * 4, :],
                               toks=(tok_x4_free,))
                tok_xqT = P.dma(sy, "s", "ld",
                                xqT4[:].rearrange("p (dc t) -> p dc t", dc=DC),
                                P.xqT.rearrange("(dc p) t -> p dc t", p=128)[:, :, c * 512:(c + 1) * 512],
                                toks=(tok_raw_read,))
                t_mrr, t_r1mu = P.emit_stats(x4, tok_x4, want_mr1n=True)
                for j in range(4):
                    ti = c * 4 + j
                    P.w(t, "t", tok_xqT)
                    P.w(t, "t", P.tok_gate_ready)
                    P.w(t, "t", tok_raw_read)
                    for dc in range(DC):
                        ins = t.matmul(pRaw[:, 0:E],
                                       xqT4[:, dc * 512 + j * 128: dc * 512 + (j + 1) * 128],
                                       P.gate_wT_sb[:, dc * E:(dc + 1) * E],
                                       start=(dc == 0), stop=(dc == DC - 1))
                    tk_raw = P.inc(ins, "pe")
                    P.w(v, "v", tk_raw)
                    P.w(v, "v", t_r1mu)
                    if P.debug:
                        P.inc(v.tensor_copy(P.dbg_raw_sb[:, ti * E:(ti + 1) * E], pRaw[:, 0:E]), "dv")
                    tk = P.inc(v.scalar_tensor_tensor(P.g_t1[:], pRaw[:, 0:E], P.r1[:, j:j + 1],
                                                      P.gbb_bc[:], ALU.mult, ALU.add), "dv")
                    tok_raw_read = tk
                    P.w(v, "v", tk)
                    tk = P.inc(v.scalar_tensor_tensor(P.g_sc[:], P.rowsum_bc[:], P.mr1n[:, j:j + 1],
                                                      P.g_t1[:], ALU.mult, ALU.add), "dv")
                    P.w(v, "v", tk)
                    if P.debug:
                        P.inc(v.tensor_copy(P.dbg_sc_sb[:, ti * E:(ti + 1) * E], P.g_sc[:]), "dv")
                        P.w(v, "v", P.tok("dv"))
                    tk = P.inc(v.reduce_max(P.g_m1[:], P.g_sc[:], AX.X), "dv")
                    P.w(v, "v", tk)
                    tk = P.inc(v.tensor_scalar(P.g_eq1[:], P.g_sc[:], P.g_m1[:, 0:1], None, ALU.is_ge), "dv")
                    P.w(v, "v", tk)
                    tk = P.inc(v.scalar_tensor_tensor(P.g_msk[:], P.g_eq1[:], -1e30, P.g_sc[:],
                                                      ALU.mult, ALU.add), "dv")
                    P.w(v, "v", tk)
                    tk = P.inc(v.reduce_max(P.g_m2[:], P.g_msk[:], AX.X), "dv")
                    P.w(v, "v", tk)
                    tk = P.inc(v.tensor_scalar(P.g_eq2[:], P.g_msk[:], P.g_m2[:, 0:1], None, ALU.is_ge), "dv")
                    P.w(v, "v", tk)
                    tk = P.inc(v.tensor_tensor(P.g_d12[:], P.g_m1[:], P.g_m2[:], ALU.subtract), "dv")
                    P.w(a, "a", tk)
                    tk = P.inc(a.activation(P.g_e[:], P.g_d12[:], AF.Exp, bias=0.0, scale=-1.0), "ac")
                    P.w(v, "v", tk)
                    tk = P.inc(v.tensor_scalar_add(P.g_den[:], P.g_e[:], 1.0), "dv")
                    P.w(v, "v", tk)
                    tk = P.inc(v.reciprocal(P.g_g1[:], P.g_den[:]), "dv")
                    P.w(v, "v", tk)
                    tk = P.inc(v.tensor_scalar(P.g_g2[:], P.g_g1[:], -1.0, 1.0, ALU.mult, ALU.add), "dv")
                    P.w(v, "v", tk)
                    tk = P.inc(v.tensor_scalar_mul(P.g_c1[:], P.g_eq1[:], P.g_g1[:, 0:1]), "dv")
                    P.w(v, "v", tk)
                    P.inc(v.scalar_tensor_tensor(P.comb[:, ti * E:(ti + 1) * E], P.g_eq2[:],
                                                 P.g_g2[:, 0:1], P.g_c1[:], ALU.mult, ALU.add), "dv")
                last_xn = last_xnl2 = None
                for j in range(4):
                    P.w(v, "v", t_mrr)
                    P.w(v, "v", t_r1mu)
                    ins = v.tensor_scalar(xn_b[:, j * D:(j + 1) * D], x4[:, j * D:(j + 1) * D],
                                          P.r1[:, j:j + 1], P.mr1n[:, j:j + 1], ALU.mult, ALU.add)
                    last_xn = P.inc(ins, "dv")
                    ins = v.tensor_scalar(xnl2[:, j * D:(j + 1) * D], x4[:, j * D:(j + 1) * D],
                                          P.rr[:, j:j + 1], P.mrr[:, j:j + 1], ALU.mult, ALU.add)
                    last_xnl2 = P.inc(ins, "dv")
                tok_x4_free = last_xnl2
                tok_xnT_ready = P.emit_transposes(xn_b, last_xn, pT, evac_tok, xnT, 512,
                                                 None, None, free_tok=tok_sp_xnT)
                tok_x2_ready = P.emit_transposes(xnl2, last_xnl2, pT, evac_tok, xn2T, 512,
                                                 P.gammaT, P.betaT, free_tok=tok_sp_x2)
                tok_sp_xn = P.dma(sy, "s", "ld", P.xnS[:, c * 4:(c + 1) * 4, :],
                                  xn_b[:].rearrange("p (s d) -> p s d", s=4),
                                  toks=(last_xn,))
                tok_sp_xnT = P.dma(sy, "s", "ld", P.xnTS[:, :, t0:t0 + 512],
                                   xnT[:].rearrange("p (dc t) -> p dc t", dc=DC),
                                   toks=(tok_xnT_ready,))
                tok_sp_x2 = P.dma(sy, "s", "ld", P.xn2TownS[:, :, t0:t0 + 512],
                                  xn2T[:].rearrange("p (dc t) -> p dc t", dc=DC),
                                  toks=(tok_x2_ready,))
            self.tok_c1_done = tok_sp_x2
            if P.debug:
                P.w(g, "g", P.tok("dv"))
                P.inc(g.dma_start(P.dbg_comb, P.comb[:]), "gd", 16)
                P.inc(g.dma_start(P.dbg_raw, P.dbg_raw_sb[:]), "gd", 16)
                P.inc(g.dma_start(P.dbg_sc, P.dbg_sc_sb[:]), "gd", 16)
                P.inc(g.dma_start(P.dbg_gw, P.gate_wT_sb[:]), "gd", 16)
                P.inc(g.dma_start(P.dbg_cons[:, 0:E], P.rowsum_bc[:]), "gd", 16)
                P.inc(g.dma_start(P.dbg_cons[:, E:2 * E], P.gbb_bc[:]), "gd", 16)

    def emit_c2(self):
        P = self
        nc = self.nc
        g, v, a, t, sy = self.g, self.v, self.a, self.t, self.sy
        with ExitStack() as phase_ctx:
            xt0 = phase_ctx.enter_context(nc.sbuf_tensor("qxt0", [128, DC * 512], BF16))
            xt1 = phase_ctx.enter_context(nc.sbuf_tensor("qxt1", [128, DC * 512], BF16))
            qtbuf = phase_ctx.enter_context(nc.sbuf_tensor("qqt", [128, DC * 512], BF16))
            wq_sb = phase_ctx.enter_context(nc.sbuf_tensor("qwq", [128, DC * D], BF16))
            pQ = phase_ctx.enter_context(nc.psum_tensor("qpQ", [128, 2 * 512], F32))
            xts = [xt0, xt1]
            P.phase_start()
            tok_wq = P.dma(g, "g", "gd", wq_sb[:].rearrange("p (dc f) -> p dc f", dc=DC),
                           P.qkv[:, 0:D].rearrange("(dc p) f -> p dc f", p=128))
            tok_xt_free = [None, None]
            tok_sp_qt = None
            evacQ = [None, None]
            for c in range(4):
                t0 = c * 512
                bk = c % 2
                tok_xt = P.dma(sy, "s", "ld", xts[bk][:].rearrange("p (dc t) -> p dc t", dc=DC),
                               P.xn2TownS[:, :, t0:t0 + 512],
                               toks=(tok_xt_free[bk], self.tok_c1_done))
                P.w(t, "t", tok_wq)
                tok_qt_ready = P.emit_proj_kc(wq_sb, xts[bk], pQ, evacQ, qtbuf,
                                              tok_sp_qt, tok_xt)
                tok_xt_free[bk] = P.tok("pe")
                tok_sp_qt = P.dma(sy, "s", "ld", P.qTS[:, :, t0:t0 + 512],
                                  qtbuf[:].rearrange("p (kc t) -> p kc t", kc=DC),
                                  toks=(tok_qt_ready,))
            self.tok_c2_done = tok_sp_qt

    def emit_d(self):
        P = self
        nc = self.nc
        g, v, a, t, sy = self.g, self.v, self.a, self.t, self.sy
        with ExitStack() as phase_ctx:
            arena = phase_ctx.enter_context(nc.sbuf_tensor("dar", [128, 4 * S], BF16))
            Pb = phase_ctx.enter_context(nc.sbuf_tensor("dP", [128, S], BF16))
            mask = phase_ctx.enter_context(nc.sbuf_tensor("dmask", [128, S], BF16))
            kvidx = phase_ctx.enter_context(nc.sbuf_tensor("dkvidx", [128, S], I16))
            kt0 = phase_ctx.enter_context(nc.sbuf_tensor("dkt0", [128, DC * 512], BF16))
            kt1 = phase_ctx.enter_context(nc.sbuf_tensor("dkt1", [128, DC * 512], BF16))
            vb0 = phase_ctx.enter_context(nc.sbuf_tensor("dvb0", [128, 4 * 1024], BF16))
            vb1 = phase_ctx.enter_context(nc.sbuf_tensor("dvb1", [128, 4 * 1024], BF16))
            qt4 = phase_ctx.enter_context(nc.sbuf_tensor("dqt4", [128, DC * 512], BF16))
            ob4 = phase_ctx.enter_context(nc.sbuf_tensor("dob4", [128, 4 * D], BF16))
            bm4 = phase_ctx.enter_context(nc.sbuf_tensor("dbm4", [128, 64], F32))
            negm4 = phase_ctx.enter_context(nc.sbuf_tensor("dnegm4", [128, 4], F32))
            negms4 = phase_ctx.enter_context(nc.sbuf_tensor("dnegms4", [128, 4], F32))
            l4 = phase_ctx.enter_context(nc.sbuf_tensor("dl4", [32, 512], F32))
            lcol4 = phase_ctx.enter_context(nc.sbuf_tensor("dlcol4", [128, 4], F32))
            linv4 = phase_ctx.enter_context(nc.sbuf_tensor("dlinv4", [128, 4], F32))
            kts = [kt0, kt1]
            vbs = [vb0, vb1]
            P.phase_start()
            g.memset(l4[:], 0.0)
            tok_kvidx = P.inc(g.iota(kvidx[:], [[1, S]], base=0, channel_multiplier=0,
                                     allow_small_or_imprecise_dtypes=True), "gd")
            tok_kt_free = [None, None]
            tok_vb_free = [None, None]
            tok_qt_free = None
            tok_arena_free = None
            tok_psum_free = None
            tok_mask_free = None
            tok_P_free = None
            tok_ob_spill = None
            tok_sp_ob = None
            for gi in range(4):
                L = 2048 * (gi + 1)
                nblk = L // 512
                nch = L // 128
                tok_qt = P.dma(sy, "s", "ld", qt4[:].rearrange("p (kc t) -> p kc t", kc=DC),
                               P.qTS[:, :, gi * 512:(gi + 1) * 512],
                               toks=(tok_qt_free, self.tok_c2_done))
                with nc.psum_tensor(f"dpS{gi}", [128, 2 * 512], F32) as pS:
                    tok_sc_evac = {}
                    nb_mm = 0
                    for b in range(nblk):
                        bk = b % 2
                        tok_kt = P.dma(sy, "s", "ld",
                                       kts[bk][:].rearrange("p (kc t) -> p kc t", kc=DC),
                                       P.ktS[:, :, b * 512:(b + 1) * 512],
                                       toks=(tok_kt_free[bk], self.tok_b_done))
                        for tl in range(4):
                            bank = nb_mm % 2
                            nb_mm += 1
                            P.w(t, "t", tok_kt)
                            P.w(t, "t", tok_qt)
                            P.w(t, "t", tok_sc_evac.get(nb_mm - 3))
                            P.w(t, "t", tok_psum_free)
                            P.w(t, "t", tok_arena_free)
                            for kc in range(DC):
                                ins = t.matmul(pS[:, bank * 512:(bank + 1) * 512],
                                               qt4[:, kc * 512 + tl * 128: kc * 512 + (tl + 1) * 128],
                                               kts[bk][:, kc * 512:(kc + 1) * 512],
                                               start=(kc == 0), stop=(kc == DC - 1))
                            tk_pe = P.inc(ins, "pe")
                            if tl == 3:
                                tok_kt_free[bk] = tk_pe
                            P.w(a, "a", tk_pe)
                            P.w(a, "a", tok_arena_free)
                            P.w(a, "a", tok_P_free)
                            ins = a.activation(arena[:, tl * S + b * 512: tl * S + (b + 1) * 512],
                                               pS[:, bank * 512:(bank + 1) * 512], AF.Copy,
                                               bias=0.0, scale=1.0)
                            tok_sc_evac[nb_mm - 1] = P.inc(ins, "ac")
                            P.w(v, "v", tok_sc_evac[nb_mm - 1])
                            P.inc(v.reduce_max(bm4[:, tl * 16 + b: tl * 16 + b + 1],
                                               arena[:, tl * S + b * 512: tl * S + (b + 1) * 512],
                                               AX.X), "dv")
                    P.w(v, "v", P.tok("dv"))
                    for tl in range(4):
                        P.inc(v.reduce_max(negm4[:, tl:tl + 1], bm4[:, tl * 16: tl * 16 + nblk],
                                           AX.X, negate=True), "dv")
                    P.w(v, "v", P.tok("dv"))
                    tok_negms = P.inc(v.tensor_scalar_mul(negms4[:], negm4[:], SCALE), "dv")
                    tok_pass1_sc = P.tok("ac")
                with nc.psum_tensor(f"dpT{gi}", [128, 512], BF16) as pTr, \
                     nc.psum_tensor(f"dpL{gi}", [128, 512], F32) as pL:
                    tok_tr_evac = None
                    tok_l_evac = None
                    tok_ptb_chunk = {}
                    for tl in range(4):
                        ti = gi * 4 + tl
                        P.w(v, "v", tok_kvidx)
                        P.w(v, "v", tok_mask_free)
                        P.w(v, "v", P.tok_qpos)
                        tok_mask = P.inc(v.tensor_scalar(mask[:, 0:L], kvidx[:, 0:L],
                                                         P.qpos[:, ti:ti + 1], None, ALU.is_le), "dv")
                        P.w(a, "a", tok_negms)
                        P.w(a, "a", tok_P_free)
                        tok_exp = P.inc(a.activation(Pb[:, 0:L], arena[:, tl * S: tl * S + L], AF.Exp,
                                                     bias=negms4[:, tl:tl + 1], scale=SCALE), "ac")
                        P.w(v, "v", tok_exp)
                        P.w(v, "v", tok_mask)
                        tok_Pm = P.inc(v.tensor_tensor(Pb[:, 0:L], Pb[:, 0:L], mask[:, 0:L], ALU.mult), "dv")
                        tok_mask_free = tok_Pm
                        for g4 in range(nch // 4):
                            P.w(t, "t", tok_tr_evac)
                            P.w(t, "t", tok_Pm)
                            if tl == 0 and g4 == 0:
                                P.w(t, "t", tok_pass1_sc)
                            for j in range(4):
                                k = g4 * 4 + j
                                ins = t.transpose(pTr[:, j * 128:(j + 1) * 128],
                                                  Pb[:, k * 128:(k + 1) * 128], P.ident[:])
                            tk_pe = P.inc(ins, "pe")
                            P.w(v, "v", tk_pe)
                            ins = v.tensor_copy(arena[:, tl * S + g4 * 512: tl * S + (g4 + 1) * 512],
                                                pTr[:, 0:512])
                            tok_tr_evac = P.inc(ins, "dv")
                            for j in range(4):
                                tok_ptb_chunk[(tl, g4 * 4 + j)] = tok_tr_evac
                        tok_P_free = P.tok("pe")
                        for k in range(nch):
                            P.w(t, "t", tok_ptb_chunk[(tl, k)])
                            if tl == 0 and k == 0:
                                P.w(t, "t", tok_l_evac)
                            ins = t.matmul(pL[0:1, tl * 128:(tl + 1) * 128], P.ones_colb[:],
                                           arena[:, tl * S + k * 128: tl * S + (k + 1) * 128],
                                           start=(k == 0), stop=(k == nch - 1))
                        tk_pe = P.inc(ins, "pe")
                        P.w(v, "v", tk_pe)
                        tok_l_evac = P.inc(v.tensor_copy(l4[0:1, tl * 128:(tl + 1) * 128],
                                                         pL[0:1, tl * 128:(tl + 1) * 128]), "dv")
                    for tl in range(4):
                        P.w(t, "t", tok_l_evac)
                        ins = t.matmul(pL[:, 256 + 2 * tl: 258 + 2 * tl], l4[0:32, tl * 128:(tl + 1) * 128],
                                       P.ones128f[0:32, 0:2], start=True, stop=True)
                    tk_pe = P.inc(ins, "pe")
                    P.w(v, "v", tk_pe)
                    for tl in range(4):
                        P.inc(v.tensor_copy(lcol4[:, tl:tl + 1], pL[:, 256 + 2 * tl: 257 + 2 * tl]), "dv")
                    P.w(v, "v", P.tok("dv"))
                    tok_linv = P.inc(v.reciprocal(linv4[:], lcol4[:]), "dv")
                    tok_pass2_pe = P.tok("pe")
                with nc.psum_tensor(f"dpO{gi}", [128, 8 * 512], F32) as pO:
                    tok_ob_evac = None
                    for dvh in range(2):
                        last_pv = None
                        for b in range(nblk):
                            bk = b % 2
                            tok_vb = P.dma(sy, "s", "ld",
                                           vbs[bk][:].rearrange("p (s d) -> p s d", s=4),
                                           P.vS[:, b * 4:(b + 1) * 4, dvh * 1024:(dvh + 1) * 1024],
                                           toks=(tok_vb_free[bk],))
                            P.w(t, "t", tok_vb)
                            if dvh == 0 and b == 0:
                                P.w(t, "t", tok_linv)
                                P.w(t, "t", tok_ob_spill)
                            if dvh == 1 and b == 0:
                                P.w(t, "t", tok_ob_evac)
                            for s4 in range(4):
                                k = b * 4 + s4
                                for tl in range(4):
                                    for dvc in range(2):
                                        ins = t.matmul(
                                            pO[:, (tl * 2 + dvc) * 512:(tl * 2 + dvc + 1) * 512],
                                            arena[:, tl * S + k * 128: tl * S + (k + 1) * 128],
                                            vbs[bk][:, s4 * 1024 + dvc * 512: s4 * 1024 + (dvc + 1) * 512],
                                            start=(k == 0), stop=(k == nch - 1))
                            last_pv = P.inc(ins, "pe")
                            tok_vb_free[bk] = last_pv
                        for tl in range(4):
                            P.w(a, "a", last_pv)
                            P.w(a, "a", tok_linv)
                            P.w(a, "a", tok_sp_ob)
                            ins = a.activation(ob4[:, tl * D + dvh * 1024: tl * D + (dvh + 1) * 1024],
                                               pO[:, tl * 1024: tl * 1024 + 1024], AF.Copy,
                                               bias=0.0, scale=linv4[:, tl:tl + 1])
                            tok_ob_evac = P.inc(ins, "ac")
                    tok_arena_free = P.tok("pe")
                    tok_psum_free = tok_ob_evac
                tok_sp_ob = P.dma(sy, "s", "ld", P.oS[:, gi * 4:(gi + 1) * 4, :],
                                  ob4[:].rearrange("p (s d) -> p s d", s=4),
                                  toks=(tok_ob_evac,))
                tok_ob_spill = tok_sp_ob
                tok_qt_free = P.tok("pe")
            self.tok_d_done = tok_sp_ob

    def emit_d2(self):
        P = self
        nc = self.nc
        g, v, a, t, sy = self.g, self.v, self.a, self.t, self.sy
        with ExitStack() as phase_ctx:
            wo_sb = phase_ctx.enter_context(nc.sbuf_tensor("ewo", [128, DC * D], BF16))
            ob = phase_ctx.enter_context(nc.sbuf_tensor("eob", [128, D], BF16))
            otb = phase_ctx.enter_context(nc.sbuf_tensor("eotb", [128, DC * 128], BF16))
            xnt = phase_ctx.enter_context(nc.sbuf_tensor("exn", [128, D], BF16))
            s_bf = phase_ctx.enter_context(nc.sbuf_tensor("esbf", [128, D], BF16))
            pS = phase_ctx.enter_context(nc.psum_tensor("epS2", [128, 2 * 512], F32))
            pTr = phase_ctx.enter_context(nc.psum_tensor("epT2", [128, 512], BF16))
            P.phase_start()
            tok_wo = P.dma(g, "g", "gd", wo_sb[:].rearrange("p (dc f) -> p dc f", dc=DC),
                           P.o_proj.rearrange("(dc p) f -> p dc f", p=128))
            tok_ob_free = None
            tok_xn_free = None
            tok_sbf_free = None
            tok_otb_free = None
            tok_pTr_free = None
            tok_sp_s = None
            tok_pS_free = [None, None]
            for qi in range(NT):
                tok_ob = P.dma(sy, "s", "ld", ob[:].rearrange("p (o d) -> p o d", o=1),
                               P.oS[:, qi:qi + 1, :], toks=(tok_ob_free, self.tok_d_done))
                tok_xn = P.dma(sy, "s", "ld", xnt[:].rearrange("p (o d) -> p o d", o=1),
                               P.xnS[:, qi:qi + 1, :], toks=(tok_xn_free,))
                tok_tr2 = tok_pTr_free
                P.w(t, "t", tok_otb_free)
                for g4 in range(4):
                    P.w(t, "t", tok_tr2)
                    P.w(t, "t", tok_ob)
                    for j in range(4):
                        dvc = g4 * 4 + j
                        ins = t.transpose(pTr[:, j * 128:(j + 1) * 128],
                                          ob[:, dvc * 128:(dvc + 1) * 128], P.ident[:])
                    tk_pe = P.inc(ins, "pe")
                    P.w(v, "v", tk_pe)
                    ins = v.tensor_copy(otb[:, g4 * 512:(g4 + 1) * 512], pTr[:, 0:512])
                    tok_tr2 = P.inc(ins, "dv")
                tok_pTr_free = tok_tr2
                P.w(t, "t", tok_wo)
                tok_s_parts = []
                for fc in range(4):
                    bank = fc % 2
                    P.w(t, "t", tok_s_parts[-2] if fc >= 2 else tok_pS_free[bank])
                    P.w(t, "t", tok_tr2)
                    for dvc in range(DC):
                        ins = t.matmul(pS[:, bank * 512:(bank + 1) * 512],
                                       otb[:, dvc * 128:(dvc + 1) * 128],
                                       wo_sb[:, dvc * D + fc * 512: dvc * D + (fc + 1) * 512],
                                       start=(dvc == 0), stop=(dvc == DC - 1))
                    tk_pe = P.inc(ins, "pe")
                    P.w(v, "v", tk_pe)
                    P.w(v, "v", tok_xn)
                    P.w(v, "v", tok_sbf_free)
                    ins = v.scalar_tensor_tensor(s_bf[:, fc * 512:(fc + 1) * 512],
                                                 pS[:, bank * 512:(bank + 1) * 512], 1.0,
                                                 xnt[:, fc * 512:(fc + 1) * 512],
                                                 ALU.mult, ALU.add)
                    tok_s_parts.append(P.inc(ins, "dv"))
                tok_pS_free = [tok_s_parts[2], tok_s_parts[3]]
                tok_sp_s = P.dma(sy, "s", "ld", P.sS[:, qi:qi + 1, :],
                                 s_bf[:].rearrange("p (o d) -> p o d", o=1),
                                 toks=(tok_s_parts[-1],))
                tok_sbf_free = tok_sp_s
                tok_ob_free = P.tok("pe")
                tok_xn_free = tok_s_parts[-1]
                tok_otb_free = P.tok("pe")
            self.tok_d2_done = tok_sp_s

    def emit_e_f(self):
        P = self
        nc = self.nc
        g, v, a, t, sy = self.g, self.v, self.a, self.t, self.sy
        with ExitStack() as phase_ctx:
            acc = phase_ctx.enter_context(nc.sbuf_tensor("eacc", [128, 8 * D], BF16))
            w1_sb = phase_ctx.enter_context(nc.sbuf_tensor("ew1", [128, DC * H], BF16))
            w2_sb = phase_ctx.enter_context(nc.sbuf_tensor("ew2", [128, HC * D], BF16))
            hT = phase_ctx.enter_context(nc.sbuf_tensor("ehT", [128, HC * 1024], BF16))
            xnTh = phase_ctx.enter_context(nc.sbuf_tensor("exnTh", [128, DC * 1024], BF16))
            b2e = phase_ctx.enter_context(nc.sbuf_tensor("eb2", [32, D], BF16))
            fs = phase_ctx.enter_context(nc.sbuf_tensor("fs", [128, D], BF16))
            fo = phase_ctx.enter_context(nc.sbuf_tensor("fo", [128, D], F32))
            pH = phase_ctx.enter_context(nc.psum_tensor("epH", [128, 2 * 512], F32))
            pY = phase_ctx.enter_context(nc.psum_tensor("epY", [128, 4 * 512], F32))
            tok_w_free = None
            P.phase_start()
            tok_hT_free = None
            tok_xnT_free = None
            evacH = [None, None]
            tok_pY = {}
            P.inc(g.memset(b2e[:], 0.0), "gd")
            tok_fs_free = None
            tok_fo_free = None
            last_store = None
            tok_acc_free = None
            tok_xcopy = P.dma(sy, "s", "ld", P.out, P.xq)
            tok_w1 = tok_w2 = tok_b1 = tok_b2 = None
            tok_w1_free = tok_w2_free = None
            first = True
            for h in range(2):
                tok_acc = {}
                tok_xnT = P.dma(sy, "s", "ld",
                                xnTh[:].rearrange("p (dc t) -> p dc t", dc=DC),
                                P.xnTS[:, :, h * 1024:(h + 1) * 1024],
                                toks=(tok_xnT_free, self.tok_c1_done))
                for e in range(E):
                    if first:
                        tok_w1 = P.dma(g, "g", "gd", w1_sb[:].rearrange("p (dc h) -> p dc h", dc=DC),
                                       P.w1T[e * D:(e + 1) * D, :].rearrange("(dc p) h -> p dc h", p=128))
                        tok_w2 = P.dma(g, "g", "gd", w2_sb[:].rearrange("p (hc d) -> p hc d", hc=HC),
                                       P.w2T[e * H:(e + 1) * H, :].rearrange("(hc p) d -> p hc d", p=128))
                        tok_b2 = P.dma(g, "g", "gd", b2e[0:1, :], P.b2[e:e + 1, :])
                        first = False
                    for s in range(2):
                        for hc in range(HC):
                            bank = hc % 2
                            P.w(t, "t", tok_w1)
                            P.w(t, "t", tok_xnT)
                            P.w(t, "t", tok_hT_free)
                            P.w(t, "t", evacH[bank])
                            for dc in range(DC):
                                ins = t.matmul(pH[:, bank * 512:(bank + 1) * 512],
                                               w1_sb[:, dc * H + hc * 128: dc * H + (hc + 1) * 128],
                                               xnTh[:, dc * 1024 + s * 512: dc * 1024 + (s + 1) * 512],
                                               start=(dc == 0), stop=(dc == DC - 1))
                            tk_pe = P.inc(ins, "pe")
                            P.w(a, "a", tk_pe)
                            P.w(a, "a", P.tok_b1col)
                            ins = a.activation(hT[:, hc * 1024 + s * 512: hc * 1024 + (s + 1) * 512],
                                               pH[:, bank * 512:(bank + 1) * 512], AF.Silu,
                                               bias=P.b1col_sb[:, e * HC + hc: e * HC + hc + 1],
                                               scale=1.0)
                            evacH[bank] = P.inc(ins, "ac")
                    tok_hT_ready = P.tok("ac")
                    ne_h, ne_e = (h, e + 1) if e + 1 < E else (h + 1, 0)
                    if ne_h < 2:
                        tok_w1_free = P.tok("pe")
                        tok_w1 = P.dma(g, "g", "gd", w1_sb[:].rearrange("p (dc h) -> p dc h", dc=DC),
                                       P.w1T[ne_e * D:(ne_e + 1) * D, :].rearrange("(dc p) h -> p dc h", p=128),
                                       toks=(tok_w1_free,))
                    for tloc in range(8):
                        ti = h * 8 + tloc
                        for dc2 in range(4):
                            P.w(t, "t", tok_w2)
                            P.w(t, "t", tok_b2)
                            P.w(t, "t", tok_hT_ready)
                            P.w(t, "t", tok_pY.get(dc2))
                            for hc in range(HC):
                                t.matmul(pY[:, dc2 * 512:(dc2 + 1) * 512],
                                         hT[:, hc * 1024 + tloc * 128: hc * 1024 + (tloc + 1) * 128],
                                         w2_sb[:, hc * D + dc2 * 512: hc * D + (dc2 + 1) * 512],
                                         start=(hc == 0), stop=False)
                            ins = t.matmul(pY[:, dc2 * 512:(dc2 + 1) * 512],
                                           P.ones_row[0:32, 0:128],
                                           b2e[0:32, dc2 * 512:(dc2 + 1) * 512],
                                           start=False, stop=True)
                            tk_pe = P.inc(ins, "pe")
                            P.w(v, "v", tk_pe)
                            cslice = P.comb[:, ti * E + e: ti * E + e + 1]
                            dst = acc[:, tloc * D + dc2 * 512: tloc * D + (dc2 + 1) * 512]
                            if e == 0:
                                P.w(v, "v", tok_acc_free)
                                ins = v.tensor_scalar_mul(dst, pY[:, dc2 * 512:(dc2 + 1) * 512], cslice)
                            else:
                                P.w(v, "v", tok_acc.get((tloc, dc2)))
                                ins = v.scalar_tensor_tensor(dst, pY[:, dc2 * 512:(dc2 + 1) * 512],
                                                             cslice, dst, ALU.mult, ALU.add)
                            tok_acc[(tloc, dc2)] = P.inc(ins, "dv")
                            tok_pY[dc2] = tok_acc[(tloc, dc2)]
                    tok_hT_free = P.tok("pe")
                    if ne_h < 2:
                        tok_w2_free = P.tok("pe")
                        tok_w2 = P.dma(g, "g", "gd", w2_sb[:].rearrange("p (hc d) -> p hc d", hc=HC),
                                       P.w2T[ne_e * H:(ne_e + 1) * H, :].rearrange("(hc p) d -> p hc d", p=128),
                                       toks=(tok_w2_free,))
                        tok_b2 = P.dma(g, "g", "gd", b2e[0:1, :], P.b2[ne_e:ne_e + 1, :])
                tok_xnT_free = P.tok("pe")
                for tloc in range(NT // 2):
                    ti = h * 8 + tloc
                    tok_fs = P.dma(sy, "s", "ld", fs[:].rearrange("p (o d) -> p o d", o=1),
                                   P.sS[:, ti:ti + 1, :],
                                   toks=(tok_fs_free, self.tok_d2_done))
                    P.w(a, "a", tok_acc[(tloc, 3)])
                    asl = acc[:, tloc * D:(tloc + 1) * D]
                    tok_gs = P.inc(a.activation(asl, asl, AF.Silu, bias=0.0, scale=1.0), "ac")
                    P.w(v, "v", tok_gs)
                    P.w(v, "v", tok_fs)
                    P.w(v, "v", tok_fo_free)
                    tok_o = P.inc(v.tensor_tensor(fo[:], fs[:], asl, ALU.mult), "dv")
                    tok_fs_free = tok_o
                    P.w(g, "g", tok_o)
                    P.w(g, "g", tok_xcopy)
                    ins = g.dma_start(P.out.rearrange("(s p) d -> p s d", p=128)[:, ti:ti + 1, :],
                                      fo[:].rearrange("p (o d) -> p o d", o=1),
                                      accum_op=ALU.add)
                    last_store = P.inc(ins, "gd", 16)
                    tok_fo_free = last_store
                tok_acc_free = P.tok("dv")
            if P.debug:
                P.w(g, "g", P.tok("dv"))
                P.inc(g.dma_start(P.dbg_acc, acc[:]), "gd", 16)
            P.w(sy, "s", last_store)
            P.w(sy, "s", P.tok("ld"))


def build_program(debug=False):
    return Prog(debug=debug).build()


def kernel(L0_attn_x, L0_attn_qkv, L0_attn_o_proj, L0_attn_gamma, L0_attn_beta,
           L0_ffn_gate_w, L0_ffn_gate_b, L0_ffn_bias, L0_ffn_w1, L0_ffn_b1,
           L0_ffn_w2, L0_ffn_b2):
    x = np.ascontiguousarray(np.asarray(L0_attn_x, dtype=np.float32))
    qkv = np.ascontiguousarray(np.asarray(L0_attn_qkv, dtype=np.float32))
    o_proj = np.ascontiguousarray(np.asarray(L0_attn_o_proj, dtype=np.float32))
    gamma = np.asarray(L0_attn_gamma, dtype=np.float32)
    beta = np.asarray(L0_attn_beta, dtype=np.float32)
    gate_w = np.asarray(L0_ffn_gate_w, dtype=np.float32)
    gate_b = np.asarray(L0_ffn_gate_b, dtype=np.float32)
    gbias = np.asarray(L0_ffn_bias, dtype=np.float32)
    w1 = np.asarray(L0_ffn_w1, dtype=np.float32)
    b1 = np.ascontiguousarray(np.asarray(L0_ffn_b1, dtype=np.float32))
    w2 = np.asarray(L0_ffn_w2, dtype=np.float32)
    b2 = np.ascontiguousarray(np.asarray(L0_ffn_b2, dtype=np.float32))

    shared = dict(
        qkv=qkv,
        o_proj=o_proj,
        gammaT=np.ascontiguousarray(gamma.reshape(DC, 128).T),
        betaT=np.ascontiguousarray(beta.reshape(DC, 128).T),
        gate_wT=np.ascontiguousarray(gate_w.T),
        gate_b=np.ascontiguousarray(np.broadcast_to(gate_b.reshape(1, E), (128, E))),
        gate_bias=np.ascontiguousarray(np.broadcast_to(gbias.reshape(1, E), (128, E))),
        w1T=np.ascontiguousarray(w1.transpose(0, 2, 1).reshape(E * D, H)),
        w2T=np.ascontiguousarray(w2.transpose(0, 2, 1).reshape(E * H, D)),
        b1col=np.ascontiguousarray(b1.reshape(E, HC, 128).transpose(2, 0, 1).reshape(128, E * HC)),
        b2=b2,
    )
    in_maps = []
    metas = []
    for core in range(8):
        bi, c = core // 4, core % 4
        tiles = np.array([16 * k + 4 * c + j for k in range(4) for j in range(4)])
        rows = (tiles[:, None] * 128 + np.arange(128)[None, :]).reshape(-1)
        xb = np.ascontiguousarray(x[bi])
        xq = np.ascontiguousarray(xb[rows])
        xqT = np.ascontiguousarray(xq.T)
        qpos_t = np.ascontiguousarray(rows.astype(np.float32).reshape(NT, 128).T)
        m = dict(xb=xb, xq=xq, xqT=xqT, qpos=qpos_t)
        m.update(shared)
        in_maps.append(m)
        metas.append((bi, rows))

    nc = build_program()
    res = run_bass_kernel_spmd(nc, in_maps, core_ids=list(range(8)))

    out = np.empty((2, S, D), dtype=np.float32)
    for core in range(8):
        bi, rows = metas[core]
        out[bi, rows] = res.results[core]["out"]
    return out


# version 06
# speedup vs baseline: 1.3634x; 1.0038x over previous
import numpy as np
from contextlib import ExitStack

import concourse.bass as bass
import concourse.mybir as mybir
from concourse.bass_utils import run_bass_kernel_spmd

F32 = mybir.dt.float32
BF16 = mybir.dt.bfloat16
I16 = mybir.dt.int16
AF = mybir.ActivationFunctionType
ALU = mybir.AluOpType
AX = mybir.AxisListType

D = 2048
S = 8192
TOWN = 2048
H = 1408
E = 8
EPS = 1e-5
SCALE = 1.0 / float(np.sqrt(2048.0))

DC = 16
HC = 11
NSB = 16
NT = 16


class Prog:
    def __init__(self, debug=False):
        self.debug = debug
        nc = bass.Bass("TRN2", target_bir_lowering=False)
        self.nc = nc
        self.ctx = ExitStack()
        dt = nc.dram_tensor
        self.xb = dt("xb", [S, D], F32, kind="ExternalInput").ap()
        self.xq = dt("xq", [TOWN, D], F32, kind="ExternalInput").ap()
        self.xqT = dt("xqT", [D, TOWN], F32, kind="ExternalInput").ap()
        self.qpos_i = dt("qpos", [128, NT], F32, kind="ExternalInput").ap()
        self.qkv = dt("qkv", [D, 3 * D], F32, kind="ExternalInput").ap()
        self.o_proj = dt("o_proj", [D, D], F32, kind="ExternalInput").ap()
        self.gammaT_i = dt("gammaT", [128, DC], F32, kind="ExternalInput").ap()
        self.betaT_i = dt("betaT", [128, DC], F32, kind="ExternalInput").ap()
        self.gate_wT = dt("gate_wT", [D, E], F32, kind="ExternalInput").ap()
        self.gate_b = dt("gate_b", [128, E], F32, kind="ExternalInput").ap()
        self.gate_bias = dt("gate_bias", [128, E], F32, kind="ExternalInput").ap()
        self.w1T = dt("w1T", [E * D, H], F32, kind="ExternalInput").ap()
        self.w2T = dt("w2T", [E * H, D], F32, kind="ExternalInput").ap()
        self.b1col = dt("b1col", [128, E * HC], F32, kind="ExternalInput").ap()
        self.b2 = dt("b2", [E, D], F32, kind="ExternalInput").ap()
        self.out = dt("out", [TOWN, D], F32, kind="ExternalOutput").ap()
        kind = "ExternalOutput" if debug else "Internal"
        self.xn2TS = dt("xn2TS", [128, DC, S], BF16, kind=kind).ap()
        self.xn2TownS = dt("xn2TownS", [128, DC, TOWN], BF16, kind=kind).ap()
        self.ktS = dt("ktS", [128, DC, S], BF16, kind=kind).ap()
        self.vS = dt("vS", [128, S // 128, D], BF16, kind=kind).ap()
        self.xnS = dt("xnS", [128, NT, D], BF16, kind=kind).ap()
        self.xnTS = dt("xnTS", [128, DC, TOWN], BF16, kind=kind).ap()
        self.qTS = dt("qTS", [128, DC, TOWN], BF16, kind=kind).ap()
        self.oS = dt("oS", [128, NT, D], BF16, kind=kind).ap()
        self.sS = dt("sS", [128, NT, D], BF16, kind=kind).ap()
        if debug:
            self.dbg_comb = dt("dbg_comb", [128, NT * E], F32, kind=kind).ap()
            self.dbg_acc = dt("dbg_acc", [128, 8 * D], F32, kind=kind).ap()
            self.dbg_raw = dt("dbg_raw", [128, NT * E], F32, kind=kind).ap()
            self.dbg_sc = dt("dbg_sc", [128, NT * E], F32, kind=kind).ap()
            self.dbg_gw = dt("dbg_gw", [128, DC * E], F32, kind=kind).ap()
            self.dbg_cons = dt("dbg_cons", [128, 2 * E], F32, kind=kind).ap()

        self.sems = {}
        self.ct = {}
        for name in ["ld", "gd", "pe", "dv", "ac"]:
            self.sems[name] = self.ctx.enter_context(nc.semaphore(name))
            self.ct[name] = 0
        self.lastw = {}

        sb = lambda n, shp, d=F32: self.ctx.enter_context(nc.sbuf_tensor(n, shp, d))
        self.ident = sb("ident", [128, 128], BF16)
        self.ones_row = sb("ones_row", [32, 512], BF16)
        self.ones128f = sb("ones128f", [128, 128], F32)
        self.ones_colb = sb("ones_colb", [128, 1], BF16)
        self.ones_colf = sb("ones_colf", [128, 1], F32)
        self.gammaT = sb("gammaT_s", [128, DC])
        self.betaT = sb("betaT_s", [128, DC])
        self.qpos = sb("qpos_s", [128, NT])
        self.gate_wT_sb = sb("gate_wT_sb", [128, DC * E])
        self.b1col_sb = sb("b1col_sb", [128, E * HC])
        self.gb_bc = sb("gb_bc", [128, E])
        self.gbias_bc = sb("gbias_bc", [128, E])
        self.rowsum_bc = sb("rowsum_bc", [128, E])
        self.gbb_bc = sb("gbb_bc", [128, E])
        self.comb = sb("comb", [128, NT * E])
        for n in ["sx", "sxx", "mu", "m2", "musq", "var", "a1", "l1", "nh", "r1",
                  "ia1", "var2", "a2", "l2", "nh2", "r2", "rr", "mrr", "mr1n"]:
            setattr(self, n, sb(n, [128, 4]))
        for n in ["g_t1", "g_sc", "g_eq1", "g_msk", "g_eq2", "g_c1"]:
            setattr(self, n, sb(n, [128, E]))
        self.dbg_raw_sb = sb("dbg_raw_sb", [128, NT * E])
        self.dbg_sc_sb = sb("dbg_sc_sb", [128, NT * E])
        for n in ["g_m1", "g_m2", "g_d12", "g_e", "g_den", "g_g1", "g_g2"]:
            setattr(self, n, sb(n, [128, 1]))

        self.g = nc.gpsimd
        self.v = nc.vector
        self.a = nc.scalar
        self.t = nc.tensor
        self.sy = nc.sync

    def inc(self, ins, sem, n=1):
        ins.then_inc(self.sems[sem], n)
        self.ct[sem] += n
        return (sem, self.ct[sem])

    def tok(self, sem):
        return (sem, self.ct[sem])

    def w(self, eng, engname, tok):
        if tok is None:
            return
        sem, val = tok
        if val <= 0:
            return
        key = (engname, sem)
        if self.lastw.get(key, 0) >= val:
            return
        eng.wait_ge(self.sems[sem], val)
        self.lastw[key] = val

    def dma(self, qeng, qname, sem, dst, src, toks=()):
        for t in toks:
            self.w(qeng, qname, t)
        ins = qeng.dma_start(dst, src)
        return self.inc(ins, sem, 16)

    def phase_start(self):
        toks = [self.tok(s) for s in ("pe", "dv", "ac", "ld", "gd")]
        for eng, name in ((self.t, "t"), (self.g, "g"), (self.sy, "s")):
            for tk in toks:
                if (name, tk[0]) not in (("t", "pe"), ("g", "gd"), ("s", "ld")):
                    self.w(eng, name, tk)

    def emit_stats(self, x4, x4_tok, want_mr1n):
        P, v, a = self, self.v, self.a
        t_sx = t_sxx = None
        for j in range(4):
            P.w(v, "v", x4_tok)
            t_sx = P.inc(v.reduce_sum(P.sx[:, j:j + 1], x4[:, j * D:(j + 1) * D], AX.X), "dv")
        for j in range(4):
            P.w(a, "a", x4_tok)
            t_sxx = P.inc(a.activation(P.sqdump[:], x4[:, j * D:(j + 1) * D], AF.Square,
                                       bias=0.0, scale=1.0, accum_out=P.sxx[:, j:j + 1]), "ac")
        P.w(v, "v", t_sx)
        tk = P.inc(v.tensor_scalar_mul(P.mu[:], P.sx[:], 1.0 / D), "dv")
        P.w(v, "v", t_sxx)
        P.w(v, "v", tk)
        tk = P.inc(v.tensor_scalar_mul(P.m2[:], P.sxx[:], 1.0 / D), "dv")
        P.w(v, "v", tk)
        tk = P.inc(v.tensor_tensor(P.musq[:], P.mu[:], P.mu[:], ALU.mult), "dv")
        P.w(v, "v", tk)
        tk = P.inc(v.tensor_tensor(P.var[:], P.m2[:], P.musq[:], ALU.subtract), "dv")
        P.w(v, "v", tk)
        t_a1 = P.inc(v.tensor_scalar_add(P.a1[:], P.var[:], EPS), "dv")
        P.w(a, "a", t_a1)
        tk = P.inc(a.activation(P.l1[:], P.a1[:], AF.Ln), "ac")
        P.w(v, "v", tk)
        tk = P.inc(v.tensor_scalar_mul(P.nh[:], P.l1[:], -0.5), "dv")
        P.w(a, "a", tk)
        t_r1 = P.inc(a.activation(P.r1[:], P.nh[:], AF.Exp), "ac")
        P.w(v, "v", t_a1)
        tk = P.inc(v.reciprocal(P.ia1[:], P.a1[:]), "dv")
        P.w(v, "v", tk)
        tk = P.inc(v.tensor_tensor(P.var2[:], P.var[:], P.ia1[:], ALU.mult), "dv")
        P.w(v, "v", tk)
        tk = P.inc(v.tensor_scalar_add(P.a2[:], P.var2[:], EPS), "dv")
        P.w(a, "a", tk)
        tk = P.inc(a.activation(P.l2[:], P.a2[:], AF.Ln), "ac")
        P.w(v, "v", tk)
        tk = P.inc(v.tensor_scalar_mul(P.nh2[:], P.l2[:], -0.5), "dv")
        P.w(a, "a", tk)
        tk = P.inc(a.activation(P.r2[:], P.nh2[:], AF.Exp), "ac")
        P.w(v, "v", tk)
        P.w(v, "v", t_r1)
        tk = P.inc(v.tensor_tensor(P.rr[:], P.r1[:], P.r2[:], ALU.mult), "dv")
        P.w(v, "v", tk)
        t_mrr = P.inc(v.scalar_tensor_tensor(P.mrr[:], P.mu[:], -1.0, P.rr[:], ALU.mult, ALU.mult), "dv")
        t_r1mu = None
        if want_mr1n:
            P.w(v, "v", t_mrr)
            t_r1mu = P.inc(v.scalar_tensor_tensor(P.mr1n[:], P.mu[:], -1.0, P.r1[:], ALU.mult, ALU.mult), "dv")
        return t_mrr, t_r1mu

    def emit_transposes(self, src_sb, src_tok, pT, evac_tok, dst, dst_dc_stride,
                        gamma, beta, n_tiles=4, free_tok=None):
        P, t, v = self, self.t, self.v
        last = None
        for dc in range(DC):
            bank = dc % 2
            P.w(t, "t", evac_tok[bank])
            P.w(t, "t", src_tok)
            if free_tok is not None:
                P.w(t, "t", free_tok)
            for j in range(n_tiles):
                ins = t.transpose(pT[bank][:, j * 128:(j + 1) * 128],
                                  src_sb[:, j * D + dc * 128: j * D + (dc + 1) * 128],
                                  P.ident[:])
            tk_pe = P.inc(ins, "pe")
            P.w(v, "v", tk_pe)
            dsl = dst[:, dc * dst_dc_stride: dc * dst_dc_stride + n_tiles * 128]
            if gamma is not None:
                ins = v.tensor_scalar(dsl, pT[bank][:, 0:n_tiles * 128],
                                      gamma[:, dc:dc + 1], beta[:, dc:dc + 1],
                                      ALU.mult, ALU.add)
            else:
                ins = v.tensor_copy(dsl, pT[bank][:, 0:n_tiles * 128])
            last = P.inc(ins, "dv")
            evac_tok[bank] = last
        return last

    def emit_proj_kc(self, w_sb, src, pX, evac_toks, dstbuf, dst_free_tok, ready_tok):
        P, t, a = self, self.t, self.a
        evac = evac_toks
        for kc in range(DC):
            bank = kc % 2
            P.w(t, "t", evac[bank])
            P.w(t, "t", ready_tok)
            for dc in range(DC):
                ins = t.matmul(pX[:, bank * 512:(bank + 1) * 512],
                               w_sb[:, dc * D + kc * 128: dc * D + (kc + 1) * 128],
                               src[:, dc * 512:(dc + 1) * 512],
                               start=(dc == 0), stop=(dc == DC - 1))
            tk_pe = P.inc(ins, "pe")
            P.w(a, "a", tk_pe)
            P.w(a, "a", dst_free_tok)
            ins = a.activation(dstbuf[:, kc * 512:(kc + 1) * 512],
                               pX[:, bank * 512:(bank + 1) * 512], AF.Copy,
                               bias=0.0, scale=1.0)
            evac[bank] = P.inc(ins, "ac")
        return evac[(DC - 1) % 2]

    def build(self):
        self.emit_prep()
        self.emit_a()
        self.emit_b()
        self.emit_c1()
        self.emit_c2()
        self.emit_d()
        self.emit_d2()
        self.emit_e_f()
        self.ctx.close()
        return self.nc

    def emit_prep(self):
        P = self
        g, v, a, t, sy = self.g, self.v, self.a, self.t, self.sy
        g.memset(P.ident[:], 0.0)
        g.affine_select(P.ident[:], P.ident[:], [[1, 128]], ALU.not_equal, 1.0,
                        base=0, channel_multiplier=-1)
        g.memset(P.ones_row[:], 0.0)
        g.memset(P.ones_row[0:1, :], 1.0)
        g.memset(P.ones128f[:], 1.0)
        g.memset(P.ones_colb[:], 1.0)
        tok_consts = P.inc(g.memset(P.ones_colf[:], 1.0), "gd")

        P.dma(sy, "s", "ld", P.gammaT[:], P.gammaT_i)
        P.dma(sy, "s", "ld", P.betaT[:], P.betaT_i)
        P.tok_qpos = P.dma(sy, "s", "ld", P.qpos[:], P.qpos_i)
        P.tok_b1col = P.dma(sy, "s", "ld", P.b1col_sb[:], P.b1col)
        tok_gw = P.dma(sy, "s", "ld",
                       P.gate_wT_sb[:].rearrange("p (dc e) -> p dc e", dc=DC),
                       P.gate_wT.rearrange("(dc p) e -> p dc e", p=128))
        tok_gb = P.dma(sy, "s", "ld", P.gb_bc[:], P.gate_b)
        tok_gbias = P.dma(sy, "s", "ld", P.gbias_bc[:], P.gate_bias)
        P.tok_gamma_ready = P.tok_qpos

        with self.nc.psum_tensor("pPrep", [128, 512], F32) as pP:
            P.w(t, "t", tok_gw)
            P.w(t, "t", tok_consts)
            for dc in range(DC):
                ins = t.matmul(pP[:, 0:E], P.ones128f[:], P.gate_wT_sb[:, dc * E:(dc + 1) * E],
                               start=(dc == 0), stop=(dc == DC - 1))
            tk_pe = P.inc(ins, "pe")
            P.w(v, "v", tk_pe)
            P.inc(v.tensor_copy(P.rowsum_bc[:], pP[:, 0:E]), "dv")
            P.w(v, "v", tok_gb)
            P.w(v, "v", tok_gbias)
            P.w(v, "v", P.tok("dv"))
            P.tok_gate_ready = P.inc(v.tensor_tensor(P.gbb_bc[:], P.gb_bc[:], P.gbias_bc[:], ALU.add), "dv")

    def emit_a(self):
        P = self
        nc = self.nc
        g, v, a, t, sy = self.g, self.v, self.a, self.t, self.sy
        with ExitStack() as phase_ctx:
            x4 = phase_ctx.enter_context(nc.sbuf_tensor("ax4", [128, 4 * D], F32))
            xnl2 = phase_ctx.enter_context(nc.sbuf_tensor("axnl2", [128, 4 * D], BF16))
            xn2T = phase_ctx.enter_context(nc.sbuf_tensor("axn2T", [128, DC * 512], BF16))
            ktbuf = phase_ctx.enter_context(nc.sbuf_tensor("aktbuf", [128, DC * 512], BF16))
            wk_sb = phase_ctx.enter_context(nc.sbuf_tensor("awk", [128, DC * D], BF16))
            self.sqdump = phase_ctx.enter_context(nc.sbuf_tensor("asq", [128, D], BF16))
            x4b = phase_ctx.enter_context(nc.sbuf_tensor("ax4b", [128, 4 * D], F32))
            pK = phase_ctx.enter_context(nc.psum_tensor("apK", [128, 2 * 512], F32))
            pT0 = phase_ctx.enter_context(nc.psum_tensor("apT0", [128, 512], BF16))
            pT1 = phase_ctx.enter_context(nc.psum_tensor("apT1", [128, 512], BF16))
            pT = [pT0, pT1]
            P.phase_start()
            tok_wk = P.dma(g, "g", "gd", wk_sb[:].rearrange("p (dc f) -> p dc f", dc=DC),
                           P.qkv[:, D:2 * D].rearrange("(dc p) f -> p dc f", p=128))
            x4s = [x4, x4b]
            tok_sp_xn2T = tok_sp_kt = None
            evac_tok = [None, None]
            evacK = [None, None]
            tok_x4 = [None] * (NSB + 1)
            xnl2_done = [None] * (NSB + 1)
            tok_x4[0] = P.dma(sy, "s", "ld",
                              x4s[0][:].rearrange("p (s d) -> p s d", s=4),
                              P.xb.rearrange("(s p) d -> p s d", p=128)[:, 0:4, :],
                              toks=(P.tok_gamma_ready,))

            def emit_ln(sb_i, war_tok):
                buf = x4s[sb_i % 2]
                t_mrr, _ = P.emit_stats(buf, tok_x4[sb_i], want_mr1n=False)
                last = None
                for j in range(4):
                    P.w(v, "v", t_mrr)
                    P.w(v, "v", war_tok)
                    ins = v.tensor_scalar(xnl2[:, j * D:(j + 1) * D], buf[:, j * D:(j + 1) * D],
                                          P.rr[:, j:j + 1], P.mrr[:, j:j + 1], ALU.mult, ALU.add)
                    last = P.inc(ins, "dv")
                xnl2_done[sb_i] = last

            emit_ln(0, None)
            for sb_i in range(NSB):
                t0 = sb_i * 512
                tok_xn2T_ready = P.emit_transposes(xnl2, xnl2_done[sb_i], pT, evac_tok, xn2T, 512,
                                                  P.gammaT, P.betaT, free_tok=tok_sp_xn2T)
                tok_tr_pe = P.tok("pe")
                if sb_i + 1 < NSB:
                    tok_x4[sb_i + 1] = P.dma(
                        sy, "s", "ld",
                        x4s[(sb_i + 1) % 2][:].rearrange("p (s d) -> p s d", s=4),
                        P.xb.rearrange("(s p) d -> p s d", p=128)[:, (sb_i + 1) * 4:(sb_i + 2) * 4, :],
                        toks=(xnl2_done[sb_i - 1] if sb_i >= 1 else None,))
                    emit_ln(sb_i + 1, tok_tr_pe)
                P.w(t, "t", tok_wk)
                tok_kt_ready = P.emit_proj_kc(wk_sb, xn2T, pK, evacK, ktbuf,
                                              tok_sp_kt, tok_xn2T_ready)
                tok_sp_xn2T = P.dma(sy, "s", "ld", P.xn2TS[:, :, t0:t0 + 512],
                                    xn2T[:].rearrange("p (dc t) -> p dc t", dc=DC),
                                    toks=(tok_xn2T_ready,))
                tok_sp_kt = P.dma(sy, "s", "ld", P.ktS[:, :, t0:t0 + 512],
                                  ktbuf[:].rearrange("p (kc t) -> p kc t", kc=DC),
                                  toks=(tok_kt_ready,))
            self.tok_a_done = tok_sp_kt

    def emit_b(self):
        P = self
        nc = self.nc
        g, v, a, t, sy = self.g, self.v, self.a, self.t, self.sy
        with ExitStack() as phase_ctx:
            xt0 = phase_ctx.enter_context(nc.sbuf_tensor("bxt0", [128, DC * 512], BF16))
            xt1 = phase_ctx.enter_context(nc.sbuf_tensor("bxt1", [128, DC * 512], BF16))
            vbuf = phase_ctx.enter_context(nc.sbuf_tensor("bvbuf", [128, 4 * D], BF16))
            wv_sb = phase_ctx.enter_context(nc.sbuf_tensor("bwv", [128, DC * D], BF16))
            pV = phase_ctx.enter_context(nc.psum_tensor("bpV", [128, 2 * 512], F32))
            xts = [xt0, xt1]
            P.phase_start()
            tok_wv = P.dma(g, "g", "gd", wv_sb[:].rearrange("p (dc f) -> p dc f", dc=DC),
                           P.qkv[:, 2 * D:3 * D].rearrange("(dc p) f -> p dc f", p=128))
            tok_xt_free = [None, None]
            tok_sp_v = None
            evacV = [None, None]
            for sb_i in range(NSB):
                t0 = sb_i * 512
                bk = sb_i % 2
                tok_xt = P.dma(sy, "s", "ld", xts[bk][:].rearrange("p (dc t) -> p dc t", dc=DC),
                               P.xn2TS[:, :, t0:t0 + 512],
                               toks=(tok_xt_free[bk], self.tok_a_done))
                P.w(t, "t", tok_wv)
                nvb = 0
                tok_v_ready = None
                for j in range(4):
                    for dvc in range(4):
                        bank = nvb % 2
                        nvb += 1
                        P.w(t, "t", evacV[bank])
                        P.w(t, "t", tok_xt)
                        for dc in range(DC):
                            ins = t.matmul(pV[:, bank * 512:(bank + 1) * 512],
                                           xts[bk][:, dc * 512 + j * 128: dc * 512 + (j + 1) * 128],
                                           wv_sb[:, dc * D + dvc * 512: dc * D + (dvc + 1) * 512],
                                           start=(dc == 0), stop=(dc == DC - 1))
                        tk_pe = P.inc(ins, "pe")
                        P.w(a, "a", tk_pe)
                        P.w(a, "a", tok_sp_v)
                        ins = a.activation(vbuf[:, j * D + dvc * 512: j * D + (dvc + 1) * 512],
                                           pV[:, bank * 512:(bank + 1) * 512], AF.Copy,
                                           bias=0.0, scale=1.0)
                        evacV[bank] = P.inc(ins, "ac")
                        tok_v_ready = evacV[bank]
                tok_xt_free[bk] = P.tok("pe")
                tok_sp_v = P.dma(sy, "s", "ld", P.vS[:, sb_i * 4:(sb_i + 1) * 4, :],
                                 vbuf[:].rearrange("p (s d) -> p s d", s=4),
                                 toks=(tok_v_ready,))
            self.tok_b_done = tok_sp_v

    def emit_c1(self):
        P = self
        nc = self.nc
        g, v, a, t, sy = self.g, self.v, self.a, self.t, self.sy
        with ExitStack() as phase_ctx:
            x4 = phase_ctx.enter_context(nc.sbuf_tensor("cx4", [128, 4 * D], F32))
            xqT4 = phase_ctx.enter_context(nc.sbuf_tensor("cxqT4", [128, DC * 512], F32))
            xnl2 = phase_ctx.enter_context(nc.sbuf_tensor("cxnl2", [128, 4 * D], BF16))
            xn2T = phase_ctx.enter_context(nc.sbuf_tensor("cxn2T", [128, DC * 512], BF16))
            xn_b = phase_ctx.enter_context(nc.sbuf_tensor("cxn", [128, 4 * D], BF16))
            xnT = phase_ctx.enter_context(nc.sbuf_tensor("cxnT", [128, DC * 512], BF16))
            self.sqdump = phase_ctx.enter_context(nc.sbuf_tensor("csq", [128, D], BF16))
            pRaw = phase_ctx.enter_context(nc.psum_tensor("cpRaw", [128, 512], F32))
            pT0 = phase_ctx.enter_context(nc.psum_tensor("cpT0", [128, 512], BF16))
            pT1 = phase_ctx.enter_context(nc.psum_tensor("cpT1", [128, 512], BF16))
            pT = [pT0, pT1]
            P.phase_start()
            tok_x4_free = None
            tok_sp_xn = tok_sp_xnT = tok_sp_x2 = None
            tok_raw_read = None
            evac_tok = [None, None]
            for c in range(4):
                t0 = c * 512
                tok_x4 = P.dma(sy, "s", "ld",
                               x4[:].rearrange("p (s d) -> p s d", s=4),
                               P.xq.rearrange("(s p) d -> p s d", p=128)[:, c * 4:(c + 1) * 4, :],
                               toks=(tok_x4_free,))
                tok_xqT = P.dma(sy, "s", "ld",
                                xqT4[:].rearrange("p (dc t) -> p dc t", dc=DC),
                                P.xqT.rearrange("(dc p) t -> p dc t", p=128)[:, :, c * 512:(c + 1) * 512],
                                toks=(tok_raw_read,))
                t_mrr, t_r1mu = P.emit_stats(x4, tok_x4, want_mr1n=True)
                for j in range(4):
                    ti = c * 4 + j
                    P.w(t, "t", tok_xqT)
                    P.w(t, "t", P.tok_gate_ready)
                    P.w(t, "t", tok_raw_read)
                    for dc in range(DC):
                        ins = t.matmul(pRaw[:, 0:E],
                                       xqT4[:, dc * 512 + j * 128: dc * 512 + (j + 1) * 128],
                                       P.gate_wT_sb[:, dc * E:(dc + 1) * E],
                                       start=(dc == 0), stop=(dc == DC - 1))
                    tk_raw = P.inc(ins, "pe")
                    P.w(v, "v", tk_raw)
                    P.w(v, "v", t_r1mu)
                    if P.debug:
                        P.inc(v.tensor_copy(P.dbg_raw_sb[:, ti * E:(ti + 1) * E], pRaw[:, 0:E]), "dv")
                    tk = P.inc(v.scalar_tensor_tensor(P.g_t1[:], pRaw[:, 0:E], P.r1[:, j:j + 1],
                                                      P.gbb_bc[:], ALU.mult, ALU.add), "dv")
                    tok_raw_read = tk
                    P.w(v, "v", tk)
                    tk = P.inc(v.scalar_tensor_tensor(P.g_sc[:], P.rowsum_bc[:], P.mr1n[:, j:j + 1],
                                                      P.g_t1[:], ALU.mult, ALU.add), "dv")
                    P.w(v, "v", tk)
                    if P.debug:
                        P.inc(v.tensor_copy(P.dbg_sc_sb[:, ti * E:(ti + 1) * E], P.g_sc[:]), "dv")
                        P.w(v, "v", P.tok("dv"))
                    tk = P.inc(v.reduce_max(P.g_m1[:], P.g_sc[:], AX.X), "dv")
                    P.w(v, "v", tk)
                    tk = P.inc(v.tensor_scalar(P.g_eq1[:], P.g_sc[:], P.g_m1[:, 0:1], None, ALU.is_ge), "dv")
                    P.w(v, "v", tk)
                    tk = P.inc(v.scalar_tensor_tensor(P.g_msk[:], P.g_eq1[:], -1e30, P.g_sc[:],
                                                      ALU.mult, ALU.add), "dv")
                    P.w(v, "v", tk)
                    tk = P.inc(v.reduce_max(P.g_m2[:], P.g_msk[:], AX.X), "dv")
                    P.w(v, "v", tk)
                    tk = P.inc(v.tensor_scalar(P.g_eq2[:], P.g_msk[:], P.g_m2[:, 0:1], None, ALU.is_ge), "dv")
                    P.w(v, "v", tk)
                    tk = P.inc(v.tensor_tensor(P.g_d12[:], P.g_m1[:], P.g_m2[:], ALU.subtract), "dv")
                    P.w(a, "a", tk)
                    tk = P.inc(a.activation(P.g_e[:], P.g_d12[:], AF.Exp, bias=0.0, scale=-1.0), "ac")
                    P.w(v, "v", tk)
                    tk = P.inc(v.tensor_scalar_add(P.g_den[:], P.g_e[:], 1.0), "dv")
                    P.w(v, "v", tk)
                    tk = P.inc(v.reciprocal(P.g_g1[:], P.g_den[:]), "dv")
                    P.w(v, "v", tk)
                    tk = P.inc(v.tensor_scalar(P.g_g2[:], P.g_g1[:], -1.0, 1.0, ALU.mult, ALU.add), "dv")
                    P.w(v, "v", tk)
                    tk = P.inc(v.tensor_scalar_mul(P.g_c1[:], P.g_eq1[:], P.g_g1[:, 0:1]), "dv")
                    P.w(v, "v", tk)
                    P.inc(v.scalar_tensor_tensor(P.comb[:, ti * E:(ti + 1) * E], P.g_eq2[:],
                                                 P.g_g2[:, 0:1], P.g_c1[:], ALU.mult, ALU.add), "dv")
                last_xn = last_xnl2 = None
                for j in range(4):
                    P.w(v, "v", t_mrr)
                    P.w(v, "v", t_r1mu)
                    ins = v.tensor_scalar(xn_b[:, j * D:(j + 1) * D], x4[:, j * D:(j + 1) * D],
                                          P.r1[:, j:j + 1], P.mr1n[:, j:j + 1], ALU.mult, ALU.add)
                    last_xn = P.inc(ins, "dv")
                    ins = v.tensor_scalar(xnl2[:, j * D:(j + 1) * D], x4[:, j * D:(j + 1) * D],
                                          P.rr[:, j:j + 1], P.mrr[:, j:j + 1], ALU.mult, ALU.add)
                    last_xnl2 = P.inc(ins, "dv")
                tok_x4_free = last_xnl2
                tok_xnT_ready = P.emit_transposes(xn_b, last_xn, pT, evac_tok, xnT, 512,
                                                 None, None, free_tok=tok_sp_xnT)
                tok_x2_ready = P.emit_transposes(xnl2, last_xnl2, pT, evac_tok, xn2T, 512,
                                                 P.gammaT, P.betaT, free_tok=tok_sp_x2)
                tok_sp_xn = P.dma(sy, "s", "ld", P.xnS[:, c * 4:(c + 1) * 4, :],
                                  xn_b[:].rearrange("p (s d) -> p s d", s=4),
                                  toks=(last_xn,))
                tok_sp_xnT = P.dma(sy, "s", "ld", P.xnTS[:, :, t0:t0 + 512],
                                   xnT[:].rearrange("p (dc t) -> p dc t", dc=DC),
                                   toks=(tok_xnT_ready,))
                tok_sp_x2 = P.dma(sy, "s", "ld", P.xn2TownS[:, :, t0:t0 + 512],
                                  xn2T[:].rearrange("p (dc t) -> p dc t", dc=DC),
                                  toks=(tok_x2_ready,))
            self.tok_c1_done = tok_sp_x2
            if P.debug:
                P.w(g, "g", P.tok("dv"))
                P.inc(g.dma_start(P.dbg_comb, P.comb[:]), "gd", 16)
                P.inc(g.dma_start(P.dbg_raw, P.dbg_raw_sb[:]), "gd", 16)
                P.inc(g.dma_start(P.dbg_sc, P.dbg_sc_sb[:]), "gd", 16)
                P.inc(g.dma_start(P.dbg_gw, P.gate_wT_sb[:]), "gd", 16)
                P.inc(g.dma_start(P.dbg_cons[:, 0:E], P.rowsum_bc[:]), "gd", 16)
                P.inc(g.dma_start(P.dbg_cons[:, E:2 * E], P.gbb_bc[:]), "gd", 16)

    def emit_c2(self):
        P = self
        nc = self.nc
        g, v, a, t, sy = self.g, self.v, self.a, self.t, self.sy
        with ExitStack() as phase_ctx:
            xt0 = phase_ctx.enter_context(nc.sbuf_tensor("qxt0", [128, DC * 512], BF16))
            xt1 = phase_ctx.enter_context(nc.sbuf_tensor("qxt1", [128, DC * 512], BF16))
            qtbuf = phase_ctx.enter_context(nc.sbuf_tensor("qqt", [128, DC * 512], BF16))
            wq_sb = phase_ctx.enter_context(nc.sbuf_tensor("qwq", [128, DC * D], BF16))
            pQ = phase_ctx.enter_context(nc.psum_tensor("qpQ", [128, 2 * 512], F32))
            xts = [xt0, xt1]
            P.phase_start()
            tok_wq = P.dma(g, "g", "gd", wq_sb[:].rearrange("p (dc f) -> p dc f", dc=DC),
                           P.qkv[:, 0:D].rearrange("(dc p) f -> p dc f", p=128))
            tok_xt_free = [None, None]
            tok_sp_qt = None
            evacQ = [None, None]
            for c in range(4):
                t0 = c * 512
                bk = c % 2
                tok_xt = P.dma(sy, "s", "ld", xts[bk][:].rearrange("p (dc t) -> p dc t", dc=DC),
                               P.xn2TownS[:, :, t0:t0 + 512],
                               toks=(tok_xt_free[bk], self.tok_c1_done))
                P.w(t, "t", tok_wq)
                tok_qt_ready = P.emit_proj_kc(wq_sb, xts[bk], pQ, evacQ, qtbuf,
                                              tok_sp_qt, tok_xt)
                tok_xt_free[bk] = P.tok("pe")
                tok_sp_qt = P.dma(sy, "s", "ld", P.qTS[:, :, t0:t0 + 512],
                                  qtbuf[:].rearrange("p (kc t) -> p kc t", kc=DC),
                                  toks=(tok_qt_ready,))
            self.tok_c2_done = tok_sp_qt

    def emit_d(self):
        P = self
        nc = self.nc
        g, v, a, t, sy = self.g, self.v, self.a, self.t, self.sy
        with ExitStack() as phase_ctx:
            arena = phase_ctx.enter_context(nc.sbuf_tensor("dar", [128, 4 * S], BF16))
            Pb = phase_ctx.enter_context(nc.sbuf_tensor("dP", [128, S], BF16))
            mask = phase_ctx.enter_context(nc.sbuf_tensor("dmask", [128, S], BF16))
            kvidx = phase_ctx.enter_context(nc.sbuf_tensor("dkvidx", [128, S], I16))
            kt0 = phase_ctx.enter_context(nc.sbuf_tensor("dkt0", [128, DC * 512], BF16))
            kt1 = phase_ctx.enter_context(nc.sbuf_tensor("dkt1", [128, DC * 512], BF16))
            vb0 = phase_ctx.enter_context(nc.sbuf_tensor("dvb0", [128, 4 * 1024], BF16))
            vb1 = phase_ctx.enter_context(nc.sbuf_tensor("dvb1", [128, 4 * 1024], BF16))
            qt4 = phase_ctx.enter_context(nc.sbuf_tensor("dqt4", [128, DC * 512], BF16))
            ob4 = phase_ctx.enter_context(nc.sbuf_tensor("dob4", [128, 4 * D], BF16))
            bm4 = phase_ctx.enter_context(nc.sbuf_tensor("dbm4", [128, 64], F32))
            negm4 = phase_ctx.enter_context(nc.sbuf_tensor("dnegm4", [128, 4], F32))
            negms4 = phase_ctx.enter_context(nc.sbuf_tensor("dnegms4", [128, 4], F32))
            l4 = phase_ctx.enter_context(nc.sbuf_tensor("dl4", [32, 512], F32))
            lcol4 = phase_ctx.enter_context(nc.sbuf_tensor("dlcol4", [128, 4], F32))
            linv4 = phase_ctx.enter_context(nc.sbuf_tensor("dlinv4", [128, 4], F32))
            kts = [kt0, kt1]
            vbs = [vb0, vb1]
            P.phase_start()
            g.memset(l4[:], 0.0)
            tok_kvidx = P.inc(g.iota(kvidx[:], [[1, S]], base=0, channel_multiplier=0,
                                     allow_small_or_imprecise_dtypes=True), "gd")
            tok_kt_free = [None, None]
            tok_vb_free = [None, None]
            tok_qt_free = None
            tok_arena_free = None
            tok_psum_free = None
            tok_mask_free = None
            tok_P_free = None
            tok_ob_spill = None
            tok_sp_ob = None
            for gi in range(4):
                L = 2048 * (gi + 1)
                nblk = L // 512
                nch = L // 128
                tok_qt = P.dma(sy, "s", "ld", qt4[:].rearrange("p (kc t) -> p kc t", kc=DC),
                               P.qTS[:, :, gi * 512:(gi + 1) * 512],
                               toks=(tok_qt_free, self.tok_c2_done))
                with nc.psum_tensor(f"dpS{gi}", [128, 2 * 512], F32) as pS:
                    tok_sc_evac = {}
                    nb_mm = 0
                    for b in range(nblk):
                        bk = b % 2
                        tok_kt = P.dma(sy, "s", "ld",
                                       kts[bk][:].rearrange("p (kc t) -> p kc t", kc=DC),
                                       P.ktS[:, :, b * 512:(b + 1) * 512],
                                       toks=(tok_kt_free[bk], self.tok_b_done))
                        for tl in range(4):
                            bank = nb_mm % 2
                            nb_mm += 1
                            P.w(t, "t", tok_kt)
                            P.w(t, "t", tok_qt)
                            P.w(t, "t", tok_sc_evac.get(nb_mm - 3))
                            P.w(t, "t", tok_psum_free)
                            P.w(t, "t", tok_arena_free)
                            for kc in range(DC):
                                ins = t.matmul(pS[:, bank * 512:(bank + 1) * 512],
                                               qt4[:, kc * 512 + tl * 128: kc * 512 + (tl + 1) * 128],
                                               kts[bk][:, kc * 512:(kc + 1) * 512],
                                               start=(kc == 0), stop=(kc == DC - 1))
                            tk_pe = P.inc(ins, "pe")
                            if tl == 3:
                                tok_kt_free[bk] = tk_pe
                            P.w(a, "a", tk_pe)
                            P.w(a, "a", tok_arena_free)
                            P.w(a, "a", tok_P_free)
                            ins = a.activation(arena[:, tl * S + b * 512: tl * S + (b + 1) * 512],
                                               pS[:, bank * 512:(bank + 1) * 512], AF.Copy,
                                               bias=0.0, scale=1.0)
                            tok_sc_evac[nb_mm - 1] = P.inc(ins, "ac")
                            P.w(v, "v", tok_sc_evac[nb_mm - 1])
                            P.inc(v.reduce_max(bm4[:, tl * 16 + b: tl * 16 + b + 1],
                                               arena[:, tl * S + b * 512: tl * S + (b + 1) * 512],
                                               AX.X), "dv")
                    P.w(v, "v", P.tok("dv"))
                    for tl in range(4):
                        P.inc(v.reduce_max(negm4[:, tl:tl + 1], bm4[:, tl * 16: tl * 16 + nblk],
                                           AX.X, negate=True), "dv")
                    P.w(v, "v", P.tok("dv"))
                    tok_negms = P.inc(v.tensor_scalar_mul(negms4[:], negm4[:], SCALE), "dv")
                    tok_pass1_sc = P.tok("ac")
                with nc.psum_tensor(f"dpT{gi}", [128, 512], BF16) as pTr, \
                     nc.psum_tensor(f"dpL{gi}", [128, 512], F32) as pL:
                    tok_tr_evac = None
                    tok_l_evac = None
                    tok_ptb_chunk = {}
                    for tl in range(4):
                        ti = gi * 4 + tl
                        P.w(v, "v", tok_kvidx)
                        P.w(v, "v", tok_mask_free)
                        P.w(v, "v", P.tok_qpos)
                        tok_mask = P.inc(v.tensor_scalar(mask[:, 0:L], kvidx[:, 0:L],
                                                         P.qpos[:, ti:ti + 1], None, ALU.is_le), "dv")
                        P.w(a, "a", tok_negms)
                        P.w(a, "a", tok_P_free)
                        tok_exp = P.inc(a.activation(Pb[:, 0:L], arena[:, tl * S: tl * S + L], AF.Exp,
                                                     bias=negms4[:, tl:tl + 1], scale=SCALE), "ac")
                        P.w(v, "v", tok_exp)
                        P.w(v, "v", tok_mask)
                        tok_Pm = P.inc(v.tensor_tensor(Pb[:, 0:L], Pb[:, 0:L], mask[:, 0:L], ALU.mult), "dv")
                        tok_mask_free = tok_Pm
                        for g4 in range(nch // 4):
                            P.w(t, "t", tok_tr_evac)
                            P.w(t, "t", tok_Pm)
                            if tl == 0 and g4 == 0:
                                P.w(t, "t", tok_pass1_sc)
                            for j in range(4):
                                k = g4 * 4 + j
                                ins = t.transpose(pTr[:, j * 128:(j + 1) * 128],
                                                  Pb[:, k * 128:(k + 1) * 128], P.ident[:])
                            tk_pe = P.inc(ins, "pe")
                            P.w(v, "v", tk_pe)
                            ins = v.tensor_copy(arena[:, tl * S + g4 * 512: tl * S + (g4 + 1) * 512],
                                                pTr[:, 0:512])
                            tok_tr_evac = P.inc(ins, "dv")
                            for j in range(4):
                                tok_ptb_chunk[(tl, g4 * 4 + j)] = tok_tr_evac
                        tok_P_free = P.tok("pe")
                        for k in range(nch):
                            P.w(t, "t", tok_ptb_chunk[(tl, k)])
                            if tl == 0 and k == 0:
                                P.w(t, "t", tok_l_evac)
                            ins = t.matmul(pL[0:1, tl * 128:(tl + 1) * 128], P.ones_colb[:],
                                           arena[:, tl * S + k * 128: tl * S + (k + 1) * 128],
                                           start=(k == 0), stop=(k == nch - 1))
                        tk_pe = P.inc(ins, "pe")
                        P.w(v, "v", tk_pe)
                        tok_l_evac = P.inc(v.tensor_copy(l4[0:1, tl * 128:(tl + 1) * 128],
                                                         pL[0:1, tl * 128:(tl + 1) * 128]), "dv")
                    for tl in range(4):
                        P.w(t, "t", tok_l_evac)
                        ins = t.matmul(pL[:, 256 + 2 * tl: 258 + 2 * tl], l4[0:32, tl * 128:(tl + 1) * 128],
                                       P.ones128f[0:32, 0:2], start=True, stop=True)
                    tk_pe = P.inc(ins, "pe")
                    P.w(v, "v", tk_pe)
                    for tl in range(4):
                        P.inc(v.tensor_copy(lcol4[:, tl:tl + 1], pL[:, 256 + 2 * tl: 257 + 2 * tl]), "dv")
                    P.w(v, "v", P.tok("dv"))
                    tok_linv = P.inc(v.reciprocal(linv4[:], lcol4[:]), "dv")
                    tok_pass2_pe = P.tok("pe")
                with nc.psum_tensor(f"dpO{gi}", [128, 8 * 512], F32) as pO:
                    tok_ob_evac = None
                    for dvh in range(2):
                        last_pv = None
                        for b in range(nblk):
                            bk = b % 2
                            tok_vb = P.dma(sy, "s", "ld",
                                           vbs[bk][:].rearrange("p (s d) -> p s d", s=4),
                                           P.vS[:, b * 4:(b + 1) * 4, dvh * 1024:(dvh + 1) * 1024],
                                           toks=(tok_vb_free[bk],))
                            P.w(t, "t", tok_vb)
                            if dvh == 0 and b == 0:
                                P.w(t, "t", tok_linv)
                                P.w(t, "t", tok_ob_spill)
                            if dvh == 1 and b == 0:
                                P.w(t, "t", tok_ob_evac)
                            for s4 in range(4):
                                k = b * 4 + s4
                                for tl in range(4):
                                    for dvc in range(2):
                                        ins = t.matmul(
                                            pO[:, (tl * 2 + dvc) * 512:(tl * 2 + dvc + 1) * 512],
                                            arena[:, tl * S + k * 128: tl * S + (k + 1) * 128],
                                            vbs[bk][:, s4 * 1024 + dvc * 512: s4 * 1024 + (dvc + 1) * 512],
                                            start=(k == 0), stop=(k == nch - 1))
                            last_pv = P.inc(ins, "pe")
                            tok_vb_free[bk] = last_pv
                        for tl in range(4):
                            P.w(a, "a", last_pv)
                            P.w(a, "a", tok_linv)
                            P.w(a, "a", tok_sp_ob)
                            ins = a.activation(ob4[:, tl * D + dvh * 1024: tl * D + (dvh + 1) * 1024],
                                               pO[:, tl * 1024: tl * 1024 + 1024], AF.Copy,
                                               bias=0.0, scale=linv4[:, tl:tl + 1])
                            tok_ob_evac = P.inc(ins, "ac")
                    tok_arena_free = P.tok("pe")
                    tok_psum_free = tok_ob_evac
                tok_sp_ob = P.dma(sy, "s", "ld", P.oS[:, gi * 4:(gi + 1) * 4, :],
                                  ob4[:].rearrange("p (s d) -> p s d", s=4),
                                  toks=(tok_ob_evac,))
                tok_ob_spill = tok_sp_ob
                tok_qt_free = P.tok("pe")
            self.tok_d_done = tok_sp_ob

    def emit_d2(self):
        P = self
        nc = self.nc
        g, v, a, t, sy = self.g, self.v, self.a, self.t, self.sy
        with ExitStack() as phase_ctx:
            wo_sb = phase_ctx.enter_context(nc.sbuf_tensor("ewo", [128, DC * D], BF16))
            ob = phase_ctx.enter_context(nc.sbuf_tensor("eob", [128, D], BF16))
            otb = phase_ctx.enter_context(nc.sbuf_tensor("eotb", [128, DC * 128], BF16))
            xnt = phase_ctx.enter_context(nc.sbuf_tensor("exn", [128, D], BF16))
            s_bf = phase_ctx.enter_context(nc.sbuf_tensor("esbf", [128, D], BF16))
            pS = phase_ctx.enter_context(nc.psum_tensor("epS2", [128, 2 * 512], F32))
            pTr = phase_ctx.enter_context(nc.psum_tensor("epT2", [128, 512], BF16))
            P.phase_start()
            tok_wo = P.dma(g, "g", "gd", wo_sb[:].rearrange("p (dc f) -> p dc f", dc=DC),
                           P.o_proj.rearrange("(dc p) f -> p dc f", p=128))
            tok_ob_free = None
            tok_xn_free = None
            tok_sbf_free = None
            tok_otb_free = None
            tok_pTr_free = None
            tok_sp_s = None
            tok_pS_free = [None, None]
            for qi in range(NT):
                tok_ob = P.dma(sy, "s", "ld", ob[:].rearrange("p (o d) -> p o d", o=1),
                               P.oS[:, qi:qi + 1, :], toks=(tok_ob_free, self.tok_d_done))
                tok_xn = P.dma(sy, "s", "ld", xnt[:].rearrange("p (o d) -> p o d", o=1),
                               P.xnS[:, qi:qi + 1, :], toks=(tok_xn_free,))
                tok_tr2 = tok_pTr_free
                P.w(t, "t", tok_otb_free)
                for g4 in range(4):
                    P.w(t, "t", tok_tr2)
                    P.w(t, "t", tok_ob)
                    for j in range(4):
                        dvc = g4 * 4 + j
                        ins = t.transpose(pTr[:, j * 128:(j + 1) * 128],
                                          ob[:, dvc * 128:(dvc + 1) * 128], P.ident[:])
                    tk_pe = P.inc(ins, "pe")
                    P.w(v, "v", tk_pe)
                    ins = v.tensor_copy(otb[:, g4 * 512:(g4 + 1) * 512], pTr[:, 0:512])
                    tok_tr2 = P.inc(ins, "dv")
                tok_pTr_free = tok_tr2
                P.w(t, "t", tok_wo)
                tok_s_parts = []
                for fc in range(4):
                    bank = fc % 2
                    P.w(t, "t", tok_s_parts[-2] if fc >= 2 else tok_pS_free[bank])
                    P.w(t, "t", tok_tr2)
                    for dvc in range(DC):
                        ins = t.matmul(pS[:, bank * 512:(bank + 1) * 512],
                                       otb[:, dvc * 128:(dvc + 1) * 128],
                                       wo_sb[:, dvc * D + fc * 512: dvc * D + (fc + 1) * 512],
                                       start=(dvc == 0), stop=(dvc == DC - 1))
                    tk_pe = P.inc(ins, "pe")
                    P.w(v, "v", tk_pe)
                    P.w(v, "v", tok_xn)
                    P.w(v, "v", tok_sbf_free)
                    ins = v.scalar_tensor_tensor(s_bf[:, fc * 512:(fc + 1) * 512],
                                                 pS[:, bank * 512:(bank + 1) * 512], 1.0,
                                                 xnt[:, fc * 512:(fc + 1) * 512],
                                                 ALU.mult, ALU.add)
                    tok_s_parts.append(P.inc(ins, "dv"))
                tok_pS_free = [tok_s_parts[2], tok_s_parts[3]]
                tok_sp_s = P.dma(sy, "s", "ld", P.sS[:, qi:qi + 1, :],
                                 s_bf[:].rearrange("p (o d) -> p o d", o=1),
                                 toks=(tok_s_parts[-1],))
                tok_sbf_free = tok_sp_s
                tok_ob_free = P.tok("pe")
                tok_xn_free = tok_s_parts[-1]
                tok_otb_free = P.tok("pe")
            self.tok_d2_done = tok_sp_s

    def emit_e_f(self):
        P = self
        nc = self.nc
        g, v, a, t, sy = self.g, self.v, self.a, self.t, self.sy
        with ExitStack() as phase_ctx:
            acc = phase_ctx.enter_context(nc.sbuf_tensor("eacc", [128, 8 * D], BF16))
            w1_sb = phase_ctx.enter_context(nc.sbuf_tensor("ew1", [128, DC * H], BF16))
            w2_sb = phase_ctx.enter_context(nc.sbuf_tensor("ew2", [128, HC * D], BF16))
            hT = phase_ctx.enter_context(nc.sbuf_tensor("ehT", [128, HC * 1024], BF16))
            xnTh = phase_ctx.enter_context(nc.sbuf_tensor("exnTh", [128, DC * 1024], BF16))
            b2e = phase_ctx.enter_context(nc.sbuf_tensor("eb2", [32, D], BF16))
            fs = phase_ctx.enter_context(nc.sbuf_tensor("fs", [128, D], BF16))
            fo = phase_ctx.enter_context(nc.sbuf_tensor("fo", [128, D], F32))
            pH = phase_ctx.enter_context(nc.psum_tensor("epH", [128, 2 * 512], F32))
            pY = phase_ctx.enter_context(nc.psum_tensor("epY", [128, 4 * 512], F32))
            tok_w_free = None
            P.phase_start()
            tok_hT_free = None
            tok_xnT_free = None
            evacH = [None, None]
            tok_pY = {}
            P.inc(g.memset(b2e[:], 0.0), "gd")
            tok_fs_free = None
            tok_fo_free = None
            last_store = None
            tok_acc_free = None
            tok_xcopy = P.dma(sy, "s", "ld", P.out, P.xq)
            tok_w1 = tok_w2 = tok_b1 = tok_b2 = None
            tok_w1_free = tok_w2_free = None
            first = True
            for h in range(2):
                tok_acc = {}
                tok_xnT = P.dma(sy, "s", "ld",
                                xnTh[:].rearrange("p (dc t) -> p dc t", dc=DC),
                                P.xnTS[:, :, h * 1024:(h + 1) * 1024],
                                toks=(tok_xnT_free, self.tok_c1_done))
                for e in range(E):
                    if first:
                        tok_w1 = P.dma(g, "g", "gd", w1_sb[:].rearrange("p (dc h) -> p dc h", dc=DC),
                                       P.w1T[e * D:(e + 1) * D, :].rearrange("(dc p) h -> p dc h", p=128))
                        tok_w2 = P.dma(g, "g", "gd", w2_sb[:].rearrange("p (hc d) -> p hc d", hc=HC),
                                       P.w2T[e * H:(e + 1) * H, :].rearrange("(hc p) d -> p hc d", p=128))
                        tok_b2 = P.dma(g, "g", "gd", b2e[0:1, :], P.b2[e:e + 1, :])
                        first = False
                    for s in range(2):
                        for hc in range(HC):
                            bank = hc % 2
                            P.w(t, "t", tok_w1)
                            P.w(t, "t", tok_xnT)
                            P.w(t, "t", tok_hT_free)
                            P.w(t, "t", evacH[bank])
                            for dc in range(DC):
                                ins = t.matmul(pH[:, bank * 512:(bank + 1) * 512],
                                               w1_sb[:, dc * H + hc * 128: dc * H + (hc + 1) * 128],
                                               xnTh[:, dc * 1024 + s * 512: dc * 1024 + (s + 1) * 512],
                                               start=(dc == 0), stop=(dc == DC - 1))
                            tk_pe = P.inc(ins, "pe")
                            P.w(a, "a", tk_pe)
                            P.w(a, "a", P.tok_b1col)
                            ins = a.activation(hT[:, hc * 1024 + s * 512: hc * 1024 + (s + 1) * 512],
                                               pH[:, bank * 512:(bank + 1) * 512], AF.Silu,
                                               bias=P.b1col_sb[:, e * HC + hc: e * HC + hc + 1],
                                               scale=1.0)
                            evacH[bank] = P.inc(ins, "ac")
                    tok_hT_ready = P.tok("ac")
                    ne_h, ne_e = (h, e + 1) if e + 1 < E else (h + 1, 0)
                    if ne_h < 2:
                        tok_w1_free = P.tok("pe")
                        tok_w1 = P.dma(g, "g", "gd", w1_sb[:].rearrange("p (dc h) -> p dc h", dc=DC),
                                       P.w1T[ne_e * D:(ne_e + 1) * D, :].rearrange("(dc p) h -> p dc h", p=128),
                                       toks=(tok_w1_free,))
                    for tloc in range(8):
                        ti = h * 8 + tloc
                        for dc2 in range(4):
                            P.w(t, "t", tok_w2)
                            P.w(t, "t", tok_b2)
                            P.w(t, "t", tok_hT_ready)
                            P.w(t, "t", tok_pY.get(dc2))
                            for hc in range(HC):
                                t.matmul(pY[:, dc2 * 512:(dc2 + 1) * 512],
                                         hT[:, hc * 1024 + tloc * 128: hc * 1024 + (tloc + 1) * 128],
                                         w2_sb[:, hc * D + dc2 * 512: hc * D + (dc2 + 1) * 512],
                                         start=(hc == 0), stop=False)
                            ins = t.matmul(pY[:, dc2 * 512:(dc2 + 1) * 512],
                                           P.ones_row[0:32, 0:128],
                                           b2e[0:32, dc2 * 512:(dc2 + 1) * 512],
                                           start=False, stop=True)
                            tk_pe = P.inc(ins, "pe")
                            P.w(v, "v", tk_pe)
                            cslice = P.comb[:, ti * E + e: ti * E + e + 1]
                            dst = acc[:, tloc * D + dc2 * 512: tloc * D + (dc2 + 1) * 512]
                            if e == 0:
                                P.w(v, "v", tok_acc_free)
                                ins = v.tensor_scalar_mul(dst, pY[:, dc2 * 512:(dc2 + 1) * 512], cslice)
                            else:
                                P.w(v, "v", tok_acc.get((tloc, dc2)))
                                ins = v.scalar_tensor_tensor(dst, pY[:, dc2 * 512:(dc2 + 1) * 512],
                                                             cslice, dst, ALU.mult, ALU.add)
                            tok_acc[(tloc, dc2)] = P.inc(ins, "dv")
                            tok_pY[dc2] = tok_acc[(tloc, dc2)]
                    tok_hT_free = P.tok("pe")
                    if ne_h < 2:
                        tok_w2_free = P.tok("pe")
                        tok_w2 = P.dma(g, "g", "gd", w2_sb[:].rearrange("p (hc d) -> p hc d", hc=HC),
                                       P.w2T[ne_e * H:(ne_e + 1) * H, :].rearrange("(hc p) d -> p hc d", p=128),
                                       toks=(tok_w2_free,))
                        tok_b2 = P.dma(g, "g", "gd", b2e[0:1, :], P.b2[ne_e:ne_e + 1, :])
                tok_xnT_free = P.tok("pe")
                for tloc in range(NT // 2):
                    ti = h * 8 + tloc
                    tok_fs = P.dma(sy, "s", "ld", fs[:].rearrange("p (o d) -> p o d", o=1),
                                   P.sS[:, ti:ti + 1, :],
                                   toks=(tok_fs_free, self.tok_d2_done))
                    P.w(a, "a", tok_acc[(tloc, 3)])
                    asl = acc[:, tloc * D:(tloc + 1) * D]
                    tok_gs = P.inc(a.activation(asl, asl, AF.Silu, bias=0.0, scale=1.0), "ac")
                    P.w(v, "v", tok_gs)
                    P.w(v, "v", tok_fs)
                    P.w(v, "v", tok_fo_free)
                    tok_o = P.inc(v.tensor_tensor(fo[:], fs[:], asl, ALU.mult), "dv")
                    tok_fs_free = tok_o
                    P.w(g, "g", tok_o)
                    P.w(g, "g", tok_xcopy)
                    ins = g.dma_start(P.out.rearrange("(s p) d -> p s d", p=128)[:, ti:ti + 1, :],
                                      fo[:].rearrange("p (o d) -> p o d", o=1),
                                      accum_op=ALU.add)
                    last_store = P.inc(ins, "gd", 16)
                    tok_fo_free = last_store
                tok_acc_free = P.tok("dv")
            if P.debug:
                P.w(g, "g", P.tok("dv"))
                P.inc(g.dma_start(P.dbg_acc, acc[:]), "gd", 16)
            P.w(sy, "s", last_store)
            P.w(sy, "s", P.tok("ld"))


def build_program(debug=False):
    return Prog(debug=debug).build()


def kernel(L0_attn_x, L0_attn_qkv, L0_attn_o_proj, L0_attn_gamma, L0_attn_beta,
           L0_ffn_gate_w, L0_ffn_gate_b, L0_ffn_bias, L0_ffn_w1, L0_ffn_b1,
           L0_ffn_w2, L0_ffn_b2):
    x = np.ascontiguousarray(np.asarray(L0_attn_x, dtype=np.float32))
    qkv = np.ascontiguousarray(np.asarray(L0_attn_qkv, dtype=np.float32))
    o_proj = np.ascontiguousarray(np.asarray(L0_attn_o_proj, dtype=np.float32))
    gamma = np.asarray(L0_attn_gamma, dtype=np.float32)
    beta = np.asarray(L0_attn_beta, dtype=np.float32)
    gate_w = np.asarray(L0_ffn_gate_w, dtype=np.float32)
    gate_b = np.asarray(L0_ffn_gate_b, dtype=np.float32)
    gbias = np.asarray(L0_ffn_bias, dtype=np.float32)
    w1 = np.asarray(L0_ffn_w1, dtype=np.float32)
    b1 = np.ascontiguousarray(np.asarray(L0_ffn_b1, dtype=np.float32))
    w2 = np.asarray(L0_ffn_w2, dtype=np.float32)
    b2 = np.ascontiguousarray(np.asarray(L0_ffn_b2, dtype=np.float32))

    shared = dict(
        qkv=qkv,
        o_proj=o_proj,
        gammaT=np.ascontiguousarray(gamma.reshape(DC, 128).T),
        betaT=np.ascontiguousarray(beta.reshape(DC, 128).T),
        gate_wT=np.ascontiguousarray(gate_w.T),
        gate_b=np.ascontiguousarray(np.broadcast_to(gate_b.reshape(1, E), (128, E))),
        gate_bias=np.ascontiguousarray(np.broadcast_to(gbias.reshape(1, E), (128, E))),
        w1T=np.ascontiguousarray(w1.transpose(0, 2, 1).reshape(E * D, H)),
        w2T=np.ascontiguousarray(w2.transpose(0, 2, 1).reshape(E * H, D)),
        b1col=np.ascontiguousarray(b1.reshape(E, HC, 128).transpose(2, 0, 1).reshape(128, E * HC)),
        b2=b2,
    )
    in_maps = []
    metas = []
    for core in range(8):
        bi, c = core // 4, core % 4
        tiles = np.array([16 * k + 4 * c + j for k in range(4) for j in range(4)])
        rows = (tiles[:, None] * 128 + np.arange(128)[None, :]).reshape(-1)
        xb = np.ascontiguousarray(x[bi])
        xq = np.ascontiguousarray(xb[rows])
        xqT = np.ascontiguousarray(xq.T)
        qpos_t = np.ascontiguousarray(rows.astype(np.float32).reshape(NT, 128).T)
        m = dict(xb=xb, xq=xq, xqT=xqT, qpos=qpos_t)
        m.update(shared)
        in_maps.append(m)
        metas.append((bi, rows))

    nc = build_program()
    res = run_bass_kernel_spmd(nc, in_maps, core_ids=list(range(8)))

    out = np.empty((2, S, D), dtype=np.float32)
    for core in range(8):
        bi, rows = metas[core]
        out[bi, rows] = res.results[core]["out"]
    return out


# version 13
# speedup vs baseline: 1.3825x; 1.0141x over previous
import numpy as np
from contextlib import ExitStack

import concourse.bass as bass
import concourse.mybir as mybir
from concourse.bass_utils import run_bass_kernel_spmd

F32 = mybir.dt.float32
BF16 = mybir.dt.bfloat16
I16 = mybir.dt.int16
AF = mybir.ActivationFunctionType
ALU = mybir.AluOpType
AX = mybir.AxisListType

D = 2048
S = 8192
TOWN = 2048
H = 1408
E = 8
EPS = 1e-5
SCALE = 1.0 / float(np.sqrt(2048.0))

DC = 16
HC = 11
NSB = 16
NT = 16


class Prog:
    def __init__(self, debug=False, n_experts=E):
        self.n_experts = n_experts
        self.debug = debug
        nc = bass.Bass("TRN2", target_bir_lowering=False)
        self.nc = nc
        self.ctx = ExitStack()
        dt = nc.dram_tensor
        self.xb = dt("xb", [S, D], F32, kind="ExternalInput").ap()
        self.xq = dt("xq", [TOWN, D], F32, kind="ExternalInput").ap()
        self.xqT = dt("xqT", [D, TOWN], F32, kind="ExternalInput").ap()
        self.qpos_i = dt("qpos", [128, NT], F32, kind="ExternalInput").ap()
        self.qkv = dt("qkv", [D, 3 * D], F32, kind="ExternalInput").ap()
        self.o_proj = dt("o_proj", [D, D], F32, kind="ExternalInput").ap()
        self.gammaT_i = dt("gammaT", [128, DC], F32, kind="ExternalInput").ap()
        self.betaT_i = dt("betaT", [128, DC], F32, kind="ExternalInput").ap()
        self.gate_wT = dt("gate_wT", [D, E], F32, kind="ExternalInput").ap()
        self.gate_b = dt("gate_b", [128, E], F32, kind="ExternalInput").ap()
        self.gate_bias = dt("gate_bias", [128, E], F32, kind="ExternalInput").ap()
        self.w1T = dt("w1T", [E * D, H], F32, kind="ExternalInput").ap()
        self.w2T = dt("w2T", [E * H, D], F32, kind="ExternalInput").ap()
        self.b1col = dt("b1col", [128, E * HC], F32, kind="ExternalInput").ap()
        self.utri = dt("utri", [128, 128], F32, kind="ExternalInput").ap()
        self.b2 = dt("b2", [E, D], F32, kind="ExternalInput").ap()
        self.out = dt("out", [TOWN, D], F32, kind="ExternalOutput").ap()
        kind = "ExternalOutput" if debug else "Internal"
        self.xn2TS = dt("xn2TS", [128, DC, S], BF16, kind=kind).ap()
        self.xn2TownS = dt("xn2TownS", [128, DC, TOWN], BF16, kind=kind).ap()
        self.ktS = dt("ktS", [128, DC, S], BF16, kind=kind).ap()
        self.vS = dt("vS", [128, S // 128, D], BF16, kind=kind).ap()
        self.xnS = dt("xnS", [128, NT, D], BF16, kind=kind).ap()
        self.xnTS = dt("xnTS", [128, DC, TOWN], BF16, kind=kind).ap()
        self.qTS = dt("qTS", [128, DC, TOWN], BF16, kind=kind).ap()
        self.oS = dt("oS", [128, NT, D], BF16, kind=kind).ap()
        self.sS = dt("sS", [128, NT, D], BF16, kind=kind).ap()
        if debug:
            self.dbg_comb = dt("dbg_comb", [128, NT * E], F32, kind=kind).ap()
            self.dbg_acc = dt("dbg_acc", [128, 8 * D], F32, kind=kind).ap()
            self.dbg_cap = dt("dbg_cap", [128, 16384], F32, kind=kind).ap()
            self.dbg_raw = dt("dbg_raw", [128, NT * E], F32, kind=kind).ap()
            self.dbg_sc = dt("dbg_sc", [128, NT * E], F32, kind=kind).ap()
            self.dbg_gw = dt("dbg_gw", [128, DC * E], F32, kind=kind).ap()
            self.dbg_cons = dt("dbg_cons", [128, 2 * E], F32, kind=kind).ap()

        self.sems = {}
        self.ct = {}
        for name in ["ld", "gd", "pe", "dv", "ac"]:
            self.sems[name] = self.ctx.enter_context(nc.semaphore(name))
            self.ct[name] = 0
        self.lastw = {}

        sb = lambda n, shp, d=F32: self.ctx.enter_context(nc.sbuf_tensor(n, shp, d))
        self.ident = sb("ident", [128, 128], BF16)
        self.ones_row = sb("ones_row", [32, 512], BF16)
        self.ones128f = sb("ones128f", [128, 128], F32)
        self.ones_colb = sb("ones_colb", [128, 1], BF16)
        self.ones_colf = sb("ones_colf", [128, 1], F32)
        self.gammaT = sb("gammaT_s", [128, DC])
        self.betaT = sb("betaT_s", [128, DC])
        self.qpos = sb("qpos_s", [128, NT])
        self.gate_wT_sb = sb("gate_wT_sb", [128, DC * E])
        self.b1col_sb = sb("b1col_sb", [128, E * HC])
        self.utri_sb = sb("utri_sb", [128, 128])
        self.onesr32f = sb("onesr32f", [32, 128])
        self.gb_bc = sb("gb_bc", [128, E])
        self.gbias_bc = sb("gbias_bc", [128, E])
        self.rowsum_bc = sb("rowsum_bc", [128, E])
        self.gbb_bc = sb("gbb_bc", [128, E])
        self.comb = sb("comb", [128, NT * E])
        for n in ["sx", "sxx", "mu", "m2", "musq", "var", "a1", "l1", "nh", "r1",
                  "ia1", "var2", "a2", "l2", "nh2", "r2", "rr", "mrr", "mr1n"]:
            setattr(self, n, sb(n, [128, 4]))
        for n in ["g_t1", "g_sc", "g_eq1", "g_msk", "g_eq2", "g_c1"]:
            setattr(self, n, sb(n, [128, E]))
        self.dbg_raw_sb = sb("dbg_raw_sb", [128, NT * E])
        self.dbg_sc_sb = sb("dbg_sc_sb", [128, NT * E])
        for n in ["g_m1", "g_m2", "g_d12", "g_e", "g_den", "g_g1", "g_g2"]:
            setattr(self, n, sb(n, [128, 1]))

        self.g = nc.gpsimd
        self.v = nc.vector
        self.a = nc.scalar
        self.t = nc.tensor
        self.sy = nc.sync

    def inc(self, ins, sem, n=1):
        ins.then_inc(self.sems[sem], n)
        self.ct[sem] += n
        return (sem, self.ct[sem])

    def tok(self, sem):
        return (sem, self.ct[sem])

    def w(self, eng, engname, tok):
        if tok is None:
            return
        sem, val = tok
        if val <= 0:
            return
        key = (engname, sem)
        if self.lastw.get(key, 0) >= val:
            return
        eng.wait_ge(self.sems[sem], val)
        self.lastw[key] = val

    def dma(self, qeng, qname, sem, dst, src, toks=()):
        for t in toks:
            self.w(qeng, qname, t)
        ins = qeng.dma_start(dst, src)
        return self.inc(ins, sem, 16)

    def phase_start(self):
        toks = [self.tok(s) for s in ("pe", "dv", "ac", "ld", "gd")]
        for eng, name in ((self.t, "t"), (self.g, "g"), (self.sy, "s")):
            for tk in toks:
                if (name, tk[0]) not in (("t", "pe"), ("g", "gd"), ("s", "ld")):
                    self.w(eng, name, tk)

    def emit_stats(self, x4, x4_tok, want_mr1n):
        P, v, a = self, self.v, self.a
        t_sx = t_sxx = None
        for j in range(4):
            P.w(v, "v", x4_tok)
            t_sx = P.inc(v.reduce_sum(P.sx[:, j:j + 1], x4[:, j * D:(j + 1) * D], AX.X), "dv")
        for j in range(4):
            P.w(a, "a", x4_tok)
            t_sxx = P.inc(a.activation(P.sqdump[:], x4[:, j * D:(j + 1) * D], AF.Square,
                                       bias=0.0, scale=1.0, accum_out=P.sxx[:, j:j + 1]), "ac")
        P.w(v, "v", t_sx)
        tk = P.inc(v.tensor_scalar_mul(P.mu[:], P.sx[:], 1.0 / D), "dv")
        P.w(v, "v", t_sxx)
        P.w(v, "v", tk)
        tk = P.inc(v.tensor_scalar_mul(P.m2[:], P.sxx[:], 1.0 / D), "dv")
        P.w(v, "v", tk)
        tk = P.inc(v.tensor_tensor(P.musq[:], P.mu[:], P.mu[:], ALU.mult), "dv")
        P.w(v, "v", tk)
        tk = P.inc(v.tensor_tensor(P.var[:], P.m2[:], P.musq[:], ALU.subtract), "dv")
        P.w(v, "v", tk)
        t_a1 = P.inc(v.tensor_scalar_add(P.a1[:], P.var[:], EPS), "dv")
        P.w(a, "a", t_a1)
        tk = P.inc(a.activation(P.l1[:], P.a1[:], AF.Ln), "ac")
        P.w(v, "v", tk)
        tk = P.inc(v.tensor_scalar_mul(P.nh[:], P.l1[:], -0.5), "dv")
        P.w(a, "a", tk)
        t_r1 = P.inc(a.activation(P.r1[:], P.nh[:], AF.Exp), "ac")
        P.w(v, "v", t_a1)
        tk = P.inc(v.reciprocal(P.ia1[:], P.a1[:]), "dv")
        P.w(v, "v", tk)
        tk = P.inc(v.tensor_tensor(P.var2[:], P.var[:], P.ia1[:], ALU.mult), "dv")
        P.w(v, "v", tk)
        tk = P.inc(v.tensor_scalar_add(P.a2[:], P.var2[:], EPS), "dv")
        P.w(a, "a", tk)
        tk = P.inc(a.activation(P.l2[:], P.a2[:], AF.Ln), "ac")
        P.w(v, "v", tk)
        tk = P.inc(v.tensor_scalar_mul(P.nh2[:], P.l2[:], -0.5), "dv")
        P.w(a, "a", tk)
        tk = P.inc(a.activation(P.r2[:], P.nh2[:], AF.Exp), "ac")
        P.w(v, "v", tk)
        P.w(v, "v", t_r1)
        tk = P.inc(v.tensor_tensor(P.rr[:], P.r1[:], P.r2[:], ALU.mult), "dv")
        P.w(v, "v", tk)
        t_mrr = P.inc(v.scalar_tensor_tensor(P.mrr[:], P.mu[:], -1.0, P.rr[:], ALU.mult, ALU.mult), "dv")
        t_r1mu = None
        if want_mr1n:
            P.w(v, "v", t_mrr)
            t_r1mu = P.inc(v.scalar_tensor_tensor(P.mr1n[:], P.mu[:], -1.0, P.r1[:], ALU.mult, ALU.mult), "dv")
        return t_mrr, t_r1mu

    def emit_transposes(self, src_sb, src_tok, pT, evac_tok, dst, dst_dc_stride,
                        gamma, beta, n_tiles=4, free_tok=None):
        P, t, v = self, self.t, self.v
        last = None
        for dc in range(DC):
            bank = dc % 2
            P.w(t, "t", evac_tok[bank])
            P.w(t, "t", src_tok)
            if free_tok is not None:
                P.w(t, "t", free_tok)
            for j in range(n_tiles):
                ins = t.transpose(pT[bank][:, j * 128:(j + 1) * 128],
                                  src_sb[:, j * D + dc * 128: j * D + (dc + 1) * 128],
                                  P.ident[:])
            tk_pe = P.inc(ins, "pe")
            P.w(v, "v", tk_pe)
            dsl = dst[:, dc * dst_dc_stride: dc * dst_dc_stride + n_tiles * 128]
            if gamma is not None:
                ins = v.tensor_scalar(dsl, pT[bank][:, 0:n_tiles * 128],
                                      gamma[:, dc:dc + 1], beta[:, dc:dc + 1],
                                      ALU.mult, ALU.add)
            else:
                ins = v.tensor_copy(dsl, pT[bank][:, 0:n_tiles * 128])
            last = P.inc(ins, "dv")
            evac_tok[bank] = last
        return last

    def emit_proj_kc(self, w_sb, src, pX, evac_toks, dstbuf, dst_free_tok, ready_tok):
        P, t, a = self, self.t, self.a
        evac = evac_toks
        for kc in range(DC):
            bank = kc % 2
            P.w(t, "t", evac[bank])
            P.w(t, "t", ready_tok)
            for dc in range(DC):
                ins = t.matmul(pX[:, bank * 512:(bank + 1) * 512],
                               w_sb[:, dc * D + kc * 128: dc * D + (kc + 1) * 128],
                               src[:, dc * 512:(dc + 1) * 512],
                               start=(dc == 0), stop=(dc == DC - 1))
            tk_pe = P.inc(ins, "pe")
            P.w(a, "a", tk_pe)
            P.w(a, "a", dst_free_tok)
            ins = a.activation(dstbuf[:, kc * 512:(kc + 1) * 512],
                               pX[:, bank * 512:(bank + 1) * 512], AF.Copy,
                               bias=0.0, scale=1.0)
            evac[bank] = P.inc(ins, "ac")
        return evac[(DC - 1) % 2]

    def build(self):
        self.emit_prep()
        self.emit_a()
        self.emit_b()
        self.emit_c1()
        self.emit_c2()
        self.emit_d()
        self.emit_d2()
        self.emit_e_f()
        self.ctx.close()
        return self.nc

    def emit_prep(self):
        P = self
        g, v, a, t, sy = self.g, self.v, self.a, self.t, self.sy
        g.memset(P.ident[:], 0.0)
        g.affine_select(P.ident[:], P.ident[:], [[1, 128]], ALU.not_equal, 1.0,
                        base=0, channel_multiplier=-1)
        g.memset(P.ones_row[:], 0.0)
        g.memset(P.ones_row[0:1, :], 1.0)
        g.memset(P.ones128f[:], 1.0)
        g.memset(P.onesr32f[:], 0.0)
        g.memset(P.onesr32f[0:1, :], 1.0)
        g.memset(P.ones_colb[:], 1.0)
        tok_consts = P.inc(g.memset(P.ones_colf[:], 1.0), "gd")

        P.dma(sy, "s", "ld", P.gammaT[:], P.gammaT_i)
        P.dma(sy, "s", "ld", P.betaT[:], P.betaT_i)
        P.tok_qpos = P.dma(sy, "s", "ld", P.qpos[:], P.qpos_i)
        P.tok_b1col = P.dma(sy, "s", "ld", P.b1col_sb[:], P.b1col)
        P.tok_utri = P.dma(sy, "s", "ld", P.utri_sb[:], P.utri)
        tok_gw = P.dma(sy, "s", "ld",
                       P.gate_wT_sb[:].rearrange("p (dc e) -> p dc e", dc=DC),
                       P.gate_wT.rearrange("(dc p) e -> p dc e", p=128))
        tok_gb = P.dma(sy, "s", "ld", P.gb_bc[:], P.gate_b)
        tok_gbias = P.dma(sy, "s", "ld", P.gbias_bc[:], P.gate_bias)
        P.tok_gamma_ready = P.tok_qpos

        with self.nc.psum_tensor("pPrep", [128, 512], F32) as pP:
            P.w(t, "t", tok_gw)
            P.w(t, "t", tok_consts)
            for dc in range(DC):
                ins = t.matmul(pP[:, 0:E], P.ones128f[:], P.gate_wT_sb[:, dc * E:(dc + 1) * E],
                               start=(dc == 0), stop=(dc == DC - 1))
            tk_pe = P.inc(ins, "pe")
            P.w(v, "v", tk_pe)
            P.inc(v.tensor_copy(P.rowsum_bc[:], pP[:, 0:E]), "dv")
            P.w(v, "v", tok_gb)
            P.w(v, "v", tok_gbias)
            P.w(v, "v", P.tok("dv"))
            P.tok_gate_ready = P.inc(v.tensor_tensor(P.gbb_bc[:], P.gb_bc[:], P.gbias_bc[:], ALU.add), "dv")

    def emit_a(self):
        P = self
        nc = self.nc
        g, v, a, t, sy = self.g, self.v, self.a, self.t, self.sy
        with ExitStack() as phase_ctx:
            x4 = phase_ctx.enter_context(nc.sbuf_tensor("ax4", [128, 4 * D], F32))
            xnl2 = phase_ctx.enter_context(nc.sbuf_tensor("axnl2", [128, 4 * D], BF16))
            xn2T = phase_ctx.enter_context(nc.sbuf_tensor("axn2T", [128, DC * 512], BF16))
            ktbuf = phase_ctx.enter_context(nc.sbuf_tensor("aktbuf", [128, DC * 512], BF16))
            wk_sb = phase_ctx.enter_context(nc.sbuf_tensor("awk", [128, DC * D], BF16))
            self.sqdump = phase_ctx.enter_context(nc.sbuf_tensor("asq", [128, D], BF16))
            x4b = phase_ctx.enter_context(nc.sbuf_tensor("ax4b", [128, 4 * D], F32))
            pK = phase_ctx.enter_context(nc.psum_tensor("apK", [128, 2 * 512], F32))
            pT0 = phase_ctx.enter_context(nc.psum_tensor("apT0", [128, 512], BF16))
            pT1 = phase_ctx.enter_context(nc.psum_tensor("apT1", [128, 512], BF16))
            pT = [pT0, pT1]
            P.phase_start()
            tok_wk = P.dma(g, "g", "gd", wk_sb[:].rearrange("p (dc f) -> p dc f", dc=DC),
                           P.qkv[:, D:2 * D].rearrange("(dc p) f -> p dc f", p=128))
            x4s = [x4, x4b]
            tok_sp_xn2T = tok_sp_kt = None
            evac_tok = [None, None]
            evacK = [None, None]
            tok_x4 = [None] * (NSB + 1)
            xnl2_done = [None] * (NSB + 1)
            tok_x4[0] = P.dma(sy, "s", "ld",
                              x4s[0][:].rearrange("p (s d) -> p s d", s=4),
                              P.xb.rearrange("(s p) d -> p s d", p=128)[:, 0:4, :],
                              toks=(P.tok_gamma_ready,))

            def emit_ln(sb_i, war_tok):
                buf = x4s[sb_i % 2]
                t_mrr, _ = P.emit_stats(buf, tok_x4[sb_i], want_mr1n=False)
                last = None
                for j in range(4):
                    P.w(v, "v", t_mrr)
                    P.w(v, "v", war_tok)
                    ins = v.tensor_scalar(xnl2[:, j * D:(j + 1) * D], buf[:, j * D:(j + 1) * D],
                                          P.rr[:, j:j + 1], P.mrr[:, j:j + 1], ALU.mult, ALU.add)
                    last = P.inc(ins, "dv")
                xnl2_done[sb_i] = last

            emit_ln(0, None)
            for sb_i in range(NSB):
                t0 = sb_i * 512
                tok_xn2T_ready = P.emit_transposes(xnl2, xnl2_done[sb_i], pT, evac_tok, xn2T, 512,
                                                  P.gammaT, P.betaT, free_tok=tok_sp_xn2T)
                tok_tr_pe = P.tok("pe")
                if sb_i + 1 < NSB:
                    tok_x4[sb_i + 1] = P.dma(
                        sy, "s", "ld",
                        x4s[(sb_i + 1) % 2][:].rearrange("p (s d) -> p s d", s=4),
                        P.xb.rearrange("(s p) d -> p s d", p=128)[:, (sb_i + 1) * 4:(sb_i + 2) * 4, :],
                        toks=(xnl2_done[sb_i - 1] if sb_i >= 1 else None,))
                    emit_ln(sb_i + 1, tok_tr_pe)
                P.w(t, "t", tok_wk)
                tok_kt_ready = P.emit_proj_kc(wk_sb, xn2T, pK, evacK, ktbuf,
                                              tok_sp_kt, tok_xn2T_ready)
                tok_sp_xn2T = P.dma(g, "g", "gd", P.xn2TS[:, :, t0:t0 + 512],
                                    xn2T[:].rearrange("p (dc t) -> p dc t", dc=DC),
                                    toks=(tok_xn2T_ready,))
                tok_sp_kt = P.dma(g, "g", "gd", P.ktS[:, :, t0:t0 + 512],
                                  ktbuf[:].rearrange("p (kc t) -> p kc t", kc=DC),
                                  toks=(tok_kt_ready,))
            self.tok_a_done = tok_sp_kt

    def emit_b(self):
        P = self
        nc = self.nc
        g, v, a, t, sy = self.g, self.v, self.a, self.t, self.sy
        with ExitStack() as phase_ctx:
            xt0 = phase_ctx.enter_context(nc.sbuf_tensor("bxt0", [128, DC * 512], BF16))
            xt1 = phase_ctx.enter_context(nc.sbuf_tensor("bxt1", [128, DC * 512], BF16))
            vbuf = phase_ctx.enter_context(nc.sbuf_tensor("bvbuf", [128, 4 * D], BF16))
            wv_sb = phase_ctx.enter_context(nc.sbuf_tensor("bwv", [128, DC * D], BF16))
            pV = phase_ctx.enter_context(nc.psum_tensor("bpV", [128, 2 * 512], F32))
            xts = [xt0, xt1]
            P.phase_start()
            tok_wv = P.dma(g, "g", "gd", wv_sb[:].rearrange("p (dc f) -> p dc f", dc=DC),
                           P.qkv[:, 2 * D:3 * D].rearrange("(dc p) f -> p dc f", p=128))
            tok_xt_free = [None, None]
            tok_sp_v = None
            evacV = [None, None]
            for sb_i in range(NSB):
                t0 = sb_i * 512
                bk = sb_i % 2
                tok_xt = P.dma(sy, "s", "ld", xts[bk][:].rearrange("p (dc t) -> p dc t", dc=DC),
                               P.xn2TS[:, :, t0:t0 + 512],
                               toks=(tok_xt_free[bk], self.tok_a_done))
                P.w(t, "t", tok_wv)
                nvb = 0
                tok_v_ready = None
                for j in range(4):
                    for dvc in range(4):
                        bank = nvb % 2
                        nvb += 1
                        P.w(t, "t", evacV[bank])
                        P.w(t, "t", tok_xt)
                        for dc in range(DC):
                            ins = t.matmul(pV[:, bank * 512:(bank + 1) * 512],
                                           xts[bk][:, dc * 512 + j * 128: dc * 512 + (j + 1) * 128],
                                           wv_sb[:, dc * D + dvc * 512: dc * D + (dvc + 1) * 512],
                                           start=(dc == 0), stop=(dc == DC - 1))
                        tk_pe = P.inc(ins, "pe")
                        P.w(a, "a", tk_pe)
                        P.w(a, "a", tok_sp_v)
                        ins = a.activation(vbuf[:, j * D + dvc * 512: j * D + (dvc + 1) * 512],
                                           pV[:, bank * 512:(bank + 1) * 512], AF.Copy,
                                           bias=0.0, scale=1.0)
                        evacV[bank] = P.inc(ins, "ac")
                        tok_v_ready = evacV[bank]
                tok_xt_free[bk] = P.tok("pe")
                tok_sp_v = P.dma(g, "g", "gd", P.vS[:, sb_i * 4:(sb_i + 1) * 4, :],
                                 vbuf[:].rearrange("p (s d) -> p s d", s=4),
                                 toks=(tok_v_ready,))
            self.tok_b_done = tok_sp_v

    def emit_c1(self):
        P = self
        nc = self.nc
        g, v, a, t, sy = self.g, self.v, self.a, self.t, self.sy
        with ExitStack() as phase_ctx:
            x4 = phase_ctx.enter_context(nc.sbuf_tensor("cx4", [128, 4 * D], F32))
            xqT4 = phase_ctx.enter_context(nc.sbuf_tensor("cxqT4", [128, DC * 512], F32))
            xnl2 = phase_ctx.enter_context(nc.sbuf_tensor("cxnl2", [128, 4 * D], BF16))
            xn2T = phase_ctx.enter_context(nc.sbuf_tensor("cxn2T", [128, DC * 512], BF16))
            xn_b = phase_ctx.enter_context(nc.sbuf_tensor("cxn", [128, 4 * D], BF16))
            xnT = phase_ctx.enter_context(nc.sbuf_tensor("cxnT", [128, DC * 512], BF16))
            self.sqdump = phase_ctx.enter_context(nc.sbuf_tensor("csq", [128, D], BF16))
            pRaw = phase_ctx.enter_context(nc.psum_tensor("cpRaw", [128, 512], F32))
            pT0 = phase_ctx.enter_context(nc.psum_tensor("cpT0", [128, 512], BF16))
            pT1 = phase_ctx.enter_context(nc.psum_tensor("cpT1", [128, 512], BF16))
            pT = [pT0, pT1]
            P.phase_start()
            tok_x4_free = None
            tok_sp_xn = tok_sp_xnT = tok_sp_x2 = None
            tok_raw_read = None
            evac_tok = [None, None]
            for c in range(4):
                t0 = c * 512
                tok_x4 = P.dma(sy, "s", "ld",
                               x4[:].rearrange("p (s d) -> p s d", s=4),
                               P.xq.rearrange("(s p) d -> p s d", p=128)[:, c * 4:(c + 1) * 4, :],
                               toks=(tok_x4_free,))
                tok_xqT = P.dma(sy, "s", "ld",
                                xqT4[:].rearrange("p (dc t) -> p dc t", dc=DC),
                                P.xqT.rearrange("(dc p) t -> p dc t", p=128)[:, :, c * 512:(c + 1) * 512],
                                toks=(tok_raw_read,))
                t_mrr, t_r1mu = P.emit_stats(x4, tok_x4, want_mr1n=True)
                for j in range(4):
                    ti = c * 4 + j
                    P.w(t, "t", tok_xqT)
                    P.w(t, "t", P.tok_gate_ready)
                    P.w(t, "t", tok_raw_read)
                    for dc in range(DC):
                        ins = t.matmul(pRaw[:, 0:E],
                                       xqT4[:, dc * 512 + j * 128: dc * 512 + (j + 1) * 128],
                                       P.gate_wT_sb[:, dc * E:(dc + 1) * E],
                                       start=(dc == 0), stop=(dc == DC - 1))
                    tk_raw = P.inc(ins, "pe")
                    P.w(v, "v", tk_raw)
                    P.w(v, "v", t_r1mu)
                    if P.debug:
                        P.inc(v.tensor_copy(P.dbg_raw_sb[:, ti * E:(ti + 1) * E], pRaw[:, 0:E]), "dv")
                    tk = P.inc(v.scalar_tensor_tensor(P.g_t1[:], pRaw[:, 0:E], P.r1[:, j:j + 1],
                                                      P.gbb_bc[:], ALU.mult, ALU.add), "dv")
                    tok_raw_read = tk
                    P.w(v, "v", tk)
                    tk = P.inc(v.scalar_tensor_tensor(P.g_sc[:], P.rowsum_bc[:], P.mr1n[:, j:j + 1],
                                                      P.g_t1[:], ALU.mult, ALU.add), "dv")
                    P.w(v, "v", tk)
                    if P.debug:
                        P.inc(v.tensor_copy(P.dbg_sc_sb[:, ti * E:(ti + 1) * E], P.g_sc[:]), "dv")
                        P.w(v, "v", P.tok("dv"))
                    tk = P.inc(v.reduce_max(P.g_m1[:], P.g_sc[:], AX.X), "dv")
                    P.w(v, "v", tk)
                    tk = P.inc(v.tensor_scalar(P.g_eq1[:], P.g_sc[:], P.g_m1[:, 0:1], None, ALU.is_ge), "dv")
                    P.w(v, "v", tk)
                    tk = P.inc(v.scalar_tensor_tensor(P.g_msk[:], P.g_eq1[:], -1e30, P.g_sc[:],
                                                      ALU.mult, ALU.add), "dv")
                    P.w(v, "v", tk)
                    tk = P.inc(v.reduce_max(P.g_m2[:], P.g_msk[:], AX.X), "dv")
                    P.w(v, "v", tk)
                    tk = P.inc(v.tensor_scalar(P.g_eq2[:], P.g_msk[:], P.g_m2[:, 0:1], None, ALU.is_ge), "dv")
                    P.w(v, "v", tk)
                    tk = P.inc(v.tensor_tensor(P.g_d12[:], P.g_m1[:], P.g_m2[:], ALU.subtract), "dv")
                    P.w(a, "a", tk)
                    tk = P.inc(a.activation(P.g_e[:], P.g_d12[:], AF.Exp, bias=0.0, scale=-1.0), "ac")
                    P.w(v, "v", tk)
                    tk = P.inc(v.tensor_scalar_add(P.g_den[:], P.g_e[:], 1.0), "dv")
                    P.w(v, "v", tk)
                    tk = P.inc(v.reciprocal(P.g_g1[:], P.g_den[:]), "dv")
                    P.w(v, "v", tk)
                    tk = P.inc(v.tensor_scalar(P.g_g2[:], P.g_g1[:], -1.0, 1.0, ALU.mult, ALU.add), "dv")
                    P.w(v, "v", tk)
                    tk = P.inc(v.tensor_scalar_mul(P.g_c1[:], P.g_eq1[:], P.g_g1[:, 0:1]), "dv")
                    P.w(v, "v", tk)
                    P.inc(v.scalar_tensor_tensor(P.comb[:, ti * E:(ti + 1) * E], P.g_eq2[:],
                                                 P.g_g2[:, 0:1], P.g_c1[:], ALU.mult, ALU.add), "dv")
                last_xn = last_xnl2 = None
                for j in range(4):
                    P.w(v, "v", t_mrr)
                    P.w(v, "v", t_r1mu)
                    ins = v.tensor_scalar(xn_b[:, j * D:(j + 1) * D], x4[:, j * D:(j + 1) * D],
                                          P.r1[:, j:j + 1], P.mr1n[:, j:j + 1], ALU.mult, ALU.add)
                    last_xn = P.inc(ins, "dv")
                    ins = v.tensor_scalar(xnl2[:, j * D:(j + 1) * D], x4[:, j * D:(j + 1) * D],
                                          P.rr[:, j:j + 1], P.mrr[:, j:j + 1], ALU.mult, ALU.add)
                    last_xnl2 = P.inc(ins, "dv")
                tok_x4_free = last_xnl2
                tok_xnT_ready = P.emit_transposes(xn_b, last_xn, pT, evac_tok, xnT, 512,
                                                 None, None, free_tok=tok_sp_xnT)
                tok_x2_ready = P.emit_transposes(xnl2, last_xnl2, pT, evac_tok, xn2T, 512,
                                                 P.gammaT, P.betaT, free_tok=tok_sp_x2)
                tok_sp_xn = P.dma(g, "g", "gd", P.xnS[:, c * 4:(c + 1) * 4, :],
                                  xn_b[:].rearrange("p (s d) -> p s d", s=4),
                                  toks=(last_xn,))
                tok_sp_xnT = P.dma(g, "g", "gd", P.xnTS[:, :, t0:t0 + 512],
                                   xnT[:].rearrange("p (dc t) -> p dc t", dc=DC),
                                   toks=(tok_xnT_ready,))
                tok_sp_x2 = P.dma(g, "g", "gd", P.xn2TownS[:, :, t0:t0 + 512],
                                  xn2T[:].rearrange("p (dc t) -> p dc t", dc=DC),
                                  toks=(tok_x2_ready,))
            self.tok_c1_done = tok_sp_x2
            if P.debug:
                P.w(g, "g", P.tok("dv"))
                P.inc(g.dma_start(P.dbg_comb, P.comb[:]), "gd", 16)
                P.inc(g.dma_start(P.dbg_raw, P.dbg_raw_sb[:]), "gd", 16)
                P.inc(g.dma_start(P.dbg_sc, P.dbg_sc_sb[:]), "gd", 16)
                P.inc(g.dma_start(P.dbg_gw, P.gate_wT_sb[:]), "gd", 16)
                P.inc(g.dma_start(P.dbg_cons[:, 0:E], P.rowsum_bc[:]), "gd", 16)
                P.inc(g.dma_start(P.dbg_cons[:, E:2 * E], P.gbb_bc[:]), "gd", 16)

    def emit_c2(self):
        P = self
        nc = self.nc
        g, v, a, t, sy = self.g, self.v, self.a, self.t, self.sy
        with ExitStack() as phase_ctx:
            xt0 = phase_ctx.enter_context(nc.sbuf_tensor("qxt0", [128, DC * 512], BF16))
            xt1 = phase_ctx.enter_context(nc.sbuf_tensor("qxt1", [128, DC * 512], BF16))
            qtbuf = phase_ctx.enter_context(nc.sbuf_tensor("qqt", [128, DC * 512], BF16))
            wq_sb = phase_ctx.enter_context(nc.sbuf_tensor("qwq", [128, DC * D], BF16))
            pQ = phase_ctx.enter_context(nc.psum_tensor("qpQ", [128, 2 * 512], F32))
            xts = [xt0, xt1]
            P.phase_start()
            tok_wq = P.dma(g, "g", "gd", wq_sb[:].rearrange("p (dc f) -> p dc f", dc=DC),
                           P.qkv[:, 0:D].rearrange("(dc p) f -> p dc f", p=128))
            tok_xt_free = [None, None]
            tok_sp_qt = None
            evacQ = [None, None]
            for c in range(4):
                t0 = c * 512
                bk = c % 2
                tok_xt = P.dma(sy, "s", "ld", xts[bk][:].rearrange("p (dc t) -> p dc t", dc=DC),
                               P.xn2TownS[:, :, t0:t0 + 512],
                               toks=(tok_xt_free[bk], self.tok_c1_done))
                P.w(t, "t", tok_wq)
                tok_qt_ready = P.emit_proj_kc(wq_sb, xts[bk], pQ, evacQ, qtbuf,
                                              tok_sp_qt, tok_xt)
                tok_xt_free[bk] = P.tok("pe")
                tok_sp_qt = P.dma(g, "g", "gd", P.qTS[:, :, t0:t0 + 512],
                                  qtbuf[:].rearrange("p (kc t) -> p kc t", kc=DC),
                                  toks=(tok_qt_ready,))
            self.tok_c2_done = tok_sp_qt

    def emit_d(self):
        P = self
        nc = self.nc
        g, v, a, t, sy = self.g, self.v, self.a, self.t, self.sy
        with ExitStack() as phase_ctx:
            arena = phase_ctx.enter_context(nc.sbuf_tensor("dar", [128, 4 * S], BF16))
            Pb = phase_ctx.enter_context(nc.sbuf_tensor("dP", [128, S], BF16))
            mask = phase_ctx.enter_context(nc.sbuf_tensor("dmask", [128, S], BF16))
            kvidx = phase_ctx.enter_context(nc.sbuf_tensor("dkvidx", [128, S], I16))
            kt0 = phase_ctx.enter_context(nc.sbuf_tensor("dkt0", [128, DC * 512], BF16))
            kt1 = phase_ctx.enter_context(nc.sbuf_tensor("dkt1", [128, DC * 512], BF16))
            vb0 = phase_ctx.enter_context(nc.sbuf_tensor("dvb0", [128, 4 * 1024], BF16))
            vb1 = phase_ctx.enter_context(nc.sbuf_tensor("dvb1", [128, 4 * 1024], BF16))
            qt4 = phase_ctx.enter_context(nc.sbuf_tensor("dqt4", [128, DC * 512], BF16))
            ob4 = phase_ctx.enter_context(nc.sbuf_tensor("dob4", [128, 4 * D], BF16))
            bm4 = phase_ctx.enter_context(nc.sbuf_tensor("dbm4", [128, 64], F32))
            negm4 = phase_ctx.enter_context(nc.sbuf_tensor("dnegm4", [128, 4], F32))
            negms4 = phase_ctx.enter_context(nc.sbuf_tensor("dnegms4", [128, 4], F32))
            l4 = phase_ctx.enter_context(nc.sbuf_tensor("dl4", [32, 512], F32))
            lcol4 = phase_ctx.enter_context(nc.sbuf_tensor("dlcol4", [128, 4], F32))
            linv4 = phase_ctx.enter_context(nc.sbuf_tensor("dlinv4", [128, 4], F32))
            kts = [kt0, kt1]
            vbs = [vb0, vb1]
            P.phase_start()
            g.memset(l4[:], 0.0)
            tok_kvidx = P.inc(g.iota(kvidx[:], [[1, S]], base=0, channel_multiplier=0,
                                     allow_small_or_imprecise_dtypes=True), "gd")
            tok_kt_free = [None, None]
            tok_vb_free = [None, None]
            tok_qt_free = None
            tok_arena_free = None
            tok_psum_free = None
            tok_mask_free = None
            tok_P_free = None
            tok_ob_spill = None
            tok_sp_ob = None
            for gi in range(4):
                L = 2048 * (gi + 1)
                nblk = L // 512
                nch = L // 128
                tok_qt = P.dma(sy, "s", "ld", qt4[:].rearrange("p (kc t) -> p kc t", kc=DC),
                               P.qTS[:, :, gi * 512:(gi + 1) * 512],
                               toks=(tok_qt_free, self.tok_c2_done))
                with nc.psum_tensor(f"dpS{gi}", [128, 2 * 512], F32) as pS:
                    tok_sc_evac = {}
                    nb_mm = 0
                    for b in range(nblk):
                        bk = b % 2
                        tok_kt = P.dma(sy, "s", "ld",
                                       kts[bk][:].rearrange("p (kc t) -> p kc t", kc=DC),
                                       P.ktS[:, :, b * 512:(b + 1) * 512],
                                       toks=(tok_kt_free[bk], self.tok_b_done))
                        for tl in range(4):
                            bank = nb_mm % 2
                            nb_mm += 1
                            P.w(t, "t", tok_kt)
                            P.w(t, "t", tok_qt)
                            P.w(t, "t", tok_sc_evac.get(nb_mm - 3))
                            P.w(t, "t", tok_psum_free)
                            P.w(t, "t", tok_arena_free)
                            for kc in range(DC):
                                ins = t.matmul(pS[:, bank * 512:(bank + 1) * 512],
                                               qt4[:, kc * 512 + tl * 128: kc * 512 + (tl + 1) * 128],
                                               kts[bk][:, kc * 512:(kc + 1) * 512],
                                               start=(kc == 0), stop=(kc == DC - 1))
                            tk_pe = P.inc(ins, "pe")
                            if tl == 3:
                                tok_kt_free[bk] = tk_pe
                            P.w(a, "a", tk_pe)
                            P.w(a, "a", tok_arena_free)
                            P.w(a, "a", tok_P_free)
                            ins = a.activation(arena[:, tl * S + b * 512: tl * S + (b + 1) * 512],
                                               pS[:, bank * 512:(bank + 1) * 512], AF.Copy,
                                               bias=0.0, scale=1.0)
                            tok_sc_evac[nb_mm - 1] = P.inc(ins, "ac")
                            P.w(v, "v", tok_sc_evac[nb_mm - 1])
                            P.inc(v.reduce_max(bm4[:, tl * 16 + b: tl * 16 + b + 1],
                                               arena[:, tl * S + b * 512: tl * S + (b + 1) * 512],
                                               AX.X), "dv")
                    P.w(v, "v", P.tok("dv"))
                    for tl in range(4):
                        P.inc(v.reduce_max(negm4[:, tl:tl + 1], bm4[:, tl * 16: tl * 16 + nblk],
                                           AX.X, negate=True), "dv")
                    P.w(v, "v", P.tok("dv"))
                    tok_negms = P.inc(v.tensor_scalar_mul(negms4[:], negm4[:], SCALE), "dv")
                    tok_pass1_sc = P.tok("ac")
                with nc.psum_tensor(f"dpT{gi}", [128, 512], BF16) as pTr, \
                     nc.psum_tensor(f"dpL{gi}", [128, 512], F32) as pL:
                    tok_tr_evac = None
                    tok_l_evac = None
                    tok_ptb_chunk = {}
                    for tl in range(4):
                        ti = gi * 4 + tl
                        P.w(v, "v", tok_kvidx)
                        P.w(v, "v", tok_mask_free)
                        P.w(v, "v", P.tok_qpos)
                        tok_mask = P.inc(v.tensor_scalar(mask[:, 0:L], kvidx[:, 0:L],
                                                         P.qpos[:, ti:ti + 1], None, ALU.is_le), "dv")
                        P.w(a, "a", tok_negms)
                        P.w(a, "a", tok_P_free)
                        tok_exp = P.inc(a.activation(Pb[:, 0:L], arena[:, tl * S: tl * S + L], AF.Exp,
                                                     bias=negms4[:, tl:tl + 1], scale=SCALE), "ac")
                        P.w(v, "v", tok_exp)
                        P.w(v, "v", tok_mask)
                        tok_Pm = P.inc(v.tensor_tensor(Pb[:, 0:L], Pb[:, 0:L], mask[:, 0:L], ALU.mult), "dv")
                        tok_mask_free = tok_Pm
                        for g4 in range(nch // 4):
                            P.w(t, "t", tok_tr_evac)
                            P.w(t, "t", tok_Pm)
                            if tl == 0 and g4 == 0:
                                P.w(t, "t", tok_pass1_sc)
                            for j in range(4):
                                k = g4 * 4 + j
                                ins = t.transpose(pTr[:, j * 128:(j + 1) * 128],
                                                  Pb[:, k * 128:(k + 1) * 128], P.ident[:])
                            tk_pe = P.inc(ins, "pe")
                            P.w(v, "v", tk_pe)
                            ins = v.tensor_copy(arena[:, tl * S + g4 * 512: tl * S + (g4 + 1) * 512],
                                                pTr[:, 0:512])
                            tok_tr_evac = P.inc(ins, "dv")
                            for j in range(4):
                                tok_ptb_chunk[(tl, g4 * 4 + j)] = tok_tr_evac
                        tok_P_free = P.tok("pe")
                        for k in range(nch):
                            P.w(t, "t", tok_ptb_chunk[(tl, k)])
                            if tl == 0 and k == 0:
                                P.w(t, "t", tok_l_evac)
                            ins = t.matmul(pL[0:1, tl * 128:(tl + 1) * 128], P.ones_colb[:],
                                           arena[:, tl * S + k * 128: tl * S + (k + 1) * 128],
                                           start=(k == 0), stop=(k == nch - 1))
                        tk_pe = P.inc(ins, "pe")
                        P.w(v, "v", tk_pe)
                        tok_l_evac = P.inc(v.tensor_copy(l4[0:1, tl * 128:(tl + 1) * 128],
                                                         pL[0:1, tl * 128:(tl + 1) * 128]), "dv")
                    for tl in range(4):
                        P.w(t, "t", tok_l_evac)
                        ins = t.matmul(pL[:, 256 + 2 * tl: 258 + 2 * tl], l4[0:32, tl * 128:(tl + 1) * 128],
                                       P.ones128f[0:32, 0:2], start=True, stop=True)
                    tk_pe = P.inc(ins, "pe")
                    P.w(v, "v", tk_pe)
                    for tl in range(4):
                        P.inc(v.tensor_copy(lcol4[:, tl:tl + 1], pL[:, 256 + 2 * tl: 257 + 2 * tl]), "dv")
                    P.w(v, "v", P.tok("dv"))
                    tok_linv = P.inc(v.reciprocal(linv4[:], lcol4[:]), "dv")
                    tok_pass2_pe = P.tok("pe")
                with nc.psum_tensor(f"dpO{gi}", [128, 8 * 512], F32) as pO:
                    tok_ob_evac = None
                    for dvh in range(2):
                        last_pv = None
                        for b in range(nblk):
                            bk = b % 2
                            tok_vb = P.dma(sy, "s", "ld",
                                           vbs[bk][:].rearrange("p (s d) -> p s d", s=4),
                                           P.vS[:, b * 4:(b + 1) * 4, dvh * 1024:(dvh + 1) * 1024],
                                           toks=(tok_vb_free[bk],))
                            P.w(t, "t", tok_vb)
                            if dvh == 0 and b == 0:
                                P.w(t, "t", tok_linv)
                                P.w(t, "t", tok_ob_spill)
                            if dvh == 1 and b == 0:
                                P.w(t, "t", tok_ob_evac)
                            for s4 in range(4):
                                k = b * 4 + s4
                                for tl in range(4):
                                    for dvc in range(2):
                                        ins = t.matmul(
                                            pO[:, (tl * 2 + dvc) * 512:(tl * 2 + dvc + 1) * 512],
                                            arena[:, tl * S + k * 128: tl * S + (k + 1) * 128],
                                            vbs[bk][:, s4 * 1024 + dvc * 512: s4 * 1024 + (dvc + 1) * 512],
                                            start=(k == 0), stop=(k == nch - 1))
                            last_pv = P.inc(ins, "pe")
                            tok_vb_free[bk] = last_pv
                        for tl in range(4):
                            P.w(a, "a", last_pv)
                            P.w(a, "a", tok_linv)
                            P.w(a, "a", tok_sp_ob)
                            ins = a.activation(ob4[:, tl * D + dvh * 1024: tl * D + (dvh + 1) * 1024],
                                               pO[:, tl * 1024: tl * 1024 + 1024], AF.Copy,
                                               bias=0.0, scale=linv4[:, tl:tl + 1])
                            tok_ob_evac = P.inc(ins, "ac")
                    tok_arena_free = P.tok("pe")
                    tok_psum_free = tok_ob_evac
                tok_sp_ob = P.dma(g, "g", "gd", P.oS[:, gi * 4:(gi + 1) * 4, :],
                                  ob4[:].rearrange("p (s d) -> p s d", s=4),
                                  toks=(tok_ob_evac,))
                tok_ob_spill = tok_sp_ob
                tok_qt_free = P.tok("pe")
            self.tok_d_done = tok_sp_ob

    def emit_d2(self):
        P = self
        nc = self.nc
        g, v, a, t, sy = self.g, self.v, self.a, self.t, self.sy
        with ExitStack() as phase_ctx:
            wo_sb = phase_ctx.enter_context(nc.sbuf_tensor("ewo", [128, DC * D], BF16))
            ob = phase_ctx.enter_context(nc.sbuf_tensor("eob", [128, D], BF16))
            otb = phase_ctx.enter_context(nc.sbuf_tensor("eotb", [128, DC * 128], BF16))
            xnt = phase_ctx.enter_context(nc.sbuf_tensor("exn", [128, D], BF16))
            s_bf = phase_ctx.enter_context(nc.sbuf_tensor("esbf", [128, D], BF16))
            pS = phase_ctx.enter_context(nc.psum_tensor("epS2", [128, 2 * 512], F32))
            pTr = phase_ctx.enter_context(nc.psum_tensor("epT2", [128, 512], BF16))
            P.phase_start()
            tok_wo = P.dma(g, "g", "gd", wo_sb[:].rearrange("p (dc f) -> p dc f", dc=DC),
                           P.o_proj.rearrange("(dc p) f -> p dc f", p=128))
            tok_ob_free = None
            tok_xn_free = None
            tok_sbf_free = None
            tok_otb_free = None
            tok_pTr_free = None
            tok_sp_s = None
            tok_pS_free = [None, None]
            for qi in range(NT):
                tok_ob = P.dma(sy, "s", "ld", ob[:].rearrange("p (o d) -> p o d", o=1),
                               P.oS[:, qi:qi + 1, :], toks=(tok_ob_free, self.tok_d_done))
                tok_xn = P.dma(sy, "s", "ld", xnt[:].rearrange("p (o d) -> p o d", o=1),
                               P.xnS[:, qi:qi + 1, :], toks=(tok_xn_free,))
                tok_tr2 = tok_pTr_free
                P.w(t, "t", tok_otb_free)
                for g4 in range(4):
                    P.w(t, "t", tok_tr2)
                    P.w(t, "t", tok_ob)
                    for j in range(4):
                        dvc = g4 * 4 + j
                        ins = t.transpose(pTr[:, j * 128:(j + 1) * 128],
                                          ob[:, dvc * 128:(dvc + 1) * 128], P.ident[:])
                    tk_pe = P.inc(ins, "pe")
                    P.w(v, "v", tk_pe)
                    ins = v.tensor_copy(otb[:, g4 * 512:(g4 + 1) * 512], pTr[:, 0:512])
                    tok_tr2 = P.inc(ins, "dv")
                tok_pTr_free = tok_tr2
                P.w(t, "t", tok_wo)
                tok_s_parts = []
                for fc in range(4):
                    bank = fc % 2
                    P.w(t, "t", tok_s_parts[-2] if fc >= 2 else tok_pS_free[bank])
                    P.w(t, "t", tok_tr2)
                    for dvc in range(DC):
                        ins = t.matmul(pS[:, bank * 512:(bank + 1) * 512],
                                       otb[:, dvc * 128:(dvc + 1) * 128],
                                       wo_sb[:, dvc * D + fc * 512: dvc * D + (fc + 1) * 512],
                                       start=(dvc == 0), stop=(dvc == DC - 1))
                    tk_pe = P.inc(ins, "pe")
                    P.w(v, "v", tk_pe)
                    P.w(v, "v", tok_xn)
                    P.w(v, "v", tok_sbf_free)
                    ins = v.scalar_tensor_tensor(s_bf[:, fc * 512:(fc + 1) * 512],
                                                 pS[:, bank * 512:(bank + 1) * 512], 1.0,
                                                 xnt[:, fc * 512:(fc + 1) * 512],
                                                 ALU.mult, ALU.add)
                    tok_s_parts.append(P.inc(ins, "dv"))
                tok_pS_free = [tok_s_parts[2], tok_s_parts[3]]
                tok_sp_s = P.dma(g, "g", "gd", P.sS[:, qi:qi + 1, :],
                                 s_bf[:].rearrange("p (o d) -> p o d", o=1),
                                 toks=(tok_s_parts[-1],))
                tok_sbf_free = tok_sp_s
                tok_ob_free = P.tok("pe")
                tok_xn_free = tok_s_parts[-1]
                tok_otb_free = P.tok("pe")
            self.tok_d2_done = tok_sp_s

    CAP = 256

    def emit_e_f(self):
        P = self
        nc = self.nc
        CAP = self.CAP
        g, v, a, t, sy = self.g, self.v, self.a, self.t, self.sy
        with ExitStack() as phase_ctx:
            acc = phase_ctx.enter_context(nc.sbuf_tensor("eacc", [128, 8 * D], BF16))
            w1_sb = phase_ctx.enter_context(nc.sbuf_tensor("ew1", [128, DC * H], BF16))
            w2_sb = phase_ctx.enter_context(nc.sbuf_tensor("ew2", [128, HC * D], BF16))
            xnblk = phase_ctx.enter_context(nc.sbuf_tensor("exnb", [128, 4 * D], BF16))
            XeT = phase_ctx.enter_context(nc.sbuf_tensor("exet", [128, DC * CAP], BF16))
            hTe = phase_ctx.enter_context(nc.sbuf_tensor("ehte", [128, HC * CAP], BF16))
            y_sb = phase_ctx.enter_context(nc.sbuf_tensor("eysb", [128, 2 * D], BF16))
            peT = phase_ctx.enter_context(nc.sbuf_tensor("epet", [128, 4 * CAP], BF16))
            iota1 = phase_ctx.enter_context(nc.sbuf_tensor("eiota", [128, CAP], I16))
            match4 = phase_ctx.enter_context(nc.sbuf_tensor("ematch", [128, 4], F32))
            slot4 = phase_ctx.enter_context(nc.sbuf_tensor("eslot", [128, 4], F32))
            tot_sb = phase_ctx.enter_context(nc.sbuf_tensor("etot", [128, 4], F32))
            offs_sb = phase_ctx.enter_context(nc.sbuf_tensor("eoffs", [128, 4], F32))
            b2e = phase_ctx.enter_context(nc.sbuf_tensor("eb2", [32, D], BF16))
            fs = phase_ctx.enter_context(nc.sbuf_tensor("fs", [128, D], BF16))
            fo = phase_ctx.enter_context(nc.sbuf_tensor("fo", [128, D], F32))
            pR = phase_ctx.enter_context(nc.psum_tensor("epR", [128, 512], F32))
            pA = phase_ctx.enter_context(nc.psum_tensor("epA", [128, 2 * 512], F32))
            pB = phase_ctx.enter_context(nc.psum_tensor("epB", [128, 4 * 512], F32))
            pTr = phase_ctx.enter_context(nc.psum_tensor("epTr", [128, 512], BF16))
            P.phase_start()
            P.inc(g.memset(b2e[:], 0.0), "gd")
            tok_iota = P.inc(g.iota(iota1[:], [[1, CAP]], base=1, channel_multiplier=0,
                                    allow_small_or_imprecise_dtypes=True), "gd")
            tok_w1 = tok_w2 = tok_b2 = None
            tok_w1_free = tok_w2_free = None
            tok_xnblk_free = None
            tok_fs_free = None
            tok_fo_free = None
            tok_acc_free = None
            tok_XeT_free = None
            tok_hTe_free = None
            tok_y_free = [None, None]
            tok_peT_free = None
            tok_pR_free = None
            tok_pA_free = [None, None]
            tok_pB_free = None
            tok_pb_bank = {}
            last_store = None
            first = True
            tok_xcopy = P.dma(sy, "s", "ld", P.out, P.xq)
            for h in range(2):
                tok_acc = {}
                for e in range(self.n_experts):
                    if first:
                        tok_w1 = P.dma(g, "g", "gd", w1_sb[:].rearrange("p (dc h) -> p dc h", dc=DC),
                                       P.w1T[e * D:(e + 1) * D, :].rearrange("(dc p) h -> p dc h", p=128))
                        tok_w2 = P.dma(g, "g", "gd", w2_sb[:].rearrange("p (hc d) -> p hc d", hc=HC),
                                       P.w2T[e * H:(e + 1) * H, :].rearrange("(hc p) d -> p hc d", p=128))
                        tok_b2 = P.dma(g, "g", "gd", b2e[0:1, :], P.b2[e:e + 1, :])
                        first = False
                    for b2i in range(2):
                        tile0 = h * 8 + b2i * 4
                        tok_xnblk = P.dma(sy, "s", "ld",
                                          xnblk[:].rearrange("p (s d) -> p s d", s=4),
                                          P.xnS[:, tile0:tile0 + 4, :],
                                          toks=(tok_xnblk_free, self.tok_c1_done))
                        for tl in range(4):
                            ti = tile0 + tl
                            P.w(v, "v", P.tok("dv"))
                            P.inc(v.tensor_scalar(match4[:, tl:tl + 1],
                                                  P.comb[:, ti * E + e: ti * E + e + 1],
                                                  0.0, None, ALU.is_gt), "dv")
                        P.w(v, "v", P.tok("dv"))
                        tok_match = P.tok("dv")
                        P.w(t, "t", tok_match)
                        P.w(t, "t", tok_pR_free)
                        ins = t.matmul(pR[:, 8:12], P.ones128f[:], match4[:], start=True, stop=True)
                        tk_pe = P.inc(ins, "pe")
                        P.w(v, "v", tk_pe)
                        tk = P.inc(v.tensor_copy(tot_sb[:], pR[:, 8:12]), "dv")
                        P.w(v, "v", tk)
                        tk = P.inc(v.memset(offs_sb[:, 0:1], 0.0), "dv")
                        for j in range(1, 4):
                            P.w(v, "v", tk)
                            tk = P.inc(v.tensor_tensor(offs_sb[:, j:j + 1], offs_sb[:, j - 1:j],
                                                       tot_sb[:, j - 1:j], ALU.add), "dv")
                        tok_offs = tk
                        P.w(t, "t", P.tok_utri)
                        P.w(t, "t", tok_offs)
                        for tp in range(2):
                            ins = t.matmul(pR[:, tp * 2: tp * 2 + 2], P.utri_sb[:],
                                           match4[:, tp * 2: tp * 2 + 2], start=True, stop=True)
                        tk_pe = P.inc(ins, "pe")
                        P.w(v, "v", tk_pe)
                        P.w(v, "v", tok_offs)
                        tok_slot = P.inc(v.tensor_tensor(slot4[:], pR[:, 0:4], offs_sb[:], ALU.add), "dv")
                        tok_pR_free = tok_slot
                        tok_peT = None
                        for tl in range(4):
                            ti = tile0 + tl
                            P.w(v, "v", tok_slot)
                            P.w(v, "v", tok_iota)
                            P.w(v, "v", tok_peT_free)
                            ins = v.tensor_scalar(peT[:, tl * CAP:(tl + 1) * CAP], iota1[:],
                                                  slot4[:, tl:tl + 1], match4[:, tl:tl + 1],
                                                  ALU.is_equal, ALU.mult)
                            tok_peT = P.inc(ins, "dv")
                        evacG = [None, None]
                        for dc in range(DC):
                            bank = dc % 2
                            P.w(t, "t", evacG[bank])
                            P.w(t, "t", tok_pA_free[bank])
                            P.w(t, "t", tok_xnblk)
                            P.w(t, "t", tok_peT)
                            P.w(t, "t", tok_XeT_free)
                            for tl in range(4):
                                ins = t.matmul(pA[:, bank * 512: bank * 512 + CAP],
                                               xnblk[:, tl * D + dc * 128: tl * D + (dc + 1) * 128],
                                               peT[:, tl * CAP:(tl + 1) * CAP],
                                               start=(tl == 0), stop=(tl == 3))
                            tk_pe = P.inc(ins, "pe")
                            P.w(a, "a", tk_pe)
                            ins = a.activation(XeT[:, dc * CAP:(dc + 1) * CAP],
                                               pA[:, bank * 512: bank * 512 + CAP], AF.Copy,
                                               bias=0.0, scale=1.0)
                            evacG[bank] = P.inc(ins, "ac")
                        tok_XeT_ready = evacG[(DC - 1) % 2]
                        tok_pA_free = [None, None]
                        evacH = [None, None]
                        for hc in range(HC):
                            bank = hc % 2
                            P.w(t, "t", tok_w1)
                            P.w(t, "t", evacH[bank])
                            P.w(t, "t", tok_hTe_free)
                            P.w(t, "t", tok_XeT_ready)
                            for dc in range(DC):
                                ins = t.matmul(pA[:, bank * 512: bank * 512 + CAP],
                                               w1_sb[:, dc * H + hc * 128: dc * H + (hc + 1) * 128],
                                               XeT[:, dc * CAP:(dc + 1) * CAP],
                                               start=(dc == 0), stop=(dc == DC - 1))
                            tk_pe = P.inc(ins, "pe")
                            P.w(a, "a", tk_pe)
                            P.w(a, "a", P.tok_b1col)
                            ins = a.activation(hTe[:, hc * CAP:(hc + 1) * CAP],
                                               pA[:, bank * 512: bank * 512 + CAP], AF.Silu,
                                               bias=P.b1col_sb[:, e * HC + hc: e * HC + hc + 1],
                                               scale=1.0)
                            evacH[bank] = P.inc(ins, "ac")
                        tok_h_ready = evacH[(HC - 1) % 2]
                        tok_XeT_free = P.tok("pe")
                        tok_y_evac = None
                        y_bank_tok = {}
                        for sc in range(2):
                            for dc2 in range(4):
                                P.w(t, "t", tok_w2)
                                P.w(t, "t", tok_b2)
                                P.w(t, "t", tok_h_ready)
                                P.w(t, "t", tok_pB_free)
                                P.w(t, "t", y_bank_tok.get(dc2))
                                for hc in range(HC):
                                    t.matmul(pB[:, dc2 * 512:(dc2 + 1) * 512],
                                             hTe[:, hc * CAP + sc * 128: hc * CAP + (sc + 1) * 128],
                                             w2_sb[:, hc * D + dc2 * 512: hc * D + (dc2 + 1) * 512],
                                             start=(hc == 0), stop=False)
                                ins = t.matmul(pB[:, dc2 * 512:(dc2 + 1) * 512],
                                               P.ones_row[0:32, 0:128],
                                               b2e[0:32, dc2 * 512:(dc2 + 1) * 512],
                                               start=False, stop=True)
                                tk_pe = P.inc(ins, "pe")
                                P.w(a, "a", tk_pe)
                                ins = a.activation(y_sb[:, sc * D + dc2 * 512: sc * D + (dc2 + 1) * 512],
                                                   pB[:, dc2 * 512:(dc2 + 1) * 512], AF.Copy,
                                                   bias=0.0, scale=1.0)
                                tok_y_evac = P.inc(ins, "ac")
                                y_bank_tok[dc2] = tok_y_evac
                        tok_hTe_free = P.tok("pe")
                        tok_tr = None
                        for tl in range(4):
                            for sc in range(2):
                                P.w(t, "t", tok_tr)
                                P.w(t, "t", tok_peT)
                                P.w(t, "t", P.tok("pe") if False else None)
                                ins = t.transpose(pTr[:, 0:128],
                                                  peT[:, tl * CAP + sc * 128: tl * CAP + (sc + 1) * 128],
                                                  P.ident[:])
                                tk_pe = P.inc(ins, "pe")
                                P.w(v, "v", tk_pe)
                                ins = v.tensor_copy(peT[:, tl * CAP + sc * 128: tl * CAP + (sc + 1) * 128],
                                                    pTr[:, 0:128])
                                tok_tr = P.inc(ins, "dv")
                        tok_peT_T = tok_tr
                        for tl in range(4):
                            atile = b2i * 4 + tl
                            for dc2 in range(4):
                                P.w(t, "t", tok_peT_T)
                                P.w(t, "t", tok_y_evac)
                                P.w(t, "t", tok_pb_bank.get(dc2))
                                for sc in range(2):
                                    ins = t.matmul(pB[:, dc2 * 512:(dc2 + 1) * 512],
                                                   peT[:, tl * CAP + sc * 128: tl * CAP + (sc + 1) * 128],
                                                   y_sb[:, sc * D + dc2 * 512: sc * D + (dc2 + 1) * 512],
                                                   start=(sc == 0), stop=(sc == 1))
                                tk_pe = P.inc(ins, "pe")
                                P.w(v, "v", tk_pe)
                                dst = acc[:, atile * D + dc2 * 512: atile * D + (dc2 + 1) * 512]
                                ti_g = h * 8 + atile
                                cslice = P.comb[:, ti_g * E + e: ti_g * E + e + 1]
                                if e == 0:
                                    P.w(v, "v", tok_acc_free)
                                    ins = v.tensor_scalar_mul(dst, pB[:, dc2 * 512:(dc2 + 1) * 512], cslice)
                                else:
                                    P.w(v, "v", tok_acc.get((atile, dc2)))
                                    ins = v.scalar_tensor_tensor(dst, pB[:, dc2 * 512:(dc2 + 1) * 512],
                                                                 cslice, dst, ALU.mult, ALU.add)
                                tok_acc[(atile, dc2)] = P.inc(ins, "dv")
                                tok_pb_bank[dc2] = tok_acc[(atile, dc2)]
                        tok_pB_free = P.tok("dv")
                        tok_peT_free = P.tok("pe")
                        tok_xnblk_free = P.tok("pe")
                    ne_h, ne_e = (h, e + 1) if e + 1 < self.n_experts else (h + 1, 0)
                    if ne_h < 2:
                        tok_w1_free = P.tok("pe")
                        tok_w1 = P.dma(g, "g", "gd", w1_sb[:].rearrange("p (dc h) -> p dc h", dc=DC),
                                       P.w1T[ne_e * D:(ne_e + 1) * D, :].rearrange("(dc p) h -> p dc h", p=128),
                                       toks=(tok_w1_free,))
                        tok_w2_free = P.tok("pe")
                        tok_w2 = P.dma(g, "g", "gd", w2_sb[:].rearrange("p (hc d) -> p hc d", hc=HC),
                                       P.w2T[ne_e * H:(ne_e + 1) * H, :].rearrange("(hc p) d -> p hc d", p=128),
                                       toks=(tok_w2_free,))
                        tok_b2 = P.dma(g, "g", "gd", b2e[0:1, :], P.b2[ne_e:ne_e + 1, :])
                for tloc in range(NT // 2):
                    ti = h * 8 + tloc
                    tok_fs = P.dma(sy, "s", "ld", fs[:].rearrange("p (o d) -> p o d", o=1),
                                   P.sS[:, ti:ti + 1, :],
                                   toks=(tok_fs_free, self.tok_d2_done))
                    P.w(a, "a", tok_acc[(tloc, 3)])
                    asl = acc[:, tloc * D:(tloc + 1) * D]
                    tok_gs = P.inc(a.activation(asl, asl, AF.Silu, bias=0.0, scale=1.0), "ac")
                    P.w(v, "v", tok_gs)
                    P.w(v, "v", tok_fs)
                    P.w(v, "v", tok_fo_free)
                    tok_o = P.inc(v.tensor_tensor(fo[:], fs[:], asl, ALU.mult), "dv")
                    tok_fs_free = tok_o
                    P.w(g, "g", tok_o)
                    P.w(g, "g", tok_xcopy)
                    ins = g.dma_start(P.out.rearrange("(s p) d -> p s d", p=128)[:, ti:ti + 1, :],
                                      fo[:].rearrange("p (o d) -> p o d", o=1),
                                      accum_op=ALU.add)
                    last_store = P.inc(ins, "gd", 16)
                    tok_fo_free = last_store
                tok_acc_free = P.tok("dv")
            if P.debug:
                P.w(g, "g", P.tok("dv"))
                P.w(g, "g", P.tok("ac"))
                P.inc(g.dma_start(P.dbg_acc, acc[:]), "gd", 16)
                CAPc = self.CAP
                P.inc(g.dma_start(P.dbg_cap[:, 0:4], slot4[:]), "gd", 16)
                P.inc(g.dma_start(P.dbg_cap[:, 4:8], match4[:]), "gd", 16)
                P.inc(g.dma_start(P.dbg_cap[:, 8:12], offs_sb[:]), "gd", 16)
                P.inc(g.dma_start(P.dbg_cap[:, 16:16 + 4 * CAPc], peT[:]), "gd", 16)
                P.inc(g.dma_start(P.dbg_cap[:, 1040:1040 + DC * CAPc], XeT[:]), "gd", 16)
                P.inc(g.dma_start(P.dbg_cap[:, 5140:5140 + HC * CAPc], hTe[:]), "gd", 16)
                P.inc(g.dma_start(P.dbg_cap[:, 8000:8000 + 2 * D], y_sb[:]), "gd", 16)
            P.w(sy, "s", last_store)
            P.w(sy, "s", P.tok("ld"))


def build_program(debug=False, n_experts=E):
    return Prog(debug=debug, n_experts=n_experts).build()


def kernel(L0_attn_x, L0_attn_qkv, L0_attn_o_proj, L0_attn_gamma, L0_attn_beta,
           L0_ffn_gate_w, L0_ffn_gate_b, L0_ffn_bias, L0_ffn_w1, L0_ffn_b1,
           L0_ffn_w2, L0_ffn_b2):
    x = np.ascontiguousarray(np.asarray(L0_attn_x, dtype=np.float32))
    qkv = np.ascontiguousarray(np.asarray(L0_attn_qkv, dtype=np.float32))
    o_proj = np.ascontiguousarray(np.asarray(L0_attn_o_proj, dtype=np.float32))
    gamma = np.asarray(L0_attn_gamma, dtype=np.float32)
    beta = np.asarray(L0_attn_beta, dtype=np.float32)
    gate_w = np.asarray(L0_ffn_gate_w, dtype=np.float32)
    gate_b = np.asarray(L0_ffn_gate_b, dtype=np.float32)
    gbias = np.asarray(L0_ffn_bias, dtype=np.float32)
    w1 = np.asarray(L0_ffn_w1, dtype=np.float32)
    b1 = np.ascontiguousarray(np.asarray(L0_ffn_b1, dtype=np.float32))
    w2 = np.asarray(L0_ffn_w2, dtype=np.float32)
    b2 = np.ascontiguousarray(np.asarray(L0_ffn_b2, dtype=np.float32))

    shared = dict(
        qkv=qkv,
        o_proj=o_proj,
        gammaT=np.ascontiguousarray(gamma.reshape(DC, 128).T),
        betaT=np.ascontiguousarray(beta.reshape(DC, 128).T),
        gate_wT=np.ascontiguousarray(gate_w.T),
        gate_b=np.ascontiguousarray(np.broadcast_to(gate_b.reshape(1, E), (128, E))),
        gate_bias=np.ascontiguousarray(np.broadcast_to(gbias.reshape(1, E), (128, E))),
        w1T=np.ascontiguousarray(w1.transpose(0, 2, 1).reshape(E * D, H)),
        w2T=np.ascontiguousarray(w2.transpose(0, 2, 1).reshape(E * H, D)),
        b1col=np.ascontiguousarray(b1.reshape(E, HC, 128).transpose(2, 0, 1).reshape(128, E * HC)),
        utri=np.ascontiguousarray(np.triu(np.ones((128, 128), dtype=np.float32))),
        b2=b2,
    )
    in_maps = []
    metas = []
    for core in range(8):
        bi, c = core // 4, core % 4
        tiles = np.array([16 * k + 4 * c + j for k in range(4) for j in range(4)])
        rows = (tiles[:, None] * 128 + np.arange(128)[None, :]).reshape(-1)
        xb = np.ascontiguousarray(x[bi])
        xq = np.ascontiguousarray(xb[rows])
        xqT = np.ascontiguousarray(xq.T)
        qpos_t = np.ascontiguousarray(rows.astype(np.float32).reshape(NT, 128).T)
        m = dict(xb=xb, xq=xq, xqT=xqT, qpos=qpos_t)
        m.update(shared)
        in_maps.append(m)
        metas.append((bi, rows))

    nc = build_program()
    res = run_bass_kernel_spmd(nc, in_maps, core_ids=list(range(8)))

    out = np.empty((2, S, D), dtype=np.float32)
    for core in range(8):
        bi, rows = metas[core]
        out[bi, rows] = res.results[core]["out"]
    return out
